# version 13
# baseline (speedup 1.0000x reference)
import os
import numpy as np
from itertools import zip_longest
from contextlib import ExitStack

import concourse.bass as bass
import concourse.mybir as mybir
from concourse.bass_utils import run_bass_kernel_spmd

F32 = mybir.dt.float32
BF16 = mybir.dt.bfloat16
I32 = mybir.dt.int32
AX = mybir.AxisListType.X
OP = mybir.AluOpType
ACTF = mybir.ActivationFunctionType

NEG_BIG = -1.0e30

REAL_CFG = dict(D=2048, S=8192, F=1024, E=16, SH=2, NC=8)


def derive(cfg):
    c = dict(cfg)
    c["H"] = c["D"] // 128
    assert c["H"] == 2 * c["NC"], "one head-pair per core"
    c["DC"] = c["D"] // 128
    c["FC"] = c["F"] // 128
    c["NT"] = 2 * c["NC"]
    c["TS"] = c["S"] // c["NT"]
    assert c["TS"] % 128 == 0 and c["TS"] <= 512
    c["T"] = 2 * c["TS"]
    c["TCH"] = c["T"] // 128
    c["TBW"] = c["TS"]
    c["QS"] = c["TBW"] // 128
    c["EXP"] = c["E"] + c["SH"]
    c["GE"] = 2
    assert c["EXP"] % c["GE"] == 0
    c["NG"] = c["EXP"] // c["GE"]
    c["DQW"] = min(512, c["D"])
    c["NDQ"] = c["D"] // c["DQW"]
    return c


ENGS = ("tensor", "vector", "scalar", "gpsimd", "sync")
FIFO_ENGS = set()


NCHAN = 8


class Tok:
    __slots__ = ("eng", "idx", "dma", "val", "chan")

    def __init__(self, eng, idx, dma, chan):
        self.eng = eng
        self.idx = idx
        self.dma = dma
        self.chan = chan
        self.val = None


class Rec:
    def __init__(self):
        self.ops = {e: [] for e in ENGS}
        self.used = set()
        self.dma_cnt = {e: 0 for e in ENGS}
        self.last_dma = {}

    def add(self, eng, fn, deps=(), dma=False, tag="", group=None):
        deps = [d for d in deps if d is not None]
        chan = None
        if dma:
            chan = self.dma_cnt[eng] % NCHAN
            self.dma_cnt[eng] += 1
            prev = self.last_dma.get((eng, chan))
            if prev is not None:
                deps = deps + [prev]
            self.last_dma[(eng, chan)] = None
        t = Tok(eng, len(self.ops[eng]), dma, chan)
        if dma:
            self.last_dma[(eng, chan)] = t
        self.ops[eng].append((fn, deps, t, tag, group))
        return t

    def join(self, eng, deps, tag="join"):
        return self.add(eng, lambda e: e.nop(nofuse=True), deps=deps, tag=tag)

    def finalize_and_emit(self, nc, sems):
        for eng in ENGS:
            cc = 0
            cd = {}
            ops = self.ops[eng]
            n = len(ops)
            i = 0
            while i < n:
                fn, deps, t, tag, group = ops[i]
                if t.dma:
                    cd[t.chan] = cd.get(t.chan, 0) + 16
                    t.val = cd[t.chan]
                    i += 1
                    continue
                j = i
                if group is not None:
                    while j + 1 < n and (not ops[j + 1][2].dma) and ops[j + 1][4] == group:
                        j += 1
                cc += 1
                for k in range(i, j + 1):
                    ops[k][2].val = cc
                i = j + 1
        rec = self

        def skey(d):
            return (d.eng, d.chan) if d.dma else (d.eng, None)

        def emit_for(eng_name):
            def closure(eng):
                waited = {}
                chain = 0
                ops = rec.ops[eng_name]
                n = len(ops)
                i = 0
                while i < n:
                    fn, deps, t, tag, group = ops[i]
                    j = i
                    if (not t.dma) and group is not None:
                        while j + 1 < n and (not ops[j + 1][2].dma) and ops[j + 1][4] == group:
                            j += 1
                    for k in range(i, j + 1):
                        fnk, depsk, tk, tagk, _g = ops[k]
                        need = {}
                        for d in depsk:
                            if d.val is None:
                                raise RuntimeError(f"unvalued dep used by {eng_name}:{tagk}")
                            kk = skey(d)
                            if need.get(kk, 0) < d.val:
                                need[kk] = d.val
                        attach = None
                        if not tk.dma:
                            need.pop((eng_name, None), None)
                            cpos = j if eng_name == "tensor" else i
                            if k == cpos and chain > 0:
                                attach = ((eng_name, None), chain)
                        else:
                            ck = (eng_name, tk.chan)
                            if ck in need:
                                attach = (ck, need.pop(ck))
                        stand = []
                        for kk, v in sorted(need.items(), key=lambda kv: str(kv[0])):
                            if waited.get(kk, 0) < v:
                                stand.append((kk, v))
                        if attach is None and stand:
                            attach = stand.pop()
                        for kk, v in stand:
                            eng.wait_ge(sems[kk], v)
                            waited[kk] = v
                        inst = fnk(eng)
                        if inst is None:
                            if attach is not None:
                                kk, v = attach
                                eng.wait_ge(sems[kk], v)
                                waited[kk] = max(waited.get(kk, 0), v)
                            continue
                        if attach is not None:
                            kk, v = attach
                            inst._wait_ge(sems[kk], v)
                            if waited.get(kk, 0) < v:
                                waited[kk] = v
                        if tk.dma:
                            inst.then_inc(sems[(eng_name, tk.chan)], 16)
                        elif k == j:
                            chain += 1
                            inst.then_inc(sems[(eng_name, None)], 1)
                    i = j + 1
            return closure

        with nc.Block() as blk:
            blk.tensor(emit_for("tensor"))
            blk.vector(emit_for("vector"))
            blk.scalar(emit_for("scalar"))
            blk.gpsimd(emit_for("gpsimd"))
            blk.sync(emit_for("sync"))


def build_nc(cfg, debug=False, phases=4):
    cf = derive(cfg)
    D, S, F, E, SH = cf["D"], cf["S"], cf["F"], cf["E"], cf["SH"]
    NC, H, DC, FC = cf["NC"], cf["H"], cf["DC"], cf["FC"]
    TS, T, TCH = cf["TS"], cf["T"], cf["TCH"]
    TBW, QS, EXP, GE, NG = cf["TBW"], cf["QS"], cf["EXP"], cf["GE"], cf["NG"]
    DQW, NDQ = cf["DQW"], cf["NDQ"]
    NTB = 2

    nc = bass.Bass("TRN2", target_bir_lowering=False, debug=debug)

    x_loc = nc.dram_tensor("x_loc", [T, D], F32, kind="ExternalInput")
    meta = nc.dram_tensor("meta", [1, 8], I32, kind="ExternalInput")
    wqkv_r = nc.dram_tensor("wqkv_r", [3 * D, DC * 128], F32, kind="ExternalInput")
    wout_r = nc.dram_tensor("wout_r", [D, D], F32, kind="ExternalInput")
    wa_r = nc.dram_tensor("wa_r", [128, DC * E], F32, kind="ExternalInput")
    gq_r = nc.dram_tensor("gq_r", [128, DC], F32, kind="ExternalInput")
    gk_r = nc.dram_tensor("gk_r", [128, DC], F32, kind="ExternalInput")
    g_all = nc.dram_tensor("g_all", [EXP * FC * 128, DC * 128], F32, kind="ExternalInput")
    w1_all = nc.dram_tensor("w1_all", [EXP * FC * 128, DC * 128], F32, kind="ExternalInput")
    w2_all = nc.dram_tensor("w2_all", [EXP * NDQ * 128, FC * DQW], F32, kind="ExternalInput")
    gb_all = nc.dram_tensor("gb_all", [EXP, F], F32, kind="ExternalInput")
    b1_all = nc.dram_tensor("b1_all", [EXP, F], F32, kind="ExternalInput")
    rb2_r = nc.dram_tensor("rb2_r", [E, D], F32, kind="ExternalInput")
    sb2_r = nc.dram_tensor("sb2_r", [SH, D], F32, kind="ExternalInput")
    out_loc = nc.dram_tensor("out_loc", [T, D], F32, kind="ExternalOutput")

    qkv_loc = nc.dram_tensor("qkv_loc", [3 * D, T], BF16)
    qkv_g = nc.dram_tensor("qkv_g", [NC * 3 * D, T], BF16,
                           addr_space="Shared" if NC > 4 else "Local")
    ctx_loc = nc.dram_tensor("ctx_loc", [256, S], BF16)
    ctx_g = nc.dram_tensor("ctx_g", [NC * 256, S], BF16,
                           addr_space="Shared" if NC > 4 else "Local")

    es = ExitStack()
    A = es.enter_context
    r = Rec()
    sems = {}
    for e in ENGS:
        sems[(e, None)] = A(nc.semaphore(f"sc_{e}"))
        for ch in range(NCHAN):
            sems[(e, ch)] = A(nc.semaphore(f"sd_{e}{ch}"))

    def dap(t, off, dims):
        return bass.AP(t.ap().tensor, off, [[int(a), int(b)] for a, b in dims])

    xn_fm = A(nc.sbuf_tensor("xn_fm", [128, DC * T], BF16))
    comb_b = A(nc.sbuf_tensor("comb_b", [E, T], BF16))
    gq2_sb = A(nc.sbuf_tensor("gq2_sb", [128, DC], F32))
    gk_sb = A(nc.sbuf_tensor("gk_sb", [128, DC], F32))
    gbT = A(nc.sbuf_tensor("gbT", [128, FC * EXP], F32))
    b1T = A(nc.sbuf_tensor("b1T", [128, FC * EXP], F32))
    rb2_sb = A(nc.sbuf_tensor("rb2_sb", [E, D], BF16))
    sb2_sb = A(nc.sbuf_tensor("sb2_sb", [SH, D], BF16))
    id_bf = A(nc.sbuf_tensor("id_bf", [128, 128], BF16))
    id_f32 = A(nc.sbuf_tensor("id_f32", [128, 128], F32))
    ones128f = A(nc.sbuf_tensor("ones128f", [128, 128], F32))
    ones128b = A(nc.sbuf_tensor("ones128b", [128, 128], BF16))
    ones_c = A(nc.sbuf_tensor("ones_c", [128, 1], BF16))
    ones_rb = A(nc.sbuf_tensor("ones_rb", [1, 128], BF16))
    ones_rf = A(nc.sbuf_tensor("ones_rf", [1, 128], F32))
    onehot = A(nc.sbuf_tensor("onehot", [E, E * 128], BF16))
    wa_sb = A(nc.sbuf_tensor("wa_sb", [128, DC * E], F32))
    meta_sb = A(nc.sbuf_tensor("meta_sb", [1, 8], I32))

    mcell = {}

    if debug:
        dbg_xn = nc.dram_tensor("dbg_xn", [128, DC * T], BF16, kind="ExternalOutput")
        dbg_comb = nc.dram_tensor("dbg_comb", [E, T], BF16, kind="ExternalOutput")
        dbg_acc = nc.dram_tensor("dbg_acc", [128, TCH * D], F32, kind="ExternalOutput")

    t_meta = r.add("sync", lambda e: e.dma_start(out=meta_sb[:], in_=meta.ap()), dma=True, tag="meta")

    r.join("sync", [t_meta], tag="metaj")

    def loadv(e):
        c = e.value_load(meta_sb[0:1, 0:1])
        mcell["c"] = c
        mcell["qk"] = e.snap(c * (256 * T))
        mcell["v"] = e.snap(c * (TCH * 128 * 256))
        mcell["ctx"] = e.snap(c * T)
        return e.nop(nofuse=True)
    r.add("sync", loadv, tag="loadv")

    t_gq = r.add("sync", lambda e: e.dma_start(out=gq2_sb[:], in_=gq_r.ap()), dma=True)
    t_gk = r.add("sync", lambda e: e.dma_start(out=gk_sb[:], in_=gk_r.ap()), dma=True)
    t_wa = r.add("sync", lambda e: e.dma_start(out=wa_sb[:], in_=wa_r.ap()), dma=True)
    t_rb2 = r.add("gpsimd", lambda e: e.dma_start(out=rb2_sb[:], in_=rb2_r.ap()), dma=True)
    t_sb2 = r.add("gpsimd", lambda e: e.dma_start(out=sb2_sb[:], in_=sb2_r.ap()), dma=True)

    t_o128f = r.add("gpsimd", lambda e: e.memset(ones128f[:], 1.0))
    t_o128b = r.add("gpsimd", lambda e: e.memset(ones128b[:], 1.0))
    t_onc = r.add("gpsimd", lambda e: e.memset(ones_c[:], 1.0))
    t_onrb = r.add("gpsimd", lambda e: e.memset(ones_rb[:], 1.0))
    t_onrf = r.add("gpsimd", lambda e: e.memset(ones_rf[:], 1.0))
    t_oh0 = r.add("gpsimd", lambda e: e.memset(onehot[:], 1.0))
    t_idb = r.add("gpsimd", lambda e: e.affine_select(
        out=id_bf[:], in_=ones128b[:], compare_op=OP.is_equal, fill=0.0,
        base=0, pattern=[[-1, 128]], channel_multiplier=1), deps=[t_o128b])
    t_idf = r.add("gpsimd", lambda e: e.affine_select(
        out=id_f32[:], in_=ones128f[:], compare_op=OP.is_equal, fill=0.0,
        base=0, pattern=[[-1, 128]], channel_multiplier=1), deps=[t_o128f])
    t_oh = []
    for ee in range(E):
        t_oh.append(r.add("gpsimd", lambda e, ee=ee: e.affine_select(
            out=onehot[:, ee * 128:(ee + 1) * 128], in_=onehot[:, ee * 128:(ee + 1) * 128],
            compare_op=OP.is_equal, fill=0.0,
            base=-ee, pattern=[[0, 128]], channel_multiplier=1), deps=[t_oh0]))
    t_oh_j = r.join("gpsimd", t_oh, tag="ohj")

    t_gq2 = r.add("vector", lambda e: e.tensor_scalar_mul(
        gq2_sb[:], gq2_sb[:], float(1.0 / np.sqrt(128.0))), deps=[t_gq])

    p1 = ExitStack()
    A1 = p1.enter_context
    bias_raw = A1(nc.sbuf_tensor("bias_raw", [EXP, 2 * F], F32))
    x_sb = [A1(nc.sbuf_tensor(f"x_sb{i}", [128, D], F32)) for i in range(2)]
    xnf = A1(nc.sbuf_tensor("xnf", [128, D], F32))
    sqf = A1(nc.sbuf_tensor("sqf", [128, D], F32))
    xnT_f = [A1(nc.sbuf_tensor(f"xnT_f{i}", [128, 128], F32)) for i in range(2)]
    stat_c = A1(nc.sbuf_tensor("stat_c", [128, 8], F32))
    sg_sb = A1(nc.sbuf_tensor("sg_sb", [128, E], F32))
    tp_sc = A1(nc.sbuf_tensor("tp_sc", [128, 2 * E], F32))
    qraw = A1(nc.sbuf_tensor("qraw", [128, DC * T], BF16))
    kraw = A1(nc.sbuf_tensor("kraw", [128, DC * T], BF16))
    sq_bf = A1(nc.sbuf_tensor("sq_bf", [128, T], BF16))
    wqk_sb = [A1(nc.sbuf_tensor(f"wqk_sb{i}", [128, DC * 128], BF16)) for i in range(2)]
    vtmp = [A1(nc.sbuf_tensor(f"vtmp{i}", [128, TBW], BF16)) for i in range(2)]
    vt_st = [A1(nc.sbuf_tensor(f"vt_st{i}", [128, QS * 128], BF16)) for i in range(2)]
    qn_sc = [A1(nc.sbuf_tensor(f"qn_sc{i}", [128, TBW], F32)) for i in range(2)]
    qn_st = [A1(nc.sbuf_tensor(f"qn_st{i}", [128, TBW], BF16)) for i in range(2)]
    rows = A1(nc.sbuf_tensor("rows", [1, 3 * T], F32))

    t_biasg = r.add("sync", lambda e: e.dma_start(out=bias_raw[:, 0:F], in_=gb_all.ap()), dma=True)
    t_biasb = r.add("sync", lambda e: e.dma_start(out=bias_raw[:, F:2 * F], in_=b1_all.ap()), dma=True)

    pA = ExitStack()
    ps_tr = [pA.enter_context(nc.psum_tensor(f"ps_tr{i}", [128, 512], F32)) for i in range(2)]
    ps_aff = [pA.enter_context(nc.psum_tensor(f"ps_aff{i}", [128, 512], F32)) for i in range(2)]

    ev_bias = {}
    for j, (soff, dst) in enumerate([(0, gbT), (F, b1T)]):
        for fc in range(FC):
            i = j * FC + fc
            tt_ = r.add("tensor", lambda e, soff=soff, fc=fc, i=i: e.transpose(
                ps_tr[i % 2][:, 0:EXP],
                bias_raw[0:EXP, soff + fc * 128: soff + (fc + 1) * 128],
                id_f32[0:EXP, 0:EXP]),
                deps=[t_biasg, t_biasb, t_idf, ev_bias.get(i - 2)], tag=f"biasT{i}")
            ev_bias[i] = r.add("vector", lambda e, dst=dst, fc=fc, i=i: e.tensor_copy(
                dst[:, fc * EXP:(fc + 1) * EXP], ps_tr[i % 2][:, 0:EXP]),
                deps=[tt_], tag=f"biasE{i}")
    t_bias_j = r.join("vector", list(ev_bias.values()), tag="biasj")

    if phases < 0.18:
        t_fin = []
        for tch in range(TCH):
            t_fin.append(r.add("sync", lambda e, tch=tch: e.dma_start(
                out=out_loc.ap()[tch * 128:(tch + 1) * 128, :],
                in_=x_loc.ap()[tch * 128:(tch + 1) * 128, :]),
                dma=True, deps=[t_bias_j, t_oh_j, t_gq2, t_rb2, t_sb2, t_idb], tag=f"dummyout{tch}"))
        r.join("sync", t_fin, tag="final")
        r.finalize_and_emit(nc, sems)
        p1.close()
        es.close()
        return nc


    ev_xn = {}
    ev_comb = {}
    ev_combT = {}
    ev_tr_last = {}
    t_sg_tok = {}
    last_xn_apply = {}
    TCHrun = TCH if phases >= 0.22 else 1
    for tch in range(TCHrun):
        t_x = r.add("sync", lambda e, tch=tch: e.dma_start(
            out=x_sb[tch % 2][:], in_=x_loc.ap()[tch * 128:(tch + 1) * 128, :]),
            dma=True, deps=[last_xn_apply.get(tch - 2)], tag=f"x{tch}")
        t_sq = r.add("scalar", lambda e, tch=tch: e.activation(
            sqf[:], x_sb[tch % 2][:], ACTF.Square),
            deps=[t_x, last_xn_apply.get(tch - 1)], tag=f"sq{tch}")
        c1 = r.add("vector", lambda e, tch=tch: e.reduce_sum(
            stat_c[:, 0:1], x_sb[tch % 2][:], axis=AX), deps=[t_x])
        c2 = r.add("vector", lambda e: e.reduce_sum(stat_c[:, 1:2], sqf[:], axis=AX),
                   deps=[t_sq])
        c3 = r.add("vector", lambda e: e.tensor_scalar_mul(stat_c[:, 0:1], stat_c[:, 0:1], 1.0 / D), deps=[c1])
        c4 = r.add("vector", lambda e: e.tensor_scalar_mul(stat_c[:, 1:2], stat_c[:, 1:2], 1.0 / D), deps=[c2])
        c5 = r.add("vector", lambda e: e.tensor_tensor(stat_c[:, 2:3], stat_c[:, 0:1], stat_c[:, 0:1], OP.mult), deps=[c3])
        c6 = r.add("vector", lambda e: e.tensor_tensor(stat_c[:, 3:4], stat_c[:, 1:2], stat_c[:, 2:3], OP.subtract), deps=[c4, c5])
        t_var = r.add("vector", lambda e: e.tensor_scalar_add(stat_c[:, 3:4], stat_c[:, 3:4], 1e-5), deps=[c6])
        t_sqrt = r.add("scalar", lambda e: e.activation(stat_c[:, 4:5], stat_c[:, 3:4], ACTF.Sqrt),
                       deps=[t_var], tag=f"sqrt{tch}")
        t_rec = r.add("vector", lambda e: e.reciprocal(stat_c[:, 5:6], stat_c[:, 4:5]),
                      deps=[t_sqrt])
        last_xn_apply[tch] = r.add("vector", lambda e, tch=tch: e.tensor_scalar(
            xnf[:], x_sb[tch % 2][:], stat_c[:, 0:1], stat_c[:, 5:6],
            OP.subtract, OP.mult),
            deps=[t_rec, c3, ev_tr_last.get(tch - 1)], tag=f"xn{tch}")
        SKIP = os.environ.get("K_SKIP", "")
        if "ln_only" in SKIP:
            ev_combT[tch] = last_xn_apply[tch]
            continue
        ev_evf = {}
        last_router_mm = {}
        for dc in range(DC):
            war = [ev_xn.get((dc - 2, tch)), ev_evf.get(dc - 2)]
            if dc < 2:
                war.append(ev_combT.get(tch - 1))
                if tch == 0:
                    war.append(t_bias_j)
            t_tr = r.add("tensor", lambda e, dc=dc: e.transpose(
                ps_tr[dc % 2][:, 0:128], xnf[:, dc * 128:(dc + 1) * 128], id_f32[:, :]),
                deps=[last_xn_apply[tch], t_idf] + war, tag=f"tr{tch}_{dc}")
            ev_evf[dc] = r.add("vector", lambda e, dc=dc: e.tensor_copy(
                xnT_f[dc % 2][:], ps_tr[dc % 2][:, 0:128]),
                deps=[t_tr, last_router_mm.get(dc - 2)], tag=f"evf{tch}_{dc}")
            ev_xn[(dc, tch)] = r.add("scalar", lambda e, dc=dc, tch=tch: e.activation(
                xn_fm[:, dc * T + tch * 128: dc * T + (tch + 1) * 128],
                ps_tr[dc % 2][:, 0:128], ACTF.Copy),
                deps=[t_tr, ev_evf[dc]], tag=f"xnb{tch}_{dc}")
            if "no_aff" in SKIP:
                last_router_mm[dc] = ev_evf[dc]
            else:
                last_router_mm[dc] = r.add("tensor", lambda e, dc=dc, tch=tch: e.matmul(
                    ps_aff[tch % 2][:, 0:E], xnT_f[dc % 2][:], wa_sb[:, dc * E:(dc + 1) * E],
                    start=(dc == 0), stop=(dc == DC - 1)),
                    deps=[ev_evf[dc], t_wa, t_sg_tok.get(tch - 2)], tag=f"aff{tch}_{dc}")
        ev_tr_last[tch] = last_router_mm[DC - 1]
        if "no_router_tail" in SKIP:
            ev_combT[tch] = r.join("scalar", [ev_xn[(dc, tch)] for dc in range(DC)], tag=f"sk{tch}")
            t_sg_tok[tch] = ev_combT[tch]
            ev_comb[tch] = ev_combT[tch]
            continue
        t_sg = r.add("scalar", lambda e, tch=tch: e.activation(
            sg_sb[:], ps_aff[tch % 2][:, 0:E], ACTF.Sigmoid),
            deps=[last_router_mm[DC - 1], ev_comb.get(tch - 1)], tag=f"sig{tch}")
        t_sg_tok[tch] = t_sg
        u1 = r.add("vector", lambda e: e.reduce_max(stat_c[:, 6:7], sg_sb[:], axis=AX), deps=[t_sg])
        u2 = r.add("vector", lambda e: e.tensor_single_scalar(
            tp_sc[:, 0:E], sg_sb[:], stat_c[:, 6:7], OP.is_ge), deps=[u1])
        u3 = r.add("vector", lambda e: e.scalar_tensor_tensor(
            tp_sc[:, E:2 * E], tp_sc[:, 0:E], NEG_BIG, sg_sb[:], OP.mult, OP.add), deps=[u2])
        u4 = r.add("vector", lambda e: e.reduce_max(stat_c[:, 7:8], tp_sc[:, E:2 * E], axis=AX), deps=[u3])
        u5 = r.add("vector", lambda e: e.tensor_single_scalar(
            tp_sc[:, E:2 * E], tp_sc[:, E:2 * E], stat_c[:, 7:8], OP.is_ge), deps=[u4])
        u6 = r.add("vector", lambda e: e.tensor_tensor(
            tp_sc[:, 0:E], tp_sc[:, 0:E], tp_sc[:, E:2 * E], OP.add), deps=[u5])
        u7 = r.add("vector", lambda e: e.tensor_tensor(
            stat_c[:, 6:7], stat_c[:, 6:7], stat_c[:, 7:8], OP.add), deps=[u4, u6])
        t8 = r.add("vector", lambda e: e.reciprocal(stat_c[:, 7:8], stat_c[:, 6:7]), deps=[u7])
        ev_comb[tch] = r.add("vector", lambda e: e.scalar_tensor_tensor(
            tp_sc[:, E:2 * E], sg_sb[:], stat_c[:, 7:8], tp_sc[:, 0:E],
            OP.mult, OP.mult), deps=[t8, u6], tag=f"comb{tch}")
        if "no_combT" in SKIP:
            ev_combT[tch] = ev_comb[tch]
            continue
        t_ct = r.add("tensor", lambda e, tch=tch: e.transpose(
            ps_tr[tch % 2][0:E, 0:128], tp_sc[:, E:2 * E], id_f32[:, :]),
            deps=[ev_comb[tch],
                  ev_xn.get((DC - 2, tch)), ev_xn.get((DC - 1, tch)),
                  ev_evf.get(DC - 2), ev_evf.get(DC - 1)],
            tag=f"combT{tch}")
        ev_combT[tch] = r.add("scalar", lambda e, tch=tch: e.activation(
            comb_b[:, tch * 128:(tch + 1) * 128], ps_tr[tch % 2][0:E, 0:128], ACTF.Copy),
            deps=[t_ct], tag=f"combE{tch}")

    t_comb_j = r.join("scalar", [ev_combT[t] for t in range(TCHrun)], tag="combj")
    pA.close()

    if phases < 0.3:
        t_fin = []
        for tch in range(TCH):
            t_fin.append(r.add("sync", lambda e, tch=tch: e.dma_start(
                out=out_loc.ap()[tch * 128:(tch + 1) * 128, :],
                in_=x_loc.ap()[tch * 128:(tch + 1) * 128, :]),
                dma=True, deps=[t_comb_j], tag=f"dummyout{tch}"))
        r.join("sync", t_fin, tag="final")
        r.finalize_and_emit(nc, sems)
        p1.close()
        es.close()
        return nc


    pB = ExitStack()
    ps_qkv = [pB.enter_context(nc.psum_tensor(f"ps_qkv{i}", [128, TBW], F32)) for i in range(2)]
    ps_vtr = [pB.enter_context(nc.psum_tensor(f"ps_vtr{i}", [128, 128], BF16)) for i in range(2)]

    t_wqk = {}
    ev_epi = {}
    mm_last = {}
    t_vb = {}
    ev_vtr = {}
    vtr_cnt = 0
    for oc in range(3 * DC):
        t_wqk[oc] = r.add("gpsimd", lambda e, oc=oc: e.dma_start(
            out=wqk_sb[oc % 2][:], in_=wqkv_r.ap()[oc * 128:(oc + 1) * 128, :]),
            dma=True, deps=[mm_last.get((oc - 2, NTB - 1))], tag=f"wqkv{oc}")
        for tb in range(NTB):
            gi = oc * NTB + tb
            pp = gi % 2
            mt = None
            for dc in range(DC):
                deps = [t_wqk[oc], ev_epi.get(gi - 2)]
                if oc == 0:
                    deps += [ev_xn[(dc, tc2)] for tc2 in range(tb * QS, (tb + 1) * QS)]
                if gi < 2 and dc == 0:
                    deps += [t_comb_j, t_sg_tok[TCH - 1], ev_tr_last[TCH - 1],
                             ev_combT[TCH - 1]]
                mt = r.add("tensor", lambda e, oc=oc, tb=tb, dc=dc, pp=pp: e.matmul(
                    ps_qkv[pp][:, :], wqk_sb[oc % 2][:, dc * 128:(dc + 1) * 128],
                    xn_fm[:, dc * T + tb * TBW: dc * T + (tb + 1) * TBW],
                    start=(dc == 0), stop=(dc == DC - 1)),
                    deps=deps, tag=f"qkvmm{oc}_{tb}_{dc}", group=("qkv", oc, tb))
            mm_last[(oc, tb)] = mt
            if oc < 2 * DC:
                dst = qraw if oc < DC else kraw
                c0 = (oc % DC) * T + tb * TBW
                ev_epi[gi] = r.add("scalar", lambda e, dst=dst, c0=c0, pp=pp: e.activation(
                    dst[:, c0:c0 + TBW], ps_qkv[pp][:, :], ACTF.Copy),
                    deps=[mt], tag=f"qke{oc}_{tb}")
            else:
                vc = oc - 2 * DC
                t_ev = r.add("scalar", lambda e, pp=pp, tb=tb: e.activation(
                    vtmp[tb % 2][:, :], ps_qkv[pp][:, :], ACTF.Copy),
                    deps=[mt, t_vb.get((vc - 1, tb))], tag=f"vev{vc}_{tb}")
                ev_epi[gi] = t_ev
                for s in range(QS):
                    vtr_cnt += 1
                    vv = vtr_cnt % 2
                    t_vt = r.add("tensor", lambda e, s=s, tb=tb, vv=vv: e.transpose(
                        ps_vtr[vv][:, :], vtmp[tb % 2][:, s * 128:(s + 1) * 128], id_bf[:, :]),
                        deps=[t_ev, t_idb, ev_vtr.get(vtr_cnt - 2)] +
                             ([t_comb_j, t_sg_tok[TCH - 1]] if vtr_cnt <= 2 else []),
                        tag=f"vtr{vc}_{tb}_{s}")
                    ev_vtr[vtr_cnt] = r.add("vector", lambda e, s=s, tb=tb, vv=vv: e.tensor_copy(
                        vt_st[tb % 2][:, s * 128:(s + 1) * 128], ps_vtr[vv][:, :]),
                        deps=[t_vt, t_vb.get((vc - 1, tb))], tag=f"vte{vc}_{tb}_{s}")
                base = 2 * D * T + (vc // 2) * (TCH * 128 * 256) + (vc % 2) * 128 \
                    + tb * QS * 128 * 256
                t_vb[(vc, tb)] = r.add("sync", lambda e, base=base, tb=tb: e.dma_start(
                    out=dap(qkv_loc, base, [[256, 128], [128 * 256, QS], [1, 128]]),
                    in_=bass.AP(vt_st[tb % 2][:].tensor, vt_st[tb % 2][:].offset,
                                [[QS * 128, 128], [128, QS], [1, 128]])),
                    dma=True, deps=[ev_vtr[vtr_cnt - s2] for s2 in range(QS)], tag=f"vb{vc}_{tb}")

    t_qk_j = r.join("scalar", [ev_epi[oc * NTB + tb] for oc in range(2 * DC) for tb in range(NTB)],
                    tag="qkj")
    t_vtr_j = r.join("vector", [ev_vtr[i + 1] for i in range(vtr_cnt)], tag="vtrj")
    pB.close()

    if phases < 0.6:
        t_fin = []
        for tch in range(TCH):
            t_fin.append(r.add("sync", lambda e, tch=tch: e.dma_start(
                out=out_loc.ap()[tch * 128:(tch + 1) * 128, :],
                in_=x_loc.ap()[tch * 128:(tch + 1) * 128, :]),
                dma=True, deps=[t_qk_j, t_vtr_j] + list(t_vb.values()), tag=f"dummyout{tch}"))
        r.join("sync", t_fin, tag="final")
        r.finalize_and_emit(nc, sems)
        p1.close()
        es.close()
        return nc


    def qk_ln(raw, gamma, row_base, tag, gates, prev_outs=None):
        pC = ExitStack()
        ps_s1 = pC.enter_context(nc.psum_tensor(f"ps_s1{tag}", [1, T], F32))
        ps_s2 = pC.enter_context(nc.psum_tensor(f"ps_s2{tag}", [1, T], F32))
        mm1 = mm2 = None
        prev_mm2 = None
        for qc in range(DC):
            t_sqx = r.add("scalar", lambda e, qc=qc: e.activation(
                sq_bf[:, :], raw[:, qc * T:(qc + 1) * T], ACTF.Square),
                deps=[t_qk_j, prev_mm2], tag=f"lnsq{tag}{qc}")
            for tb in range(NTB):
                mm1 = r.add("tensor", lambda e, qc=qc, tb=tb: e.matmul(
                    ps_s1[0:1, tb * TBW:(tb + 1) * TBW], ones_c[:, 0:1],
                    raw[:, qc * T + tb * TBW: qc * T + (tb + 1) * TBW],
                    start=(qc == 0), stop=(qc == DC - 1)),
                    deps=[t_qk_j, t_onc] + (gates if qc == 0 else []), tag=f"lns1{tag}{qc}_{tb}")
                mm2 = r.add("tensor", lambda e, qc=qc, tb=tb: e.matmul(
                    ps_s2[0:1, tb * TBW:(tb + 1) * TBW], ones_c[:, 0:1],
                    sq_bf[:, tb * TBW:(tb + 1) * TBW],
                    start=(qc == 0), stop=(qc == DC - 1)),
                    deps=[t_sqx, t_onc] + (gates if qc == 0 else []), tag=f"lns2{tag}{qc}_{tb}")
            prev_mm2 = mm2
        t_m = r.add("vector", lambda e: e.tensor_scalar_mul(rows[0:1, 0:T], ps_s1[0:1, :], 1.0 / D),
                    deps=[mm1], tag=f"lnm{tag}")
        t_s = r.add("vector", lambda e: e.tensor_scalar_mul(rows[0:1, 2 * T:3 * T], ps_s2[0:1, :], 1.0 / D),
                    deps=[mm2], tag=f"lns{tag}")
        w1_ = r.add("vector", lambda e: e.tensor_tensor(rows[0:1, T:2 * T], rows[0:1, 0:T], rows[0:1, 0:T], OP.mult), deps=[t_m])
        w2_ = r.add("vector", lambda e: e.tensor_tensor(rows[0:1, 2 * T:3 * T], rows[0:1, 2 * T:3 * T], rows[0:1, T:2 * T], OP.subtract), deps=[t_s, w1_])
        t_v = r.add("vector", lambda e: e.tensor_scalar_add(rows[0:1, 2 * T:3 * T], rows[0:1, 2 * T:3 * T], 1e-6), deps=[w2_])
        t_sq2 = r.add("scalar", lambda e: e.activation(rows[0:1, T:2 * T], rows[0:1, 2 * T:3 * T], ACTF.Sqrt),
                      deps=[t_v], tag=f"lnsqrt{tag}")
        t_r = r.add("vector", lambda e: e.reciprocal(rows[0:1, T:2 * T], rows[0:1, T:2 * T]),
                    deps=[t_sq2])
        pC.close()
        pD = ExitStack()
        ps_bm = pD.enter_context(nc.psum_tensor(f"ps_bm{tag}", [128, TBW], F32))
        ps_br = pD.enter_context(nc.psum_tensor(f"ps_br{tag}", [128, TBW], F32))
        outs = []
        appl = []
        for tb in range(NTB):
            bgate = [t_m, t_s, t_r] + appl[-2:]
            bm = r.add("tensor", lambda e, tb=tb: e.matmul(
                ps_bm[:, :], ones_rf[0:1, :], rows[0:1, tb * TBW:(tb + 1) * TBW],
                start=True, stop=True),
                deps=[t_m, t_onrf] + bgate, tag=f"lnbm{tag}{tb}")
            br = r.add("tensor", lambda e, tb=tb: e.matmul(
                ps_br[:, :], ones_rf[0:1, :], rows[0:1, T + tb * TBW: T + (tb + 1) * TBW],
                start=True, stop=True),
                deps=[t_r, t_onrf] + bgate, tag=f"lnbr{tag}{tb}")
            for qc in range(DC):
                i = tb * DC + qc
                pp = i % 2
                war_st = outs[i - 2] if i >= 2 else (prev_outs[i - 2] if prev_outs else None)
                tA = r.add("vector", lambda e, qc=qc, tb=tb, pp=pp: e.tensor_tensor(
                    qn_sc[pp][:, :], raw[:, qc * T + tb * TBW: qc * T + (tb + 1) * TBW],
                    ps_bm[:, :], OP.subtract),
                    deps=[bm, war_st], tag=f"lnap{tag}{i}")
                tB = r.add("vector", lambda e, qc=qc, pp=pp: e.scalar_tensor_tensor(
                    qn_st[pp][:, :], qn_sc[pp][:, :], gamma[:, qc:qc + 1], ps_br[:, :],
                    OP.mult, OP.mult),
                    deps=[tA, br, t_gq2, t_gk, war_st], tag=f"lnap2{tag}{i}")
                appl.append(tB)
                td = r.add("sync", lambda e, qc=qc, tb=tb, pp=pp, row_base=row_base: e.dma_start(
                    out=qkv_loc.ap()[row_base + qc * 128: row_base + (qc + 1) * 128,
                                     tb * TBW:(tb + 1) * TBW],
                    in_=qn_st[pp][:, :]),
                    dma=True, deps=[tB], tag=f"lnb{tag}{i}")
                outs.append(td)
        pD.close()
        gate_next = r.join("vector", appl[-4:] + [t_m, t_s, t_r], tag=f"lng{tag}")
        return outs, gate_next

    qb_outs, gate_q = qk_ln(qraw, gq2_sb, 0, "q", gates=[t_qk_j, t_vtr_j])
    kb_outs, gate_k = qk_ln(kraw, gk_sb, D, "k", gates=[gate_q], prev_outs=qb_outs)

    if phases < 0.9:
        t_fin = []
        for tch in range(TCH):
            t_fin.append(r.add("sync", lambda e, tch=tch: e.dma_start(
                out=out_loc.ap()[tch * 128:(tch + 1) * 128, :],
                in_=x_loc.ap()[tch * 128:(tch + 1) * 128, :]),
                dma=True, deps=qb_outs + kb_outs + list(t_vb.values()), tag=f"dummyout{tch}"))
        r.join("sync", t_fin, tag="final")
        r.finalize_and_emit(nc, sems)
        p1.close()
        es.close()
        return nc


    all_bounce = qb_outs + kb_outs + list(t_vb.values())
    t_ag1 = r.add("gpsimd", lambda e: e.collective_compute(
        "AllGather", OP.bypass, replica_groups=[list(range(NC))],
        ins=[qkv_loc.ap().opt()], outs=[qkv_g.ap().opt()]),
        deps=all_bounce, tag="ag1")

    if debug:
        r.add("sync", lambda e: e.dma_start(out=dbg_xn.ap(), in_=xn_fm[:, :]),
              dma=True, deps=[t_qk_j], tag="dbgxn")
        r.add("sync", lambda e: e.dma_start(out=dbg_comb.ap(), in_=comb_b[:, :]),
              dma=True, deps=[t_comb_j], tag="dbgcomb")

    p1.close()

    if phases < 2:
        t_fin = []
        for tch in range(TCH):
            t_fin.append(r.add("sync", lambda e, tch=tch: e.dma_start(
                out=out_loc.ap()[tch * 128:(tch + 1) * 128, :],
                in_=x_loc.ap()[tch * 128:(tch + 1) * 128, :]),
                dma=True, deps=[t_ag1], tag=f"dummyout{tch}"))
        r.join("sync", t_fin, tag="final")
        r.finalize_and_emit(nc, sems)
        es.close()
        return nc

    p3 = ExitStack()
    A3 = p3.enter_context
    q2_sb = A3(nc.sbuf_tensor("q2_sb", [128, 2 * S], BF16))
    k2_sb = A3(nc.sbuf_tensor("k2_sb", [128, 2 * S], BF16))
    v2_sb = A3(nc.sbuf_tensor("v2_sb", [128, 2 * S], BF16))
    maskA = A3(nc.sbuf_tensor("maskA", [128, QS * TBW], BF16))
    maskB = A3(nc.sbuf_tensor("maskB", [128, QS * TBW], BF16))
    sA_sc = [A3(nc.sbuf_tensor(f"sA_sc{i}", [128, TBW], F32)) for i in range(2)]
    sB_sc = [A3(nc.sbuf_tensor(f"sB_sc{i}", [128, TBW], F32)) for i in range(3)]
    pt_sb = [A3(nc.sbuf_tensor(f"pt_sb{i}", [128, TBW], BF16)) for i in range(4)]
    nm_sb = A3(nc.sbuf_tensor("nm_sb", [128, 2 * QS + 2], F32))
    nmrow = [A3(nc.sbuf_tensor(f"nmrow{i}", [1, TBW], BF16)) for i in range(2)]
    l_row = [A3(nc.sbuf_tensor(f"l_row{i}", [1, TBW], F32)) for i in range(2)]
    linv_sb = [A3(nc.sbuf_tensor(f"linv_sb{i}", [128, TBW], F32)) for i in range(2)]
    ctx_st = [A3(nc.sbuf_tensor(f"ctx_st{i}", [128, TBW], BF16)) for i in range(2)]

    pP3 = ExitStack()
    ps_sa = [pP3.enter_context(nc.psum_tensor(f"ps_sa{i}", [128, TBW], F32)) for i in range(2)]
    ps_sb_ = [pP3.enter_context(nc.psum_tensor(f"ps_sb{i}", [128, TBW], F32)) for i in range(4)]
    ps_ctx = [pP3.enter_context(nc.psum_tensor("ps_ctx0", [128, TBW], F32))]
    ps_lb = pP3.enter_context(nc.psum_tensor("ps_lb", [128, TBW], F32))
    ps_l = ps_lb

    t_mA0 = r.add("gpsimd", lambda e: e.memset(maskA[:], 0.0), deps=[t_ag1])
    t_mB0 = r.add("gpsimd", lambda e: e.memset(maskB[:], 0.0), deps=[t_ag1])
    t_mA = []
    t_mB = []
    for qs in range(QS):
        t_mA.append(r.add("gpsimd", lambda e, qs=qs: e.affine_select(
            out=maskA[:, qs * TBW:(qs + 1) * TBW], in_=maskA[:, qs * TBW:(qs + 1) * TBW],
            compare_op=OP.is_ge, fill=NEG_BIG,
            base=qs * 128, pattern=[[-1, TBW]], channel_multiplier=1), deps=[t_mA0]))
    for kc in range(QS):
        t_mB.append(r.add("gpsimd", lambda e, kc=kc: e.affine_select(
            out=maskB[:, kc * TBW:(kc + 1) * TBW], in_=maskB[:, kc * TBW:(kc + 1) * TBW],
            compare_op=OP.is_ge, fill=NEG_BIG,
            base=-kc * 128, pattern=[[1, TBW]], channel_multiplier=-1), deps=[t_mB0]))
    t_mA_j = r.join("gpsimd", t_mA, tag="mAj")
    t_mB_j = r.join("gpsimd", t_mB, tag="mBj")

    t_qld_all, t_kld_all = [], []
    for hh in range(2):
        t_qld_all.append(r.add("sync", lambda e, hh=hh: e.dma_start(
            out=bass.AP(q2_sb[:].tensor, q2_sb[:].offset + hh * S,
                        [[2 * S, 128], [T, NC], [1, T]]),
            in_=dap(qkv_g, mcell["qk"] + hh * 128 * T,
                    [[T, 128], [3 * D * T, NC], [1, T]])),
            dma=True, deps=[t_ag1], tag=f"qld{hh}"))
        t_kld_all.append(r.add("sync", lambda e, hh=hh: e.dma_start(
            out=bass.AP(k2_sb[:].tensor, k2_sb[:].offset + hh * S,
                        [[2 * S, 128], [T, NC], [1, T]]),
            in_=dap(qkv_g, mcell["qk"] + (D + hh * 128) * T,
                    [[T, 128], [3 * D * T, NC], [1, T]])),
            dma=True, deps=[t_ag1], tag=f"kld{hh}"))
    t_vld = {}
    for j in range(NC):
        t_vld[j] = r.add("sync", lambda e, j=j: e.dma_start(
            out=v2_sb[:, j * (TCH * 256): (j + 1) * (TCH * 256)],
            in_=dap(qkv_g, mcell["v"] + (j * 3 * D + 2 * D) * T,
                    [[256, 128], [128 * 256, TCH], [1, 256]])),
            dma=True, deps=[t_ag1], tag=f"vld{j}")
    t_qld = {k_: t_qld_all[k_[1]] for k_ in [(j, hh) for j in range(NC) for hh in range(2)]}
    t_kld = {k_: t_kld_all[k_[1]] for k_ in [(j, hh) for j in range(NC) for hh in range(2)]}

    NQT = S // TBW
    seq = [(hh, qt) for hh in range(2) for qt in range(NQT)]
    ev_red = {}
    ev_exp = {}
    lastA = {}
    ev_ptuse = {}
    ev_norm = {}
    ev_nmT = {}
    ev_nmrow = {}
    ev_lastB = {}
    ev_linv = {}
    t_ctxst = []

    Aseq = {}
    t_nmdma = {}
    t_nmj = {}

    def pass_A(i):
        hh, qt = seq[i]
        scr = 2 * QS
        for kt in range(qt + 1):
            for qs in range(QS):
                eix = kt * QS + qs
                c0 = 2 * qs
                pp = eix % 2
                diag = (kt == qt)
                war = [Aseq.get((i, eix - 2))]
                if eix < 2:
                    war.append(Aseq.get((i - 1, lastA.get(i - 1, 0) - 1)))
                    war.append(Aseq.get((i - 1, lastA.get(i - 1, 0) - 2)))
                mm = r.add("tensor", lambda e, hh=hh, qt=qt, qs=qs, kt=kt, pp=pp: e.matmul(
                    ps_sa[pp][:, :],
                    q2_sb[:, hh * S + qt * TBW + qs * 128: hh * S + qt * TBW + (qs + 1) * 128],
                    k2_sb[:, hh * S + kt * TBW: hh * S + (kt + 1) * TBW],
                    start=True, stop=True),
                    deps=[t_qld[((qt * TBW) // T, hh)], t_kld[((kt * TBW) // T, hh)]] + war,
                    tag=f"pA{i}_{qs}_{kt}")
                src_ = ps_sa[pp]
                dep_m = [mm]
                if diag:
                    tmk = r.add("vector", lambda e, qs=qs, pp=pp: e.tensor_tensor(
                        sA_sc[pp][:, :], ps_sa[pp][:, :],
                        maskA[:, qs * TBW:(qs + 1) * TBW], OP.add),
                        deps=[mm, t_mA_j], tag=f"pAm{i}_{qs}")
                    src_ = sA_sc[pp]
                    dep_m = [tmk]
                if kt == 0:
                    ev_red[(i, qs, kt)] = r.add("vector", lambda e, src_=src_, c0=c0: e.reduce_max(
                        nm_sb[:, c0:c0 + 1], src_[:, :], axis=AX, negate=True),
                        deps=dep_m + [t_nmdma.get((i - 1, qs)), t_nmdma.get((i - 2, qs))],
                        tag=f"pAr{i}_{qs}_{kt}")
                else:
                    t_rm = r.add("vector", lambda e, src_=src_: e.reduce_max(
                        nm_sb[:, scr:scr + 1], src_[:, :], axis=AX, negate=True),
                        deps=dep_m + [ev_red[(i, qs, kt - 1)]], tag=f"pAr{i}_{qs}_{kt}")
                    ev_red[(i, qs, kt)] = r.add("vector", lambda e, kt=kt, c0=c0: e.tensor_tensor(
                        nm_sb[:, c0 + kt % 2:c0 + kt % 2 + 1],
                        nm_sb[:, c0 + (kt - 1) % 2:c0 + (kt - 1) % 2 + 1],
                        nm_sb[:, scr:scr + 1], OP.min),
                        deps=[t_rm, ev_red[(i, qs, kt - 1)]])
                Aseq[(i, eix)] = ev_red[(i, qs, kt)]
                yield
        lastA[i] = (qt + 1) * QS
        for qs in range(QS):
            fin = 2 * qs + qt % 2
            t_nmdma[(i, qs)] = r.add("gpsimd", lambda e, i=i, qs=qs, fin=fin: e.dma_start(
                out=nmrow[i % 2][0:1, qs * 128:(qs + 1) * 128],
                in_=nm_sb[:, fin:fin + 1]),
                dma=True, deps=[ev_red[(i, qs, qt)], ev_lastB.get(i - 2)],
                tag=f"nmD{i}_{qs}")
        t_nmj[i] = r.join("gpsimd", [t_nmdma[(i, qs)] for qs in range(QS)], tag=f"nmJ{i}")
        yield

    Bexp = []
    Pgrp = {}
    Qgrp = {}
    Bgrp_base = {}
    sB_last = {}
    CH = 4

    def emit_exp(i, kc):
        hh, qt = seq[i]
        g = Bgrp_base[i] + kc
        bb = g % CH
        diag = (kc >= qt * QS)
        if diag:
            kcd = kc - qt * QS
            tmk = r.add("vector", lambda e, kcd=kcd, bb=bb: e.tensor_tensor(
                sB_sc[bb % 2][:, :], ps_sb_[bb][:, :],
                maskB[:, kcd * TBW:(kcd + 1) * TBW], OP.add),
                deps=[Pgrp[g // CH], Qgrp.get(g // CH - 1), t_mB_j,
                      sB_last.get(bb % 2)], tag=f"pBm{i}_{kc}")
            te = r.add("scalar", lambda e, bb=bb: e.activation(
                pt_sb[bb][:, :], sB_sc[bb % 2][:, :], ACTF.Exp),
                deps=[tmk], tag=f"pBe{i}_{kc}")
            sB_last[bb % 2] = te
        else:
            te = r.add("scalar", lambda e, bb=bb: e.activation(
                pt_sb[bb][:, :], ps_sb_[bb][:, :], ACTF.Exp),
                deps=[Pgrp[g // CH], Qgrp.get(g // CH - 1)], tag=f"pBe{i}_{kc}")
        assert len(Bexp) == g
        Bexp.append(te)

    def emit_P(i, c):
        hh, qt = seq[i]
        gc = (Bgrp_base[i] + c * CH) // CH
        grp = ("pbP", i, c)
        war = Bexp[CH * gc - 1] if CH * gc >= 1 else None
        tok = None
        for j in range(CH):
            kc = c * CH + j
            tok = r.add("tensor", lambda e, hh=hh, qt=qt, kc=kc, j=j: e.matmul(
                ps_sb_[j][:, :],
                k2_sb[:, hh * S + kc * 128: hh * S + (kc + 1) * 128],
                q2_sb[:, hh * S + qt * TBW: hh * S + (qt + 1) * TBW],
                start=True, stop=False),
                deps=[war], tag=f"pB{i}_{kc}", group=grp)
            tok = r.add("tensor", lambda e, i=i, j=j: e.matmul(
                ps_sb_[j][:, :], ones_rb[0:1, :], nmrow[i % 2][0:1, :],
                start=False, stop=True),
                deps=[t_nmj[i], t_onrb], tag=f"pBr{i}_{kc}", group=grp)
        Pgrp[gc] = tok

    def emit_Q(i, c):
        hh, qt = seq[i]
        nkc = (qt + 1) * QS
        gc = (Bgrp_base[i] + c * CH) // CH
        grp = ("pbQ", i, c)
        war = Bexp[CH * gc + CH - 1]
        tok = None
        for j in range(CH):
            kt_ = c * CH + j
            tok = r.add("tensor", lambda e, kt_=kt_, j=j, nkc=nkc: e.matmul(
                ps_l[0:1, :], ones_c[:, 0:1], pt_sb[j][:, :],
                start=(kt_ == 0), stop=(kt_ == nkc - 1)),
                deps=[war, t_onc,
                      (ev_linv.get(i - 1) if kt_ == 0 else None)],
                tag=f"pBl{i}_{kt_}", group=grp)
            tok = r.add("tensor", lambda e, hh=hh, kt_=kt_, j=j, nkc=nkc, i=i: e.matmul(
                ps_ctx[0][:, :],
                v2_sb[:, kt_ * 256 + hh * 128: kt_ * 256 + (hh + 1) * 128],
                pt_sb[j][:, :],
                start=(kt_ == 0), stop=(kt_ == nkc - 1)),
                deps=[war, t_vld[(kt_ * 128) // T],
                      (ev_norm.get(i - 1) if kt_ == 0 else None)],
                tag=f"pBc{i}_{kt_}", group=grp)
            ev_ptuse[(i, kt_)] = tok
        Qgrp[gc] = tok

    def pass_B(i):
        hh, qt = seq[i]
        nkc = (qt + 1) * QS
        nch = nkc // CH
        Bgrp_base[i] = len(Bexp)
        for c in range(nch):
            if c >= 1:
                for j in range(CH):
                    emit_exp(i, CH * (c - 1) + j)
            emit_P(i, c)
            if c >= 1:
                emit_Q(i, c - 1)
            yield
        for j in range(CH):
            emit_exp(i, CH * (nch - 1) + j)
        emit_Q(i, nch - 1)
        ev_lastB[i] = ev_ptuse[(i, nkc - 1)]
        t_lev = r.add("vector", lambda e, i=i: e.tensor_copy(l_row[i % 2][0:1, :], ps_l[0:1, 0:TBW]),
                      deps=[ev_ptuse[(i, nkc - 1)]], tag=f"lev{i}")
        t_lrec = r.add("vector", lambda e, i=i: e.reciprocal(l_row[i % 2][0:1, :], l_row[i % 2][0:1, :]),
                       deps=[t_lev])
        yield
        yield
        t_lb = r.add("tensor", lambda e, i=i: e.matmul(
            ps_lb[:, :], ones_rf[0:1, :], l_row[i % 2][0:1, :], start=True, stop=True),
            deps=[t_lrec, t_onrf, ev_linv.get(i - 1)], tag=f"lb{i}")
        ev_linv[i] = r.add("vector", lambda e, i=i: e.tensor_copy(linv_sb[i % 2][:, :], ps_lb[:, :]),
                           deps=[t_lb], tag=f"linv{i}")
        ev_norm[i] = r.add("vector", lambda e, i=i: e.tensor_tensor(
            ctx_st[i % 2][:, :], ps_ctx[0][:, :], linv_sb[i % 2][:, :], OP.mult),
            deps=[ev_linv[i], ev_ptuse[(i, nkc - 1)],
                  (t_ctxst[i - 2] if i >= 2 else None)], tag=f"norm{i}")
        t_ctxst.append(r.add("sync", lambda e, hh=hh, qt=qt, i=i: e.dma_start(
            out=ctx_loc.ap()[hh * 128:(hh + 1) * 128, qt * TBW:(qt + 1) * TBW],
            in_=ctx_st[i % 2][:, :]),
            dma=True, deps=[ev_norm[i]], tag=f"ctxst{i}"))
        yield

    SENT = object()
    for i in range(len(seq)):
        ga = pass_A(i)
        gb = pass_B(i - 1) if i > 0 else None
        a_done = False
        b_done = gb is None
        while not (a_done and b_done):
            if not b_done:
                b_done = next(gb, SENT) is SENT
            for _ in range(4):
                if not a_done:
                    a_done = next(ga, SENT) is SENT
    for _ in pass_B(len(seq) - 1):
        pass

    t_ag2 = r.add("gpsimd", lambda e: e.collective_compute(
        "AllGather", OP.bypass, replica_groups=[list(range(NC))],
        ins=[ctx_loc.ap().opt()], outs=[ctx_g.ap().opt()]),
        deps=t_ctxst, tag="ag2")

    pP3.close()
    p3.close()

    if phases < 3:
        t_fin = []
        for tch in range(TCH):
            t_fin.append(r.add("sync", lambda e, tch=tch: e.dma_start(
                out=out_loc.ap()[tch * 128:(tch + 1) * 128, :],
                in_=x_loc.ap()[tch * 128:(tch + 1) * 128, :]),
                dma=True, deps=[t_ag2], tag=f"dummyout{tch}"))
        r.join("sync", t_fin, tag="final")
        r.finalize_and_emit(nc, sems)
        es.close()
        return nc

    p2a = ExitStack()
    acc = p2a.enter_context(nc.sbuf_tensor("acc", [128, TCH * D], F32))
    p2 = ExitStack()
    A2 = p2.enter_context
    y_sb = A2(nc.sbuf_tensor("y_sb", [128, GE * FC * NTB * TBW], BF16))
    wg_sb = [A2(nc.sbuf_tensor(f"wg_sb{i}", [128, DC * 128], BF16)) for i in range(2)]
    w1s_sb = [A2(nc.sbuf_tensor(f"w1s_sb{i}", [128, DC * 128], BF16)) for i in range(2)]
    w2b_sb = [A2(nc.sbuf_tensor(f"w2b_sb{i}", [128, GE * FC * DQW], BF16)) for i in range(2)]
    gel_sb = [A2(nc.sbuf_tensor(f"gel_sb{i}", [128, TBW], F32)) for i in range(2)]
    cb_sb = [A2(nc.sbuf_tensor(f"cb_sb{i}", [128, TBW], F32)) for i in range(2)]
    yt_sb = [A2(nc.sbuf_tensor(f"yt_sb{i}", [128, TBW], F32)) for i in range(2)]

    pP2 = ExitStack()
    ps_a = [pP2.enter_context(nc.psum_tensor(f"ps_a{i}", [128, TBW], F32)) for i in range(2)]
    ps_b = [pP2.enter_context(nc.psum_tensor(f"ps_b{i}", [128, TBW], F32)) for i in range(2)]
    ps_cb = [pP2.enter_context(nc.psum_tensor(f"ps_cb{i}", [128, TBW], F32)) for i in range(2)]
    ps_o = [pP2.enter_context(nc.psum_tensor(f"ps_o{i}", [128, DQW], F32)) for i in range(2)]

    gate_p2 = r.join("vector", [ev_norm[len(seq) - 1], ev_lastB[len(seq) - 1],
                                t_ctxst[-1], t_ctxst[-2]], tag="p3done")

    ev_gel = {}
    ev_yt = {}
    ev_y = {}
    ev_yfc = {}
    t_cb = {}
    ev_spill = {}
    t_spill_j = {}
    ev_bmmw = {}
    cnt_a = 0
    sb_cnt = 0
    for g in range(NG):
        exps = [g * GE + i for i in range(GE)]
        for esl, e_ in enumerate(exps):
            routed = e_ < E
            for fc in range(FC):
                wi = (g * GE + esl) * FC + fc
                t_wg = r.add("gpsimd", lambda e, e_=e_, fc=fc, wi=wi: e.dma_start(
                    out=wg_sb[wi % 2][:],
                    in_=g_all.ap()[(e_ * FC + fc) * 128:(e_ * FC + fc + 1) * 128, :]),
                    dma=True, deps=[gate_p2, ev_yfc.get((wi - 2, NTB - 1))], tag=f"wg{wi}")
                t_w1s = r.add("gpsimd", lambda e, e_=e_, fc=fc, wi=wi: e.dma_start(
                    out=w1s_sb[wi % 2][:],
                    in_=w1_all.ap()[(e_ * FC + fc) * 128:(e_ * FC + fc + 1) * 128, :]),
                    dma=True, deps=[gate_p2, ev_yfc.get((wi - 2, NTB - 1))], tag=f"w1{wi}")
                for tb in range(NTB):
                    if routed and fc == 0:
                        t_cbm = r.add("tensor", lambda e, e_=e_, tb=tb: e.matmul(
                            ps_cb[tb][:, :], onehot[:, e_ * 128:(e_ + 1) * 128],
                            comb_b[:, tb * TBW:(tb + 1) * TBW], start=True, stop=True),
                            deps=[t_comb_j, t_oh_j,
                                  t_cb.get((e_ - 1, tb))],
                            tag=f"cb{e_}_{tb}")
                        t_cb[(e_, tb)] = r.add("vector", lambda e, tb=tb: e.tensor_copy(
                            cb_sb[tb][:, :], ps_cb[tb][:, :]),
                            deps=[t_cbm, ev_yt.get((e_ - 1, FC - 1, tb))],
                            tag=f"cbe{e_}_{tb}")
                    cnt_a += 1
                    ppa = cnt_a % 2
                    mt = None
                    for dc in range(DC):
                        mt = r.add("tensor", lambda e, wi=wi, dc=dc, tb=tb, ppa=ppa: e.matmul(
                            ps_a[ppa][:, :], wg_sb[wi % 2][:, dc * 128:(dc + 1) * 128],
                            xn_fm[:, dc * T + tb * TBW: dc * T + (tb + 1) * TBW],
                            start=(dc == 0), stop=(dc == DC - 1)),
                            deps=[t_wg, ev_gel.get(cnt_a - 2)], tag=f"ffA{wi}_{tb}_{dc}",
                            group=("ffa", wi, tb))
                    mtb = None
                    for dc in range(DC):
                        mtb = r.add("tensor", lambda e, wi=wi, dc=dc, tb=tb, ppa=ppa: e.matmul(
                            ps_b[ppa][:, :], w1s_sb[wi % 2][:, dc * 128:(dc + 1) * 128],
                            xn_fm[:, dc * T + tb * TBW: dc * T + (tb + 1) * TBW],
                            start=(dc == 0), stop=(dc == DC - 1)),
                            deps=[t_w1s, ev_yt.get(("b", cnt_a - 2))], tag=f"ffB{wi}_{tb}_{dc}",
                            group=("ffb", wi, tb))
                    bidx = fc * EXP + e_
                    ev_gel[cnt_a] = r.add("scalar", lambda e, bidx=bidx, ppa=ppa: e.activation(
                        gel_sb[ppa][:, :], ps_a[ppa][:, :], ACTF.Gelu,
                        bias=gbT[:, bidx:bidx + 1]),
                        deps=[mt, t_bias_j, ev_y.get(cnt_a - 2)], tag=f"gel{wi}_{tb}")
                    slot = ((esl * FC + fc) * NTB + tb) * TBW
                    warY = ev_bmmw.get(("g", g - 1))
                    if routed:
                        tyt = r.add("vector", lambda e, bidx=bidx, ppa=ppa, tb=tb: e.scalar_tensor_tensor(
                            yt_sb[ppa][:, :], ps_b[ppa][:, :], b1T[:, bidx:bidx + 1],
                            cb_sb[tb][:, :], OP.add, OP.mult),
                            deps=[mtb, t_cb[(e_, tb)], t_bias_j, ev_y.get(cnt_a - 2)],
                            tag=f"yt{wi}_{tb}")
                        ev_yt[("b", cnt_a)] = tyt
                        ev_yt[(e_, fc, tb)] = tyt
                        ev_y[cnt_a] = r.add("vector", lambda e, slot=slot, ppa=ppa: e.tensor_tensor(
                            y_sb[:, slot:slot + TBW], yt_sb[ppa][:, :], gel_sb[ppa][:, :], OP.mult),
                            deps=[tyt, ev_gel[cnt_a], warY], tag=f"y{wi}_{tb}")
                    else:
                        tyt = r.add("vector", lambda e, bidx=bidx, ppa=ppa, slot=slot: e.scalar_tensor_tensor(
                            y_sb[:, slot:slot + TBW], ps_b[ppa][:, :], b1T[:, bidx:bidx + 1],
                            gel_sb[ppa][:, :], OP.add, OP.mult),
                            deps=[mtb, ev_gel[cnt_a], t_bias_j, warY], tag=f"ysh{wi}_{tb}")
                        ev_yt[("b", cnt_a)] = tyt
                        ev_yt[(e_, fc, tb)] = tyt
                        ev_y[cnt_a] = tyt
                    ev_yfc[(wi, tb)] = ev_y[cnt_a]
        y_done = {tb: [ev_yfc[((g * GE + esl) * FC + fc, tb)]
                       for esl in range(GE) for fc in range(FC)] for tb in range(NTB)}
        bmms_all = []
        for dq in range(NDQ):
            w = g * NDQ + dq
            dmas = []
            for esl, e_ in enumerate(exps):
                dmas.append(r.add("gpsimd", lambda e, e_=e_, dq=dq, w=w, esl=esl: e.dma_start(
                    out=w2b_sb[w % 2][:, esl * FC * DQW:(esl + 1) * FC * DQW],
                    in_=w2_all.ap()[(e_ * NDQ + dq) * 128:(e_ * NDQ + dq + 1) * 128, :]),
                    dma=True, deps=[gate_p2, ev_bmmw.get(w - 2)], tag=f"w2{w}_{esl}"))
            t_w2b = r.join("gpsimd", dmas, tag=f"w2j{w}")
            bmms = []
            for tb in range(NTB):
                for tc in range(QS):
                    oi = (tb * QS + tc) * NDQ + dq
                    sb_cnt += 1
                    po = sb_cnt % 2
                    warO = ev_spill.get(sb_cnt - 2)
                    if warO is None and g > 0:
                        warO = t_spill_j[g - 1]
                    mt = None
                    nmm = GE * FC
                    im = 0
                    for esl in range(GE):
                        for fc in range(FC):
                            slot = ((esl * FC + fc) * NTB + tb) * TBW + tc * 128
                            mt = r.add("tensor", lambda e, slot=slot, esl=esl, fc=fc, po=po, im=im, nmm=nmm, w=w: e.matmul(
                                ps_o[po][:, :], y_sb[:, slot:slot + 128],
                                w2b_sb[w % 2][:, (esl * FC + fc) * DQW:(esl * FC + fc + 1) * DQW],
                                start=(im == 0), stop=(im == nmm - 1)),
                                deps=[t_w2b] + ([warO] + y_done[tb] if im == 0 else []),
                                tag=f"ffO{g}_{oi}_{im}", group=("ffo", g, sb_cnt))
                            im += 1
                    bmms.append(mt)
                    bmms_all.append(mt)
                    aslice = (tb * QS + tc) * D + dq * DQW
                    if g == 0:
                        ev_spill[sb_cnt] = r.add("vector", lambda e, aslice=aslice, po=po: e.tensor_copy(
                            acc[:, aslice:aslice + DQW], ps_o[po][:, :]),
                            deps=[mt], tag=f"sp{g}_{oi}")
                    else:
                        ev_spill[sb_cnt] = r.add("vector", lambda e, aslice=aslice, po=po: e.tensor_tensor(
                            acc[:, aslice:aslice + DQW], ps_o[po][:, :],
                            acc[:, aslice:aslice + DQW], OP.add),
                            deps=[mt, ev_spill.get(sb_cnt - TCH * NDQ)], tag=f"sp{g}_{oi}")
            ev_bmmw[w] = r.join("tensor", bmms, tag=f"bmmw{w}")
        ev_bmmw[("g", g)] = ev_bmmw[g * NDQ + NDQ - 1]
        t_spill_j[g] = r.join("vector", [ev_spill[sb_cnt - i] for i in range(TCH * NDQ)],
                              tag=f"spj{g}")

    t_ffn_j = t_spill_j[NG - 1]
    if debug:
        t_dbg_acc = r.add("sync", lambda e: e.dma_start(out=dbg_acc.ap(), in_=acc[:, :]),
                          dma=True, deps=[t_ffn_j], tag="dbgacc")
        t_ffn_j = r.join("vector", [t_ffn_j, t_dbg_acc], tag="ffnjd")
    pP2.close()
    p2.close()

    if phases < 4:
        t_fin = []
        for tch in range(TCH):
            t_fin.append(r.add("sync", lambda e, tch=tch: e.dma_start(
                out=out_loc.ap()[tch * 128:(tch + 1) * 128, :],
                in_=x_loc.ap()[tch * 128:(tch + 1) * 128, :]),
                dma=True, deps=[t_ffn_j], tag=f"dummyout{tch}"))
        r.join("sync", t_fin, tag="final")
        p2a.close()
        r.finalize_and_emit(nc, sems)
        es.close()
        return nc

    p4 = ExitStack()
    A4 = p4.enter_context
    CG = 4 if DC % 4 == 0 else 2
    ctx_gr = [A4(nc.sbuf_tensor(f"ctx_gr{i}", [128, CG * T], BF16)) for i in range(2)]
    wo_sb = [A4(nc.sbuf_tensor(f"wo_sb{i}", [128, CG * D], BF16)) for i in range(2)]
    x2_sb = [A4(nc.sbuf_tensor(f"x2_sb{i}", [128, D], F32)) for i in range(2)]
    xnT_sb = [A4(nc.sbuf_tensor(f"xnT_sb{i}", [128, D], BF16)) for i in range(2)]
    stg_sb = [A4(nc.sbuf_tensor(f"stg_sb{i}", [128, D], F32)) for i in range(2)]

    pP4 = ExitStack()
    ps_p = [pP4.enter_context(nc.psum_tensor(f"ps_p{i}", [128, DQW], F32)) for i in range(2)]
    ps_tx = [pP4.enter_context(nc.psum_tensor(f"ps_tx{i}", [128, 128], BF16)) for i in range(2)]

    gate_p4 = t_ffn_j

    ev_psp = {}
    pj_cnt = 0
    last_psp = {}
    for cg in range(DC // CG):
        ccs = [cg * CG + i for i in range(CG)]
        lds = [r.add("sync", lambda e, cg=cg: e.dma_start(
            out=bass.AP(ctx_gr[cg % 2][:].tensor, ctx_gr[cg % 2][:].offset,
                        [[CG * T, 128], [T, CG], [1, T]]),
            in_=dap(ctx_g, mcell["ctx"] + cg * CG * 128 * S,
                    [[S, 128], [128 * S, CG], [1, T]])),
            dma=True, deps=[t_ag2, gate_p4, last_psp.get(cg - 2)],
            tag=f"ctxg{cg}")]
        for cc in ccs:
            lds.append(r.add("gpsimd", lambda e, cc=cc, cg=cg: e.dma_start(
                out=wo_sb[cg % 2][:, (cc % CG) * D:(cc % CG + 1) * D],
                in_=wout_r.ap()[cc * 128:(cc + 1) * 128, :]),
                dma=True, deps=[gate_p4, last_psp.get(cg - 2)],
                tag=f"wog{cc}"))
        t_ldj = r.join("sync", lds, tag=f"ldj{cg}")
        for tch in range(TCH):
            for dq in range(NDQ):
                pj_cnt += 1
                pi = pj_cnt % 2
                mt = None
                for i, cc in enumerate(ccs):
                    mt = r.add("tensor", lambda e, cc=cc, i=i, tch=tch, dq=dq, pi=pi, cg=cg: e.matmul(
                        ps_p[pi][:, :],
                        ctx_gr[cg % 2][:, (cc % CG) * T + tch * 128:(cc % CG) * T + (tch + 1) * 128],
                        wo_sb[cg % 2][:, (cc % CG) * D + dq * DQW:(cc % CG) * D + (dq + 1) * DQW],
                        start=(i == 0), stop=(i == CG - 1)),
                        deps=[t_ldj] + ([ev_psp.get(pj_cnt - 2)] if i == 0 else []),
                        tag=f"prj{cg}_{tch}_{dq}_{i}", group=("prj", pj_cnt))
                aslice = tch * D + dq * DQW
                ev_psp[pj_cnt] = r.add("vector", lambda e, aslice=aslice, pi=pi: e.tensor_tensor(
                    acc[:, aslice:aslice + DQW], ps_p[pi][:, :],
                    acc[:, aslice:aslice + DQW], OP.add),
                    deps=[mt, t_ffn_j, ev_psp.get(pj_cnt - TCH * NDQ)], tag=f"prjsp{cg}_{tch}_{dq}")
        last_psp[cg] = r.join("vector", [ev_psp[pj_cnt - i] for i in range(TCH * NDQ)],
                              tag=f"pspj{cg}")

    t_proj_j = last_psp[DC // CG - 1]
    t_out = []
    ev_fin = {}
    ev_xnT = {}
    fin_cnt = 0
    tx_cnt = 0
    for tch in range(TCH):
        t_x2 = r.add("sync", lambda e, tch=tch: e.dma_start(
            out=x2_sb[tch % 2][:], in_=x_loc.ap()[tch * 128:(tch + 1) * 128, :]),
            dma=True, deps=[t_ag2, (t_out[tch - 2] if tch >= 2 else None)], tag=f"x2{tch}")
        xts = []
        for dc in range(DC):
            tx_cnt += 1
            pv = tx_cnt % 2
            t_tx = r.add("tensor", lambda e, dc=dc, tch=tch, pv=pv: e.transpose(
                ps_tx[pv][:, :], xn_fm[:, dc * T + tch * 128: dc * T + (tch + 1) * 128],
                id_bf[:, :]),
                deps=[t_idb, ev_xnT.get(tx_cnt - 2),
                      (t_out[tch - 2] if (tch >= 2 and dc < 2) else None),
                      (t_proj_j if (tch == 0 and dc < 2) else None)],
                tag=f"xtx{tch}_{dc}")
            ev_xnT[tx_cnt] = r.add("vector", lambda e, dc=dc, tch=tch, pv=pv: e.tensor_copy(
                xnT_sb[tch % 2][:, dc * 128:(dc + 1) * 128], ps_tx[pv][:, :]),
                deps=[t_tx, (t_out[tch - 2] if tch >= 2 else None)], tag=f"xte{tch}_{dc}")
            xts.append(ev_xnT[tx_cnt])
        for dq in range(NDQ):
            fin_cnt += 1
            pi = fin_cnt % 2
            m1 = r.add("tensor", lambda e, tch=tch, dq=dq, pi=pi: e.matmul(
                ps_p[pi][:, :], comb_b[:, tch * 128:(tch + 1) * 128],
                rb2_sb[:, dq * DQW:(dq + 1) * DQW], start=True, stop=False),
                deps=[t_comb_j, t_rb2, t_proj_j, ev_fin.get(fin_cnt - 2)],
                tag=f"fb2{tch}_{dq}", group=("fin", fin_cnt))
            m2 = r.add("tensor", lambda e, dq=dq, pi=pi: e.matmul(
                ps_p[pi][:, :], ones128b[0:SH, 0:128], sb2_sb[:, dq * DQW:(dq + 1) * DQW],
                start=False, stop=True),
                deps=[t_sb2, t_o128b], tag=f"fsb2{tch}_{dq}", group=("fin", fin_cnt))
            aslice = tch * D + dq * DQW
            f1 = r.add("vector", lambda e, aslice=aslice, pi=pi, tch=tch, dq=dq: e.tensor_tensor(
                stg_sb[tch % 2][:, dq * DQW:(dq + 1) * DQW], ps_p[pi][:, :],
                acc[:, aslice:aslice + DQW], OP.add),
                deps=[m2, t_proj_j,
                      (t_out[tch - 2] if tch >= 2 else None)], tag=f"f1_{tch}_{dq}")
            f2 = r.add("vector", lambda e, tch=tch, dq=dq: e.tensor_tensor(
                stg_sb[tch % 2][:, dq * DQW:(dq + 1) * DQW],
                stg_sb[tch % 2][:, dq * DQW:(dq + 1) * DQW],
                x2_sb[tch % 2][:, dq * DQW:(dq + 1) * DQW], OP.add),
                deps=[t_x2, f1])
            ev_fin[fin_cnt] = r.add("vector", lambda e, tch=tch, dq=dq: e.tensor_tensor(
                stg_sb[tch % 2][:, dq * DQW:(dq + 1) * DQW],
                stg_sb[tch % 2][:, dq * DQW:(dq + 1) * DQW],
                xnT_sb[tch % 2][:, dq * DQW:(dq + 1) * DQW], OP.add),
                deps=[f2] + xts, tag=f"fin{tch}_{dq}")
        t_out.append(r.add("sync", lambda e, tch=tch: e.dma_start(
            out=out_loc.ap()[tch * 128:(tch + 1) * 128, :], in_=stg_sb[tch % 2][:, :]),
            dma=True, deps=[ev_fin[fin_cnt - i] for i in range(NDQ)], tag=f"out{tch}"))

    r.join("sync", t_out, tag="final")

    pP4.close()
    p4.close()
    p2a.close()

    r.finalize_and_emit(nc, sems)
    es.close()
    return nc


def prep_inputs(cfg, inp):
    cf = derive(cfg)
    D, S, F, E, SH = cf["D"], cf["S"], cf["F"], cf["E"], cf["SH"]
    NC, DC, FC = cf["NC"], cf["DC"], cf["FC"]
    T, DQW, NDQ, EXP = cf["T"], cf["DQW"], cf["NDQ"], cf["EXP"]

    f32 = lambda a: np.ascontiguousarray(np.asarray(a, dtype=np.float32))
    x = f32(inp["L0_attn_x"]).reshape(S, D)
    Wqkv = f32(inp["L0_attn_Wqkv"])
    outW = f32(inp["L0_attn_out_W"])
    wa = f32(inp["L0_ffn_wa"])
    gq = f32(inp["L0_attn_q_ln_scale"])
    gk = f32(inp["L0_attn_k_ln_scale"])
    rg, rgb = f32(inp["L0_ffn_rg"]), f32(inp["L0_ffn_rgb"])
    rw1, rb1 = f32(inp["L0_ffn_rw1"]), f32(inp["L0_ffn_rb1"])
    rw2, rb2 = f32(inp["L0_ffn_rw2"]), f32(inp["L0_ffn_rb2"])
    sg, sgb = f32(inp["L0_ffn_sg"]), f32(inp["L0_ffn_sgb"])
    sw1, sb1 = f32(inp["L0_ffn_sw1"]), f32(inp["L0_ffn_sb1"])
    sw2, sb2 = f32(inp["L0_ffn_sw2"]), f32(inp["L0_ffn_sb2"])

    def qk_layout(w):
        return np.ascontiguousarray(
            w.reshape(DC, 128, DC, 128).transpose(2, 1, 0, 3).reshape(D, DC * 128))

    wqkv_r = np.ascontiguousarray(np.concatenate(
        [qk_layout(Wqkv[:, 0:D]), qk_layout(Wqkv[:, D:2 * D]), qk_layout(Wqkv[:, 2 * D:3 * D])],
        axis=0))
    wa_r = np.ascontiguousarray(wa.reshape(DC, 128, E).transpose(1, 0, 2).reshape(128, DC * E))
    gq_r = np.ascontiguousarray(gq.reshape(DC, 128).T)
    gk_r = np.ascontiguousarray(gk.reshape(DC, 128).T)

    gmat = np.concatenate([rg, sg], axis=0)
    w1mat = np.concatenate([rw1, sw1], axis=0)
    w2mat = np.concatenate([rw2, sw2], axis=0)
    g_allr = np.ascontiguousarray(
        gmat.reshape(EXP, DC, 128, FC, 128).transpose(0, 3, 2, 1, 4).reshape(EXP * FC * 128, DC * 128))
    w1_allr = np.ascontiguousarray(
        w1mat.reshape(EXP, DC, 128, FC, 128).transpose(0, 3, 2, 1, 4).reshape(EXP * FC * 128, DC * 128))
    w2_allr = np.ascontiguousarray(
        w2mat.reshape(EXP, FC, 128, NDQ, DQW).transpose(0, 3, 2, 1, 4).reshape(EXP * NDQ * 128, FC * DQW))
    gb_allr = np.ascontiguousarray(np.concatenate([rgb, sgb], axis=0))
    b1_allr = np.ascontiguousarray(np.concatenate([rb1, sb1], axis=0))

    shared = dict(wqkv_r=wqkv_r, wout_r=outW, wa_r=wa_r, gq_r=gq_r, gk_r=gk_r,
                  g_all=g_allr, w1_all=w1_allr, w2_all=w2_allr,
                  gb_all=gb_allr, b1_all=b1_allr, rb2_r=np.ascontiguousarray(rb2),
                  sb2_r=np.ascontiguousarray(sb2))
    in_maps = []
    for c in range(NC):
        m = dict(shared)
        m["x_loc"] = np.ascontiguousarray(x[c * T:(c + 1) * T])
        m["meta"] = np.array([[c, 0, 0, 0, 0, 0, 0, 0]], np.int32)
        in_maps.append(m)
    return in_maps


def kernel(**inputs):
    cfg = REAL_CFG
    cf = derive(cfg)
    nc = build_nc(cfg)
    in_maps = prep_inputs(cfg, inputs)
    res = run_bass_kernel_spmd(nc, in_maps, core_ids=list(range(cfg["NC"])))
    out = np.concatenate([res.results[i]["out_loc"] for i in range(cfg["NC"])], axis=0)
    return out.reshape(1, cfg["S"], cfg["D"]).astype(np.float32)


# version 14
# speedup vs baseline: 1.1642x; 1.1642x over previous
import os
import numpy as np
from itertools import zip_longest
from contextlib import ExitStack

import concourse.bass as bass
import concourse.mybir as mybir
from concourse.bass_utils import run_bass_kernel_spmd

F32 = mybir.dt.float32
BF16 = mybir.dt.bfloat16
I32 = mybir.dt.int32
AX = mybir.AxisListType.X
OP = mybir.AluOpType
ACTF = mybir.ActivationFunctionType

NEG_BIG = -1.0e30

REAL_CFG = dict(D=2048, S=8192, F=1024, E=16, SH=2, NC=8)


def derive(cfg):
    c = dict(cfg)
    c["H"] = c["D"] // 128
    assert c["H"] == 2 * c["NC"], "one head-pair per core"
    c["DC"] = c["D"] // 128
    c["FC"] = c["F"] // 128
    c["NT"] = 2 * c["NC"]
    c["TS"] = c["S"] // c["NT"]
    assert c["TS"] % 128 == 0 and c["TS"] <= 512
    c["T"] = 2 * c["TS"]
    c["TCH"] = c["T"] // 128
    c["TBW"] = c["TS"]
    c["QS"] = c["TBW"] // 128
    c["EXP"] = c["E"] + c["SH"]
    c["GE"] = 2
    assert c["EXP"] % c["GE"] == 0
    c["NG"] = c["EXP"] // c["GE"]
    c["DQW"] = min(512, c["D"])
    c["NDQ"] = c["D"] // c["DQW"]
    return c


ENGS = ("tensor", "vector", "scalar", "gpsimd", "sync")
FIFO_ENGS = set()


NCHAN = 8


class Tok:
    __slots__ = ("eng", "idx", "dma", "val", "chan")

    def __init__(self, eng, idx, dma, chan):
        self.eng = eng
        self.idx = idx
        self.dma = dma
        self.chan = chan
        self.val = None


class Rec:
    def __init__(self):
        self.ops = {e: [] for e in ENGS}
        self.used = set()
        self.dma_cnt = {e: 0 for e in ENGS}
        self.last_dma = {}

    def add(self, eng, fn, deps=(), dma=False, tag="", group=None):
        deps = [d for d in deps if d is not None]
        chan = None
        if dma:
            chan = self.dma_cnt[eng] % NCHAN
            self.dma_cnt[eng] += 1
            prev = self.last_dma.get((eng, chan))
            if prev is not None:
                deps = deps + [prev]
            self.last_dma[(eng, chan)] = None
        t = Tok(eng, len(self.ops[eng]), dma, chan)
        if dma:
            self.last_dma[(eng, chan)] = t
        self.ops[eng].append((fn, deps, t, tag, group))
        return t

    def join(self, eng, deps, tag="join"):
        return self.add(eng, lambda e: e.nop(nofuse=True), deps=deps, tag=tag)

    def finalize_and_emit(self, nc, sems):
        for eng in ENGS:
            cc = 0
            cd = {}
            ops = self.ops[eng]
            n = len(ops)
            i = 0
            while i < n:
                fn, deps, t, tag, group = ops[i]
                if t.dma:
                    cd[t.chan] = cd.get(t.chan, 0) + 16
                    t.val = cd[t.chan]
                    i += 1
                    continue
                j = i
                if group is not None:
                    while j + 1 < n and (not ops[j + 1][2].dma) and ops[j + 1][4] == group:
                        j += 1
                cc += 1
                for k in range(i, j + 1):
                    ops[k][2].val = cc
                i = j + 1
        rec = self

        def skey(d):
            return (d.eng, d.chan) if d.dma else (d.eng, None)

        def emit_for(eng_name):
            def closure(eng):
                waited = {}
                chain = 0
                ops = rec.ops[eng_name]
                n = len(ops)
                i = 0
                while i < n:
                    fn, deps, t, tag, group = ops[i]
                    j = i
                    if (not t.dma) and group is not None:
                        while j + 1 < n and (not ops[j + 1][2].dma) and ops[j + 1][4] == group:
                            j += 1
                    for k in range(i, j + 1):
                        fnk, depsk, tk, tagk, _g = ops[k]
                        need = {}
                        for d in depsk:
                            if d.val is None:
                                raise RuntimeError(f"unvalued dep used by {eng_name}:{tagk}")
                            kk = skey(d)
                            if need.get(kk, 0) < d.val:
                                need[kk] = d.val
                        attach = None
                        if not tk.dma:
                            need.pop((eng_name, None), None)
                            cpos = j if eng_name == "tensor" else i
                            if k == cpos and chain > 0:
                                attach = ((eng_name, None), chain)
                        else:
                            ck = (eng_name, tk.chan)
                            if ck in need:
                                attach = (ck, need.pop(ck))
                        stand = []
                        for kk, v in sorted(need.items(), key=lambda kv: str(kv[0])):
                            if waited.get(kk, 0) < v:
                                stand.append((kk, v))
                        if attach is None and stand:
                            attach = stand.pop()
                        for kk, v in stand:
                            eng.wait_ge(sems[kk], v)
                            waited[kk] = v
                        inst = fnk(eng)
                        if inst is None:
                            if attach is not None:
                                kk, v = attach
                                eng.wait_ge(sems[kk], v)
                                waited[kk] = max(waited.get(kk, 0), v)
                            continue
                        if attach is not None:
                            kk, v = attach
                            inst._wait_ge(sems[kk], v)
                            if waited.get(kk, 0) < v:
                                waited[kk] = v
                        if tk.dma:
                            inst.then_inc(sems[(eng_name, tk.chan)], 16)
                        elif k == j:
                            chain += 1
                            inst.then_inc(sems[(eng_name, None)], 1)
                    i = j + 1
            return closure

        with nc.Block() as blk:
            blk.tensor(emit_for("tensor"))
            blk.vector(emit_for("vector"))
            blk.scalar(emit_for("scalar"))
            blk.gpsimd(emit_for("gpsimd"))
            blk.sync(emit_for("sync"))


def build_nc(cfg, debug=False, phases=4):
    cf = derive(cfg)
    D, S, F, E, SH = cf["D"], cf["S"], cf["F"], cf["E"], cf["SH"]
    NC, H, DC, FC = cf["NC"], cf["H"], cf["DC"], cf["FC"]
    TS, T, TCH = cf["TS"], cf["T"], cf["TCH"]
    TBW, QS, EXP, GE, NG = cf["TBW"], cf["QS"], cf["EXP"], cf["GE"], cf["NG"]
    DQW, NDQ = cf["DQW"], cf["NDQ"]
    NTB = 2

    nc = bass.Bass("TRN2", target_bir_lowering=False, debug=debug)

    x_loc = nc.dram_tensor("x_loc", [T, D], F32, kind="ExternalInput")
    meta = nc.dram_tensor("meta", [1, 8], I32, kind="ExternalInput")
    wqkv_r = nc.dram_tensor("wqkv_r", [3 * D, DC * 128], F32, kind="ExternalInput")
    wout_r = nc.dram_tensor("wout_r", [D, D], F32, kind="ExternalInput")
    wa_r = nc.dram_tensor("wa_r", [128, DC * E], F32, kind="ExternalInput")
    gq_r = nc.dram_tensor("gq_r", [128, DC], F32, kind="ExternalInput")
    gk_r = nc.dram_tensor("gk_r", [128, DC], F32, kind="ExternalInput")
    g_all = nc.dram_tensor("g_all", [EXP * FC * 128, DC * 128], F32, kind="ExternalInput")
    w1_all = nc.dram_tensor("w1_all", [EXP * FC * 128, DC * 128], F32, kind="ExternalInput")
    w2_all = nc.dram_tensor("w2_all", [EXP * NDQ * 128, FC * DQW], F32, kind="ExternalInput")
    gb_all = nc.dram_tensor("gb_all", [EXP, F], F32, kind="ExternalInput")
    b1_all = nc.dram_tensor("b1_all", [EXP, F], F32, kind="ExternalInput")
    rb2_r = nc.dram_tensor("rb2_r", [E, D], F32, kind="ExternalInput")
    sb2_r = nc.dram_tensor("sb2_r", [SH, D], F32, kind="ExternalInput")
    out_loc = nc.dram_tensor("out_loc", [T, D], F32, kind="ExternalOutput")

    qkv_loc = nc.dram_tensor("qkv_loc", [3 * D, T], BF16)
    qkv_g = nc.dram_tensor("qkv_g", [NC * 3 * D, T], BF16,
                           addr_space="Shared" if NC > 4 else "Local")
    ctx_loc = nc.dram_tensor("ctx_loc", [256, S], BF16)
    ctx_g = nc.dram_tensor("ctx_g", [NC * 256, S], BF16,
                           addr_space="Shared" if NC > 4 else "Local")

    es = ExitStack()
    A = es.enter_context
    r = Rec()
    sems = {}
    for e in ENGS:
        sems[(e, None)] = A(nc.semaphore(f"sc_{e}"))
        for ch in range(NCHAN):
            sems[(e, ch)] = A(nc.semaphore(f"sd_{e}{ch}"))

    def dap(t, off, dims):
        return bass.AP(t.ap().tensor, off, [[int(a), int(b)] for a, b in dims])

    xn_fm = A(nc.sbuf_tensor("xn_fm", [128, DC * T], BF16))
    comb_b = A(nc.sbuf_tensor("comb_b", [E, T], BF16))
    gq2_sb = A(nc.sbuf_tensor("gq2_sb", [128, DC], F32))
    gk_sb = A(nc.sbuf_tensor("gk_sb", [128, DC], F32))
    gbT = A(nc.sbuf_tensor("gbT", [128, FC * EXP], F32))
    b1T = A(nc.sbuf_tensor("b1T", [128, FC * EXP], F32))
    rb2_sb = A(nc.sbuf_tensor("rb2_sb", [E, D], BF16))
    sb2_sb = A(nc.sbuf_tensor("sb2_sb", [SH, D], BF16))
    id_bf = A(nc.sbuf_tensor("id_bf", [128, 128], BF16))
    id_f32 = A(nc.sbuf_tensor("id_f32", [128, 128], F32))
    ones128f = A(nc.sbuf_tensor("ones128f", [128, 128], F32))
    ones128b = A(nc.sbuf_tensor("ones128b", [128, 128], BF16))
    ones_c = A(nc.sbuf_tensor("ones_c", [128, 1], BF16))
    ones_rb = A(nc.sbuf_tensor("ones_rb", [1, 128], BF16))
    ones_rf = A(nc.sbuf_tensor("ones_rf", [1, 128], F32))
    onehot = A(nc.sbuf_tensor("onehot", [E, E * 128], BF16))
    wa_sb = A(nc.sbuf_tensor("wa_sb", [128, DC * E], F32))
    meta_sb = A(nc.sbuf_tensor("meta_sb", [1, 8], I32))

    mcell = {}

    if debug:
        dbg_xn = nc.dram_tensor("dbg_xn", [128, DC * T], BF16, kind="ExternalOutput")
        dbg_comb = nc.dram_tensor("dbg_comb", [E, T], BF16, kind="ExternalOutput")
        dbg_acc = nc.dram_tensor("dbg_acc", [128, TCH * D], F32, kind="ExternalOutput")

    t_meta = r.add("sync", lambda e: e.dma_start(out=meta_sb[:], in_=meta.ap()), dma=True, tag="meta")

    r.join("sync", [t_meta], tag="metaj")

    def loadv(e):
        c = e.value_load(meta_sb[0:1, 0:1])
        mcell["c"] = c
        mcell["qk"] = e.snap(c * (256 * T))
        mcell["v"] = e.snap(c * (TCH * 128 * 256))
        mcell["ctx"] = e.snap(c * T)
        return e.nop(nofuse=True)
    r.add("sync", loadv, tag="loadv")

    t_gq = r.add("sync", lambda e: e.dma_start(out=gq2_sb[:], in_=gq_r.ap()), dma=True)
    t_gk = r.add("sync", lambda e: e.dma_start(out=gk_sb[:], in_=gk_r.ap()), dma=True)
    t_wa = r.add("sync", lambda e: e.dma_start(out=wa_sb[:], in_=wa_r.ap()), dma=True)
    t_rb2 = r.add("gpsimd", lambda e: e.dma_start(out=rb2_sb[:], in_=rb2_r.ap()), dma=True)
    t_sb2 = r.add("gpsimd", lambda e: e.dma_start(out=sb2_sb[:], in_=sb2_r.ap()), dma=True)

    t_o128f = r.add("gpsimd", lambda e: e.memset(ones128f[:], 1.0))
    t_o128b = r.add("gpsimd", lambda e: e.memset(ones128b[:], 1.0))
    t_onc = r.add("gpsimd", lambda e: e.memset(ones_c[:], 1.0))
    t_onrb = r.add("gpsimd", lambda e: e.memset(ones_rb[:], 1.0))
    t_onrf = r.add("gpsimd", lambda e: e.memset(ones_rf[:], 1.0))
    t_oh0 = r.add("gpsimd", lambda e: e.memset(onehot[:], 1.0))
    t_idb = r.add("gpsimd", lambda e: e.affine_select(
        out=id_bf[:], in_=ones128b[:], compare_op=OP.is_equal, fill=0.0,
        base=0, pattern=[[-1, 128]], channel_multiplier=1), deps=[t_o128b])
    t_idf = r.add("gpsimd", lambda e: e.affine_select(
        out=id_f32[:], in_=ones128f[:], compare_op=OP.is_equal, fill=0.0,
        base=0, pattern=[[-1, 128]], channel_multiplier=1), deps=[t_o128f])
    t_oh = []
    for ee in range(E):
        t_oh.append(r.add("gpsimd", lambda e, ee=ee: e.affine_select(
            out=onehot[:, ee * 128:(ee + 1) * 128], in_=onehot[:, ee * 128:(ee + 1) * 128],
            compare_op=OP.is_equal, fill=0.0,
            base=-ee, pattern=[[0, 128]], channel_multiplier=1), deps=[t_oh0]))
    t_oh_j = r.join("gpsimd", t_oh, tag="ohj")

    t_gq2 = r.add("vector", lambda e: e.tensor_scalar_mul(
        gq2_sb[:], gq2_sb[:], float(1.0 / np.sqrt(128.0))), deps=[t_gq])

    p1 = ExitStack()
    A1 = p1.enter_context
    bias_raw = A1(nc.sbuf_tensor("bias_raw", [EXP, 2 * F], F32))
    x_sb = [A1(nc.sbuf_tensor(f"x_sb{i}", [128, D], F32)) for i in range(2)]
    xnf = A1(nc.sbuf_tensor("xnf", [128, D], F32))
    sqf = A1(nc.sbuf_tensor("sqf", [128, D], F32))
    xnT_f = [A1(nc.sbuf_tensor(f"xnT_f{i}", [128, 128], F32)) for i in range(2)]
    stat_c = A1(nc.sbuf_tensor("stat_c", [128, 8], F32))
    sg_sb = A1(nc.sbuf_tensor("sg_sb", [128, E], F32))
    tp_sc = A1(nc.sbuf_tensor("tp_sc", [128, 2 * E], F32))
    qraw = A1(nc.sbuf_tensor("qraw", [128, DC * T], BF16))
    kraw = A1(nc.sbuf_tensor("kraw", [128, DC * T], BF16))
    sq_bf = A1(nc.sbuf_tensor("sq_bf", [128, T], BF16))
    wqk_sb = [A1(nc.sbuf_tensor(f"wqk_sb{i}", [128, DC * 128], BF16)) for i in range(2)]
    vtmp = [A1(nc.sbuf_tensor(f"vtmp{i}", [128, TBW], BF16)) for i in range(2)]
    vt_st = [A1(nc.sbuf_tensor(f"vt_st{i}", [128, QS * 128], BF16)) for i in range(2)]
    qn_sc = [A1(nc.sbuf_tensor(f"qn_sc{i}", [128, TBW], F32)) for i in range(2)]
    qn_st = [A1(nc.sbuf_tensor(f"qn_st{i}", [128, TBW], BF16)) for i in range(2)]
    rows = A1(nc.sbuf_tensor("rows", [1, 3 * T], F32))

    t_biasg = r.add("sync", lambda e: e.dma_start(out=bias_raw[:, 0:F], in_=gb_all.ap()), dma=True)
    t_biasb = r.add("sync", lambda e: e.dma_start(out=bias_raw[:, F:2 * F], in_=b1_all.ap()), dma=True)

    pA = ExitStack()
    ps_tr = [pA.enter_context(nc.psum_tensor(f"ps_tr{i}", [128, 512], F32)) for i in range(2)]
    ps_aff = [pA.enter_context(nc.psum_tensor(f"ps_aff{i}", [128, 512], F32)) for i in range(2)]

    ev_bias = {}
    for j, (soff, dst) in enumerate([(0, gbT), (F, b1T)]):
        for fc in range(FC):
            i = j * FC + fc
            tt_ = r.add("tensor", lambda e, soff=soff, fc=fc, i=i: e.transpose(
                ps_tr[i % 2][:, 0:EXP],
                bias_raw[0:EXP, soff + fc * 128: soff + (fc + 1) * 128],
                id_f32[0:EXP, 0:EXP]),
                deps=[t_biasg, t_biasb, t_idf, ev_bias.get(i - 2)], tag=f"biasT{i}")
            ev_bias[i] = r.add("vector", lambda e, dst=dst, fc=fc, i=i: e.tensor_copy(
                dst[:, fc * EXP:(fc + 1) * EXP], ps_tr[i % 2][:, 0:EXP]),
                deps=[tt_], tag=f"biasE{i}")
    t_bias_j = r.join("vector", list(ev_bias.values()), tag="biasj")

    if phases < 0.18:
        t_fin = []
        for tch in range(TCH):
            t_fin.append(r.add("sync", lambda e, tch=tch: e.dma_start(
                out=out_loc.ap()[tch * 128:(tch + 1) * 128, :],
                in_=x_loc.ap()[tch * 128:(tch + 1) * 128, :]),
                dma=True, deps=[t_bias_j, t_oh_j, t_gq2, t_rb2, t_sb2, t_idb], tag=f"dummyout{tch}"))
        r.join("sync", t_fin, tag="final")
        r.finalize_and_emit(nc, sems)
        p1.close()
        es.close()
        return nc


    ev_xn = {}
    ev_comb = {}
    ev_combT = {}
    ev_tr_last = {}
    t_sg_tok = {}
    last_xn_apply = {}
    TCHrun = TCH if phases >= 0.22 else 1
    for tch in range(TCHrun):
        t_x = r.add("sync", lambda e, tch=tch: e.dma_start(
            out=x_sb[tch % 2][:], in_=x_loc.ap()[tch * 128:(tch + 1) * 128, :]),
            dma=True, deps=[last_xn_apply.get(tch - 2)], tag=f"x{tch}")
        t_sq = r.add("scalar", lambda e, tch=tch: e.activation(
            sqf[:], x_sb[tch % 2][:], ACTF.Square),
            deps=[t_x, last_xn_apply.get(tch - 1)], tag=f"sq{tch}")
        c1 = r.add("vector", lambda e, tch=tch: e.reduce_sum(
            stat_c[:, 0:1], x_sb[tch % 2][:], axis=AX), deps=[t_x])
        c2 = r.add("vector", lambda e: e.reduce_sum(stat_c[:, 1:2], sqf[:], axis=AX),
                   deps=[t_sq])
        c3 = r.add("vector", lambda e: e.tensor_scalar_mul(stat_c[:, 0:1], stat_c[:, 0:1], 1.0 / D), deps=[c1])
        c4 = r.add("vector", lambda e: e.tensor_scalar_mul(stat_c[:, 1:2], stat_c[:, 1:2], 1.0 / D), deps=[c2])
        c5 = r.add("vector", lambda e: e.tensor_tensor(stat_c[:, 2:3], stat_c[:, 0:1], stat_c[:, 0:1], OP.mult), deps=[c3])
        c6 = r.add("vector", lambda e: e.tensor_tensor(stat_c[:, 3:4], stat_c[:, 1:2], stat_c[:, 2:3], OP.subtract), deps=[c4, c5])
        t_var = r.add("vector", lambda e: e.tensor_scalar_add(stat_c[:, 3:4], stat_c[:, 3:4], 1e-5), deps=[c6])
        t_sqrt = r.add("scalar", lambda e: e.activation(stat_c[:, 4:5], stat_c[:, 3:4], ACTF.Sqrt),
                       deps=[t_var], tag=f"sqrt{tch}")
        t_rec = r.add("vector", lambda e: e.reciprocal(stat_c[:, 5:6], stat_c[:, 4:5]),
                      deps=[t_sqrt])
        last_xn_apply[tch] = r.add("vector", lambda e, tch=tch: e.tensor_scalar(
            xnf[:], x_sb[tch % 2][:], stat_c[:, 0:1], stat_c[:, 5:6],
            OP.subtract, OP.mult),
            deps=[t_rec, c3, ev_tr_last.get(tch - 1)], tag=f"xn{tch}")
        SKIP = os.environ.get("K_SKIP", "")
        if "ln_only" in SKIP:
            ev_combT[tch] = last_xn_apply[tch]
            continue
        ev_evf = {}
        last_router_mm = {}
        for dc in range(DC):
            war = [ev_xn.get((dc - 2, tch)), ev_evf.get(dc - 2)]
            if dc < 2:
                war.append(ev_combT.get(tch - 1))
                if tch == 0:
                    war.append(t_bias_j)
            t_tr = r.add("tensor", lambda e, dc=dc: e.transpose(
                ps_tr[dc % 2][:, 0:128], xnf[:, dc * 128:(dc + 1) * 128], id_f32[:, :]),
                deps=[last_xn_apply[tch], t_idf] + war, tag=f"tr{tch}_{dc}")
            ev_evf[dc] = r.add("vector", lambda e, dc=dc: e.tensor_copy(
                xnT_f[dc % 2][:], ps_tr[dc % 2][:, 0:128]),
                deps=[t_tr, last_router_mm.get(dc - 2)], tag=f"evf{tch}_{dc}")
            ev_xn[(dc, tch)] = r.add("scalar", lambda e, dc=dc, tch=tch: e.activation(
                xn_fm[:, dc * T + tch * 128: dc * T + (tch + 1) * 128],
                ps_tr[dc % 2][:, 0:128], ACTF.Copy),
                deps=[t_tr, ev_evf[dc]], tag=f"xnb{tch}_{dc}")
            if "no_aff" in SKIP:
                last_router_mm[dc] = ev_evf[dc]
            else:
                last_router_mm[dc] = r.add("tensor", lambda e, dc=dc, tch=tch: e.matmul(
                    ps_aff[tch % 2][:, 0:E], xnT_f[dc % 2][:], wa_sb[:, dc * E:(dc + 1) * E],
                    start=(dc == 0), stop=(dc == DC - 1)),
                    deps=[ev_evf[dc], t_wa, t_sg_tok.get(tch - 2)], tag=f"aff{tch}_{dc}")
        ev_tr_last[tch] = last_router_mm[DC - 1]
        if "no_router_tail" in SKIP:
            ev_combT[tch] = r.join("scalar", [ev_xn[(dc, tch)] for dc in range(DC)], tag=f"sk{tch}")
            t_sg_tok[tch] = ev_combT[tch]
            ev_comb[tch] = ev_combT[tch]
            continue
        t_sg = r.add("scalar", lambda e, tch=tch: e.activation(
            sg_sb[:], ps_aff[tch % 2][:, 0:E], ACTF.Sigmoid),
            deps=[last_router_mm[DC - 1], ev_comb.get(tch - 1)], tag=f"sig{tch}")
        t_sg_tok[tch] = t_sg
        u1 = r.add("vector", lambda e: e.reduce_max(stat_c[:, 6:7], sg_sb[:], axis=AX), deps=[t_sg])
        u2 = r.add("vector", lambda e: e.tensor_single_scalar(
            tp_sc[:, 0:E], sg_sb[:], stat_c[:, 6:7], OP.is_ge), deps=[u1])
        u3 = r.add("vector", lambda e: e.scalar_tensor_tensor(
            tp_sc[:, E:2 * E], tp_sc[:, 0:E], NEG_BIG, sg_sb[:], OP.mult, OP.add), deps=[u2])
        u4 = r.add("vector", lambda e: e.reduce_max(stat_c[:, 7:8], tp_sc[:, E:2 * E], axis=AX), deps=[u3])
        u5 = r.add("vector", lambda e: e.tensor_single_scalar(
            tp_sc[:, E:2 * E], tp_sc[:, E:2 * E], stat_c[:, 7:8], OP.is_ge), deps=[u4])
        u6 = r.add("vector", lambda e: e.tensor_tensor(
            tp_sc[:, 0:E], tp_sc[:, 0:E], tp_sc[:, E:2 * E], OP.add), deps=[u5])
        u7 = r.add("vector", lambda e: e.tensor_tensor(
            stat_c[:, 6:7], stat_c[:, 6:7], stat_c[:, 7:8], OP.add), deps=[u4, u6])
        t8 = r.add("vector", lambda e: e.reciprocal(stat_c[:, 7:8], stat_c[:, 6:7]), deps=[u7])
        ev_comb[tch] = r.add("vector", lambda e: e.scalar_tensor_tensor(
            tp_sc[:, E:2 * E], sg_sb[:], stat_c[:, 7:8], tp_sc[:, 0:E],
            OP.mult, OP.mult), deps=[t8, u6], tag=f"comb{tch}")
        if "no_combT" in SKIP:
            ev_combT[tch] = ev_comb[tch]
            continue
        t_ct = r.add("tensor", lambda e, tch=tch: e.transpose(
            ps_tr[tch % 2][0:E, 0:128], tp_sc[:, E:2 * E], id_f32[:, :]),
            deps=[ev_comb[tch],
                  ev_xn.get((DC - 2, tch)), ev_xn.get((DC - 1, tch)),
                  ev_evf.get(DC - 2), ev_evf.get(DC - 1)],
            tag=f"combT{tch}")
        ev_combT[tch] = r.add("scalar", lambda e, tch=tch: e.activation(
            comb_b[:, tch * 128:(tch + 1) * 128], ps_tr[tch % 2][0:E, 0:128], ACTF.Copy),
            deps=[t_ct], tag=f"combE{tch}")

    t_comb_j = r.join("scalar", [ev_combT[t] for t in range(TCHrun)], tag="combj")
    pA.close()

    if phases < 0.3:
        t_fin = []
        for tch in range(TCH):
            t_fin.append(r.add("sync", lambda e, tch=tch: e.dma_start(
                out=out_loc.ap()[tch * 128:(tch + 1) * 128, :],
                in_=x_loc.ap()[tch * 128:(tch + 1) * 128, :]),
                dma=True, deps=[t_comb_j], tag=f"dummyout{tch}"))
        r.join("sync", t_fin, tag="final")
        r.finalize_and_emit(nc, sems)
        p1.close()
        es.close()
        return nc


    pB = ExitStack()
    ps_qkv = [pB.enter_context(nc.psum_tensor(f"ps_qkv{i}", [128, TBW], F32)) for i in range(2)]
    ps_vtr = [pB.enter_context(nc.psum_tensor(f"ps_vtr{i}", [128, 128], BF16)) for i in range(2)]

    t_wqk = {}
    ev_epi = {}
    mm_last = {}
    t_vb = {}
    ev_vtr = {}
    vtr_cnt = 0
    for oc in range(3 * DC):
        t_wqk[oc] = r.add("gpsimd", lambda e, oc=oc: e.dma_start(
            out=wqk_sb[oc % 2][:], in_=wqkv_r.ap()[oc * 128:(oc + 1) * 128, :]),
            dma=True, deps=[mm_last.get((oc - 2, NTB - 1))], tag=f"wqkv{oc}")
        for tb in range(NTB):
            gi = oc * NTB + tb
            pp = gi % 2
            mt = None
            for dc in range(DC):
                deps = [t_wqk[oc], ev_epi.get(gi - 2)]
                if oc == 0:
                    deps += [ev_xn[(dc, tc2)] for tc2 in range(tb * QS, (tb + 1) * QS)]
                if gi < 2 and dc == 0:
                    deps += [t_comb_j, t_sg_tok[TCH - 1], ev_tr_last[TCH - 1],
                             ev_combT[TCH - 1]]
                mt = r.add("tensor", lambda e, oc=oc, tb=tb, dc=dc, pp=pp: e.matmul(
                    ps_qkv[pp][:, :], wqk_sb[oc % 2][:, dc * 128:(dc + 1) * 128],
                    xn_fm[:, dc * T + tb * TBW: dc * T + (tb + 1) * TBW],
                    start=(dc == 0), stop=(dc == DC - 1)),
                    deps=deps, tag=f"qkvmm{oc}_{tb}_{dc}", group=("qkv", oc, tb))
            mm_last[(oc, tb)] = mt
            if oc < 2 * DC:
                dst = qraw if oc < DC else kraw
                c0 = (oc % DC) * T + tb * TBW
                ev_epi[gi] = r.add("scalar", lambda e, dst=dst, c0=c0, pp=pp: e.activation(
                    dst[:, c0:c0 + TBW], ps_qkv[pp][:, :], ACTF.Copy),
                    deps=[mt], tag=f"qke{oc}_{tb}")
            else:
                vc = oc - 2 * DC
                t_ev = r.add("scalar", lambda e, pp=pp, tb=tb: e.activation(
                    vtmp[tb % 2][:, :], ps_qkv[pp][:, :], ACTF.Copy),
                    deps=[mt, t_vb.get((vc - 1, tb))], tag=f"vev{vc}_{tb}")
                ev_epi[gi] = t_ev
                for s in range(QS):
                    vtr_cnt += 1
                    vv = vtr_cnt % 2
                    t_vt = r.add("tensor", lambda e, s=s, tb=tb, vv=vv: e.transpose(
                        ps_vtr[vv][:, :], vtmp[tb % 2][:, s * 128:(s + 1) * 128], id_bf[:, :]),
                        deps=[t_ev, t_idb, ev_vtr.get(vtr_cnt - 2)] +
                             ([t_comb_j, t_sg_tok[TCH - 1]] if vtr_cnt <= 2 else []),
                        tag=f"vtr{vc}_{tb}_{s}")
                    ev_vtr[vtr_cnt] = r.add("vector", lambda e, s=s, tb=tb, vv=vv: e.tensor_copy(
                        vt_st[tb % 2][:, s * 128:(s + 1) * 128], ps_vtr[vv][:, :]),
                        deps=[t_vt, t_vb.get((vc - 1, tb))], tag=f"vte{vc}_{tb}_{s}")
                base = 2 * D * T + (vc // 2) * (TCH * 128 * 256) + (vc % 2) * 128 \
                    + tb * QS * 128 * 256
                t_vb[(vc, tb)] = r.add("sync", lambda e, base=base, tb=tb: e.dma_start(
                    out=dap(qkv_loc, base, [[256, 128], [128 * 256, QS], [1, 128]]),
                    in_=bass.AP(vt_st[tb % 2][:].tensor, vt_st[tb % 2][:].offset,
                                [[QS * 128, 128], [128, QS], [1, 128]])),
                    dma=True, deps=[ev_vtr[vtr_cnt - s2] for s2 in range(QS)], tag=f"vb{vc}_{tb}")

    t_qk_j = r.join("scalar", [ev_epi[oc * NTB + tb] for oc in range(2 * DC) for tb in range(NTB)],
                    tag="qkj")
    t_vtr_j = r.join("vector", [ev_vtr[i + 1] for i in range(vtr_cnt)], tag="vtrj")
    pB.close()

    if phases < 0.6:
        t_fin = []
        for tch in range(TCH):
            t_fin.append(r.add("sync", lambda e, tch=tch: e.dma_start(
                out=out_loc.ap()[tch * 128:(tch + 1) * 128, :],
                in_=x_loc.ap()[tch * 128:(tch + 1) * 128, :]),
                dma=True, deps=[t_qk_j, t_vtr_j] + list(t_vb.values()), tag=f"dummyout{tch}"))
        r.join("sync", t_fin, tag="final")
        r.finalize_and_emit(nc, sems)
        p1.close()
        es.close()
        return nc


    def qk_ln(raw, gamma, row_base, tag, gates, prev_outs=None):
        pC = ExitStack()
        ps_s1 = pC.enter_context(nc.psum_tensor(f"ps_s1{tag}", [1, T], F32))
        ps_s2 = pC.enter_context(nc.psum_tensor(f"ps_s2{tag}", [1, T], F32))
        mm1 = mm2 = None
        prev_mm2 = None
        for qc in range(DC):
            t_sqx = r.add("scalar", lambda e, qc=qc: e.activation(
                sq_bf[:, :], raw[:, qc * T:(qc + 1) * T], ACTF.Square),
                deps=[t_qk_j, prev_mm2], tag=f"lnsq{tag}{qc}")
            for tb in range(NTB):
                mm1 = r.add("tensor", lambda e, qc=qc, tb=tb: e.matmul(
                    ps_s1[0:1, tb * TBW:(tb + 1) * TBW], ones_c[:, 0:1],
                    raw[:, qc * T + tb * TBW: qc * T + (tb + 1) * TBW],
                    start=(qc == 0), stop=(qc == DC - 1)),
                    deps=[t_qk_j, t_onc] + (gates if qc == 0 else []), tag=f"lns1{tag}{qc}_{tb}")
                mm2 = r.add("tensor", lambda e, qc=qc, tb=tb: e.matmul(
                    ps_s2[0:1, tb * TBW:(tb + 1) * TBW], ones_c[:, 0:1],
                    sq_bf[:, tb * TBW:(tb + 1) * TBW],
                    start=(qc == 0), stop=(qc == DC - 1)),
                    deps=[t_sqx, t_onc] + (gates if qc == 0 else []), tag=f"lns2{tag}{qc}_{tb}")
            prev_mm2 = mm2
        t_m = r.add("vector", lambda e: e.tensor_scalar_mul(rows[0:1, 0:T], ps_s1[0:1, :], 1.0 / D),
                    deps=[mm1], tag=f"lnm{tag}")
        t_s = r.add("vector", lambda e: e.tensor_scalar_mul(rows[0:1, 2 * T:3 * T], ps_s2[0:1, :], 1.0 / D),
                    deps=[mm2], tag=f"lns{tag}")
        w1_ = r.add("vector", lambda e: e.tensor_tensor(rows[0:1, T:2 * T], rows[0:1, 0:T], rows[0:1, 0:T], OP.mult), deps=[t_m])
        w2_ = r.add("vector", lambda e: e.tensor_tensor(rows[0:1, 2 * T:3 * T], rows[0:1, 2 * T:3 * T], rows[0:1, T:2 * T], OP.subtract), deps=[t_s, w1_])
        t_v = r.add("vector", lambda e: e.tensor_scalar_add(rows[0:1, 2 * T:3 * T], rows[0:1, 2 * T:3 * T], 1e-6), deps=[w2_])
        t_sq2 = r.add("scalar", lambda e: e.activation(rows[0:1, T:2 * T], rows[0:1, 2 * T:3 * T], ACTF.Sqrt),
                      deps=[t_v], tag=f"lnsqrt{tag}")
        t_r = r.add("vector", lambda e: e.reciprocal(rows[0:1, T:2 * T], rows[0:1, T:2 * T]),
                    deps=[t_sq2])
        pC.close()
        pD = ExitStack()
        ps_bm = pD.enter_context(nc.psum_tensor(f"ps_bm{tag}", [128, TBW], F32))
        ps_br = pD.enter_context(nc.psum_tensor(f"ps_br{tag}", [128, TBW], F32))
        outs = []
        appl = []
        for tb in range(NTB):
            bgate = [t_m, t_s, t_r] + appl[-2:]
            bm = r.add("tensor", lambda e, tb=tb: e.matmul(
                ps_bm[:, :], ones_rf[0:1, :], rows[0:1, tb * TBW:(tb + 1) * TBW],
                start=True, stop=True),
                deps=[t_m, t_onrf] + bgate, tag=f"lnbm{tag}{tb}")
            br = r.add("tensor", lambda e, tb=tb: e.matmul(
                ps_br[:, :], ones_rf[0:1, :], rows[0:1, T + tb * TBW: T + (tb + 1) * TBW],
                start=True, stop=True),
                deps=[t_r, t_onrf] + bgate, tag=f"lnbr{tag}{tb}")
            for qc in range(DC):
                i = tb * DC + qc
                pp = i % 2
                war_st = outs[i - 2] if i >= 2 else (prev_outs[i - 2] if prev_outs else None)
                tA = r.add("vector", lambda e, qc=qc, tb=tb, pp=pp: e.tensor_tensor(
                    qn_sc[pp][:, :], raw[:, qc * T + tb * TBW: qc * T + (tb + 1) * TBW],
                    ps_bm[:, :], OP.subtract),
                    deps=[bm, war_st], tag=f"lnap{tag}{i}")
                tB = r.add("vector", lambda e, qc=qc, pp=pp: e.scalar_tensor_tensor(
                    qn_st[pp][:, :], qn_sc[pp][:, :], gamma[:, qc:qc + 1], ps_br[:, :],
                    OP.mult, OP.mult),
                    deps=[tA, br, t_gq2, t_gk, war_st], tag=f"lnap2{tag}{i}")
                appl.append(tB)
                td = r.add("sync", lambda e, qc=qc, tb=tb, pp=pp, row_base=row_base: e.dma_start(
                    out=qkv_loc.ap()[row_base + qc * 128: row_base + (qc + 1) * 128,
                                     tb * TBW:(tb + 1) * TBW],
                    in_=qn_st[pp][:, :]),
                    dma=True, deps=[tB], tag=f"lnb{tag}{i}")
                outs.append(td)
        pD.close()
        gate_next = r.join("vector", appl[-4:] + [t_m, t_s, t_r], tag=f"lng{tag}")
        return outs, gate_next

    qb_outs, gate_q = qk_ln(qraw, gq2_sb, 0, "q", gates=[t_qk_j, t_vtr_j])
    kb_outs, gate_k = qk_ln(kraw, gk_sb, D, "k", gates=[gate_q], prev_outs=qb_outs)

    if phases < 0.9:
        t_fin = []
        for tch in range(TCH):
            t_fin.append(r.add("sync", lambda e, tch=tch: e.dma_start(
                out=out_loc.ap()[tch * 128:(tch + 1) * 128, :],
                in_=x_loc.ap()[tch * 128:(tch + 1) * 128, :]),
                dma=True, deps=qb_outs + kb_outs + list(t_vb.values()), tag=f"dummyout{tch}"))
        r.join("sync", t_fin, tag="final")
        r.finalize_and_emit(nc, sems)
        p1.close()
        es.close()
        return nc


    all_bounce = qb_outs + kb_outs + list(t_vb.values())
    t_ag1 = r.add("gpsimd", lambda e: e.collective_compute(
        "AllGather", OP.bypass, replica_groups=[list(range(NC))],
        ins=[qkv_loc.ap().opt()], outs=[qkv_g.ap().opt()]),
        deps=all_bounce, tag="ag1")

    if debug:
        r.add("sync", lambda e: e.dma_start(out=dbg_xn.ap(), in_=xn_fm[:, :]),
              dma=True, deps=[t_qk_j], tag="dbgxn")
        r.add("sync", lambda e: e.dma_start(out=dbg_comb.ap(), in_=comb_b[:, :]),
              dma=True, deps=[t_comb_j], tag="dbgcomb")

    p1.close()

    if phases < 2:
        t_fin = []
        for tch in range(TCH):
            t_fin.append(r.add("sync", lambda e, tch=tch: e.dma_start(
                out=out_loc.ap()[tch * 128:(tch + 1) * 128, :],
                in_=x_loc.ap()[tch * 128:(tch + 1) * 128, :]),
                dma=True, deps=[t_ag1], tag=f"dummyout{tch}"))
        r.join("sync", t_fin, tag="final")
        r.finalize_and_emit(nc, sems)
        es.close()
        return nc

    p3 = ExitStack()
    A3 = p3.enter_context
    q2_sb = A3(nc.sbuf_tensor("q2_sb", [128, 2 * S], BF16))
    k2_sb = A3(nc.sbuf_tensor("k2_sb", [128, 2 * S], BF16))
    v2_sb = A3(nc.sbuf_tensor("v2_sb", [128, 2 * S], BF16))
    maskA = A3(nc.sbuf_tensor("maskA", [128, QS * TBW], BF16))
    maskB = A3(nc.sbuf_tensor("maskB", [128, QS * TBW], BF16))
    sA_sc = [A3(nc.sbuf_tensor(f"sA_sc{i}", [128, TBW], F32)) for i in range(2)]
    sB_sc = [A3(nc.sbuf_tensor(f"sB_sc{i}", [128, TBW], F32)) for i in range(3)]
    pt_sb = [A3(nc.sbuf_tensor(f"pt_sb{i}", [128, TBW], BF16)) for i in range(4)]
    nm_sb = A3(nc.sbuf_tensor("nm_sb", [128, 2 * QS + 2], F32))
    nmrow = [A3(nc.sbuf_tensor(f"nmrow{i}", [1, TBW], BF16)) for i in range(2)]
    l_row = [A3(nc.sbuf_tensor(f"l_row{i}", [1, TBW], F32)) for i in range(2)]
    linv_sb = [A3(nc.sbuf_tensor(f"linv_sb{i}", [128, TBW], F32)) for i in range(2)]
    ctx_st = [A3(nc.sbuf_tensor(f"ctx_st{i}", [128, TBW], BF16)) for i in range(2)]

    pP3 = ExitStack()
    ps_sa = [pP3.enter_context(nc.psum_tensor(f"ps_sa{i}", [128, TBW], F32)) for i in range(4)]
    ps_sb_ = [pP3.enter_context(nc.psum_tensor(f"ps_sb{i}", [128, TBW], F32)) for i in range(2)]
    ps_ctx = [pP3.enter_context(nc.psum_tensor("ps_ctx0", [128, TBW], F32))]
    ps_lb = pP3.enter_context(nc.psum_tensor("ps_lb", [128, TBW], F32))
    ps_l = ps_lb

    t_mA0 = r.add("gpsimd", lambda e: e.memset(maskA[:], 0.0), deps=[t_ag1])
    t_mB0 = r.add("gpsimd", lambda e: e.memset(maskB[:], 0.0), deps=[t_ag1])
    t_mA = []
    t_mB = []
    for qs in range(QS):
        t_mA.append(r.add("gpsimd", lambda e, qs=qs: e.affine_select(
            out=maskA[:, qs * TBW:(qs + 1) * TBW], in_=maskA[:, qs * TBW:(qs + 1) * TBW],
            compare_op=OP.is_ge, fill=NEG_BIG,
            base=qs * 128, pattern=[[-1, TBW]], channel_multiplier=1), deps=[t_mA0]))
    for kc in range(QS):
        t_mB.append(r.add("gpsimd", lambda e, kc=kc: e.affine_select(
            out=maskB[:, kc * TBW:(kc + 1) * TBW], in_=maskB[:, kc * TBW:(kc + 1) * TBW],
            compare_op=OP.is_ge, fill=NEG_BIG,
            base=-kc * 128, pattern=[[1, TBW]], channel_multiplier=-1), deps=[t_mB0]))
    t_mA_j = r.join("gpsimd", t_mA, tag="mAj")
    t_mB_j = r.join("gpsimd", t_mB, tag="mBj")

    t_qld_all, t_kld_all = [], []
    for hh in range(2):
        t_qld_all.append(r.add("sync", lambda e, hh=hh: e.dma_start(
            out=bass.AP(q2_sb[:].tensor, q2_sb[:].offset + hh * S,
                        [[2 * S, 128], [T, NC], [1, T]]),
            in_=dap(qkv_g, mcell["qk"] + hh * 128 * T,
                    [[T, 128], [3 * D * T, NC], [1, T]])),
            dma=True, deps=[t_ag1], tag=f"qld{hh}"))
        t_kld_all.append(r.add("sync", lambda e, hh=hh: e.dma_start(
            out=bass.AP(k2_sb[:].tensor, k2_sb[:].offset + hh * S,
                        [[2 * S, 128], [T, NC], [1, T]]),
            in_=dap(qkv_g, mcell["qk"] + (D + hh * 128) * T,
                    [[T, 128], [3 * D * T, NC], [1, T]])),
            dma=True, deps=[t_ag1], tag=f"kld{hh}"))
    t_vld = {}
    for j in range(NC):
        t_vld[j] = r.add("sync", lambda e, j=j: e.dma_start(
            out=v2_sb[:, j * (TCH * 256): (j + 1) * (TCH * 256)],
            in_=dap(qkv_g, mcell["v"] + (j * 3 * D + 2 * D) * T,
                    [[256, 128], [128 * 256, TCH], [1, 256]])),
            dma=True, deps=[t_ag1], tag=f"vld{j}")
    t_qld = {k_: t_qld_all[k_[1]] for k_ in [(j, hh) for j in range(NC) for hh in range(2)]}
    t_kld = {k_: t_kld_all[k_[1]] for k_ in [(j, hh) for j in range(NC) for hh in range(2)]}

    NQT = S // TBW
    seq = [(hh, qt) for hh in range(2) for qt in range(NQT)]
    ev_red = {}
    ev_exp = {}
    ev_ptuse = {}
    ev_norm = {}
    ev_nmT = {}
    ev_nmrow = {}
    ev_lastB = {}
    ev_linv = {}
    t_ctxst = []

    Acons = {}
    t_nmdma = {}
    t_nmj = {}
    arot_box = [0]

    def pass_A(i):
        hh, qt = seq[i]
        scr = 2 * QS
        for kt in range(qt + 1):
            for half in range(2):
                arot = arot_box[0]
                grp = ("pa", i, kt, half)
                cons = []
                war = Acons.get(arot - 2, [])
                for t2 in range(2):
                    qs = 2 * half + t2
                    c0 = 2 * qs
                    bk = (2 * arot + t2) % 4
                    diag = (kt == qt)
                    mm = r.add("tensor", lambda e, hh=hh, qt=qt, qs=qs, kt=kt, bk=bk: e.matmul(
                        ps_sa[bk][:, :],
                        q2_sb[:, hh * S + qt * TBW + qs * 128: hh * S + qt * TBW + (qs + 1) * 128],
                        k2_sb[:, hh * S + kt * TBW: hh * S + (kt + 1) * TBW],
                        start=True, stop=True),
                        deps=[t_qld[((qt * TBW) // T, hh)], t_kld[((kt * TBW) // T, hh)]] + war,
                        tag=f"pA{i}_{qs}_{kt}", group=grp)
                    war = []
                    src_ = ps_sa[bk]
                    dep_m = [mm]
                    if diag:
                        tmk = r.add("vector", lambda e, qs=qs, bk=bk: e.tensor_tensor(
                            sA_sc[qs % 2][:, :], ps_sa[bk][:, :],
                            maskA[:, qs * TBW:(qs + 1) * TBW], OP.add),
                            deps=[mm, t_mA_j], tag=f"pAm{i}_{qs}")
                        src_ = sA_sc[qs % 2]
                        dep_m = [tmk]
                        cons.append(tmk)
                    if kt == 0:
                        ev_red[(i, qs, kt)] = r.add("vector", lambda e, src_=src_, c0=c0: e.reduce_max(
                            nm_sb[:, c0:c0 + 1], src_[:, :], axis=AX, negate=True),
                            deps=dep_m + [t_nmdma.get((i - 1, qs)), t_nmdma.get((i - 2, qs))],
                            tag=f"pAr{i}_{qs}_{kt}")
                    else:
                        t_rm = r.add("vector", lambda e, src_=src_: e.reduce_max(
                            nm_sb[:, scr:scr + 1], src_[:, :], axis=AX, negate=True),
                            deps=dep_m + [ev_red[(i, qs, kt - 1)]], tag=f"pAr{i}_{qs}_{kt}")
                        ev_red[(i, qs, kt)] = r.add("vector", lambda e, kt=kt, c0=c0: e.tensor_tensor(
                            nm_sb[:, c0 + kt % 2:c0 + kt % 2 + 1],
                            nm_sb[:, c0 + (kt - 1) % 2:c0 + (kt - 1) % 2 + 1],
                            nm_sb[:, scr:scr + 1], OP.min),
                            deps=[t_rm, ev_red[(i, qs, kt - 1)]])
                    if not diag:
                        cons.append(ev_red[(i, qs, kt)])
                Acons[arot] = cons
                arot_box[0] = arot + 1
                if kt == qt:
                    for qs in (2 * half, 2 * half + 1):
                        fin = 2 * qs + qt % 2
                        t_nmdma[(i, qs)] = r.add("gpsimd", lambda e, i=i, qs=qs, fin=fin: e.dma_start(
                            out=nmrow[i % 2][0:1, qs * 128:(qs + 1) * 128],
                            in_=nm_sb[:, fin:fin + 1]),
                            dma=True, deps=[ev_red[(i, qs, qt)], ev_lastB.get(i - 2)],
                            tag=f"nmD{i}_{qs}")
                yield
        t_nmj[i] = r.join("gpsimd", [t_nmdma[(i, qs)] for qs in range(QS)], tag=f"nmJ{i}")
        yield

    Bexp = []
    Pgrp = {}
    Qgrp = {}
    Bgrp_base = {}
    sB_last = {}
    CH = 2

    def emit_exp(i, kc):
        hh, qt = seq[i]
        g = Bgrp_base[i] + kc
        bb = g % CH
        pt4 = g % 4
        diag = (kc >= qt * QS)
        if diag:
            kcd = kc - qt * QS
            tmk = r.add("vector", lambda e, kcd=kcd, bb=bb: e.tensor_tensor(
                sB_sc[bb][:, :], ps_sb_[bb][:, :],
                maskB[:, kcd * TBW:(kcd + 1) * TBW], OP.add),
                deps=[Pgrp[g // CH], Qgrp.get(g // CH - 2), t_mB_j,
                      sB_last.get(bb)], tag=f"pBm{i}_{kc}")
            te = r.add("scalar", lambda e, bb=bb, pt4=pt4: e.activation(
                pt_sb[pt4][:, :], sB_sc[bb][:, :], ACTF.Exp),
                deps=[tmk], tag=f"pBe{i}_{kc}")
            sB_last[bb] = te
        else:
            te = r.add("scalar", lambda e, bb=bb, pt4=pt4: e.activation(
                pt_sb[pt4][:, :], ps_sb_[bb][:, :], ACTF.Exp),
                deps=[Pgrp[g // CH], Qgrp.get(g // CH - 2)], tag=f"pBe{i}_{kc}")
        assert len(Bexp) == g
        Bexp.append(te)

    def emit_P(i, c):
        hh, qt = seq[i]
        gc = (Bgrp_base[i] + c * CH) // CH
        grp = ("pbP", i, c)
        war = Bexp[CH * gc - 1] if CH * gc >= 1 else None
        tok = None
        for j in range(CH):
            kc = c * CH + j
            tok = r.add("tensor", lambda e, hh=hh, qt=qt, kc=kc, j=j: e.matmul(
                ps_sb_[j][:, :],
                k2_sb[:, hh * S + kc * 128: hh * S + (kc + 1) * 128],
                q2_sb[:, hh * S + qt * TBW: hh * S + (qt + 1) * TBW],
                start=True, stop=False),
                deps=[war], tag=f"pB{i}_{kc}", group=grp)
            war = None
            tok = r.add("tensor", lambda e, i=i, j=j: e.matmul(
                ps_sb_[j][:, :], ones_rb[0:1, :], nmrow[i % 2][0:1, :],
                start=False, stop=True),
                deps=[t_nmj[i], t_onrb], tag=f"pBr{i}_{kc}", group=grp)
        Pgrp[gc] = tok

    def emit_Q(i, c):
        hh, qt = seq[i]
        nkc = (qt + 1) * QS
        gc = (Bgrp_base[i] + c * CH) // CH
        grp = ("pbQ", i, c)
        war = Bexp[CH * gc + CH - 1]
        tok = None
        for j in range(CH):
            kt_ = c * CH + j
            pt4 = (CH * gc + j) % 4
            tok = r.add("tensor", lambda e, kt_=kt_, pt4=pt4, nkc=nkc: e.matmul(
                ps_l[0:1, :], ones_c[:, 0:1], pt_sb[pt4][:, :],
                start=(kt_ == 0), stop=(kt_ == nkc - 1)),
                deps=[war, t_onc,
                      (ev_linv.get(i - 1) if kt_ == 0 else None)],
                tag=f"pBl{i}_{kt_}", group=grp)
            war = None
            tok = r.add("tensor", lambda e, hh=hh, kt_=kt_, pt4=pt4, nkc=nkc, i=i: e.matmul(
                ps_ctx[0][:, :],
                v2_sb[:, kt_ * 256 + hh * 128: kt_ * 256 + (hh + 1) * 128],
                pt_sb[pt4][:, :],
                start=(kt_ == 0), stop=(kt_ == nkc - 1)),
                deps=[war, t_vld[(kt_ * 128) // T],
                      (ev_norm.get(i - 1) if kt_ == 0 else None)],
                tag=f"pBc{i}_{kt_}", group=grp)
            ev_ptuse[(i, kt_)] = tok
        Qgrp[gc] = tok

    def pass_B(i):
        hh, qt = seq[i]
        nkc = (qt + 1) * QS
        nch = nkc // CH
        Bgrp_base[i] = len(Bexp)
        for c in range(nch):
            if c >= 1:
                for j in range(CH):
                    emit_exp(i, CH * (c - 1) + j)
            emit_P(i, c)
            if c >= 1:
                emit_Q(i, c - 1)
            yield
        for j in range(CH):
            emit_exp(i, CH * (nch - 1) + j)
        yield
        emit_Q(i, nch - 1)
        ev_lastB[i] = ev_ptuse[(i, nkc - 1)]
        t_lev = r.add("vector", lambda e, i=i: e.tensor_copy(l_row[i % 2][0:1, :], ps_l[0:1, 0:TBW]),
                      deps=[ev_ptuse[(i, nkc - 1)]], tag=f"lev{i}")
        t_lrec = r.add("vector", lambda e, i=i: e.reciprocal(l_row[i % 2][0:1, :], l_row[i % 2][0:1, :]),
                       deps=[t_lev])
        yield
        yield
        t_lb = r.add("tensor", lambda e, i=i: e.matmul(
            ps_lb[:, :], ones_rf[0:1, :], l_row[i % 2][0:1, :], start=True, stop=True),
            deps=[t_lrec, t_onrf, ev_linv.get(i - 1)], tag=f"lb{i}")
        ev_linv[i] = r.add("vector", lambda e, i=i: e.tensor_copy(linv_sb[i % 2][:, :], ps_lb[:, :]),
                           deps=[t_lb], tag=f"linv{i}")
        ev_norm[i] = r.add("vector", lambda e, i=i: e.tensor_tensor(
            ctx_st[i % 2][:, :], ps_ctx[0][:, :], linv_sb[i % 2][:, :], OP.mult),
            deps=[ev_linv[i], ev_ptuse[(i, nkc - 1)],
                  (t_ctxst[i - 2] if i >= 2 else None)], tag=f"norm{i}")
        t_ctxst.append(r.add("sync", lambda e, hh=hh, qt=qt, i=i: e.dma_start(
            out=ctx_loc.ap()[hh * 128:(hh + 1) * 128, qt * TBW:(qt + 1) * TBW],
            in_=ctx_st[i % 2][:, :]),
            dma=True, deps=[ev_norm[i]], tag=f"ctxst{i}"))
        yield

    SENT = object()
    for i in range(len(seq)):
        ga = pass_A(i)
        gb = pass_B(i - 1) if i > 0 else None
        a_done = False
        b_done = gb is None
        for _ in range(3):
            if not a_done:
                a_done = next(ga, SENT) is SENT
        while not (a_done and b_done):
            if not b_done:
                b_done = next(gb, SENT) is SENT
            if not a_done:
                a_done = next(ga, SENT) is SENT
    for _ in pass_B(len(seq) - 1):
        pass

    t_ag2 = r.add("gpsimd", lambda e: e.collective_compute(
        "AllGather", OP.bypass, replica_groups=[list(range(NC))],
        ins=[ctx_loc.ap().opt()], outs=[ctx_g.ap().opt()]),
        deps=t_ctxst, tag="ag2")

    pP3.close()
    p3.close()

    if phases < 3:
        t_fin = []
        for tch in range(TCH):
            t_fin.append(r.add("sync", lambda e, tch=tch: e.dma_start(
                out=out_loc.ap()[tch * 128:(tch + 1) * 128, :],
                in_=x_loc.ap()[tch * 128:(tch + 1) * 128, :]),
                dma=True, deps=[t_ag2], tag=f"dummyout{tch}"))
        r.join("sync", t_fin, tag="final")
        r.finalize_and_emit(nc, sems)
        es.close()
        return nc

    p2a = ExitStack()
    acc = p2a.enter_context(nc.sbuf_tensor("acc", [128, TCH * D], F32))
    p2 = ExitStack()
    A2 = p2.enter_context
    y_sb = A2(nc.sbuf_tensor("y_sb", [128, GE * FC * NTB * TBW], BF16))
    wg_sb = [A2(nc.sbuf_tensor(f"wg_sb{i}", [128, DC * 128], BF16)) for i in range(2)]
    w1s_sb = [A2(nc.sbuf_tensor(f"w1s_sb{i}", [128, DC * 128], BF16)) for i in range(2)]
    w2b_sb = [A2(nc.sbuf_tensor(f"w2b_sb{i}", [128, GE * FC * DQW], BF16)) for i in range(2)]
    gel_sb = [A2(nc.sbuf_tensor(f"gel_sb{i}", [128, TBW], F32)) for i in range(2)]
    cb_sb = [A2(nc.sbuf_tensor(f"cb_sb{i}", [128, TBW], F32)) for i in range(2)]
    yt_sb = [A2(nc.sbuf_tensor(f"yt_sb{i}", [128, TBW], F32)) for i in range(2)]

    pP2 = ExitStack()
    ps_a = [pP2.enter_context(nc.psum_tensor(f"ps_a{i}", [128, TBW], F32)) for i in range(2)]
    ps_b = [pP2.enter_context(nc.psum_tensor(f"ps_b{i}", [128, TBW], F32)) for i in range(2)]
    ps_cb = [pP2.enter_context(nc.psum_tensor(f"ps_cb{i}", [128, TBW], F32)) for i in range(2)]
    ps_o = [pP2.enter_context(nc.psum_tensor(f"ps_o{i}", [128, DQW], F32)) for i in range(2)]

    gate_p2 = r.join("vector", [ev_norm[len(seq) - 1], ev_lastB[len(seq) - 1],
                                t_ctxst[-1], t_ctxst[-2]], tag="p3done")

    ev_gel = {}
    ev_yt = {}
    ev_y = {}
    ev_yfc = {}
    t_cb = {}
    ev_spill = {}
    t_spill_j = {}
    ev_bmmw = {}
    cnt_a = 0
    sb_cnt = 0
    for g in range(NG):
        exps = [g * GE + i for i in range(GE)]
        for esl, e_ in enumerate(exps):
            routed = e_ < E
            for fc in range(FC):
                wi = (g * GE + esl) * FC + fc
                t_wg = r.add("gpsimd", lambda e, e_=e_, fc=fc, wi=wi: e.dma_start(
                    out=wg_sb[wi % 2][:],
                    in_=g_all.ap()[(e_ * FC + fc) * 128:(e_ * FC + fc + 1) * 128, :]),
                    dma=True, deps=[gate_p2, ev_yfc.get((wi - 2, NTB - 1))], tag=f"wg{wi}")
                t_w1s = r.add("gpsimd", lambda e, e_=e_, fc=fc, wi=wi: e.dma_start(
                    out=w1s_sb[wi % 2][:],
                    in_=w1_all.ap()[(e_ * FC + fc) * 128:(e_ * FC + fc + 1) * 128, :]),
                    dma=True, deps=[gate_p2, ev_yfc.get((wi - 2, NTB - 1))], tag=f"w1{wi}")
                for tb in range(NTB):
                    if routed and fc == 0:
                        t_cbm = r.add("tensor", lambda e, e_=e_, tb=tb: e.matmul(
                            ps_cb[tb][:, :], onehot[:, e_ * 128:(e_ + 1) * 128],
                            comb_b[:, tb * TBW:(tb + 1) * TBW], start=True, stop=True),
                            deps=[t_comb_j, t_oh_j,
                                  t_cb.get((e_ - 1, tb))],
                            tag=f"cb{e_}_{tb}")
                        t_cb[(e_, tb)] = r.add("vector", lambda e, tb=tb: e.tensor_copy(
                            cb_sb[tb][:, :], ps_cb[tb][:, :]),
                            deps=[t_cbm, ev_yt.get((e_ - 1, FC - 1, tb))],
                            tag=f"cbe{e_}_{tb}")
                    cnt_a += 1
                    ppa = cnt_a % 2
                    mt = None
                    for dc in range(DC):
                        mt = r.add("tensor", lambda e, wi=wi, dc=dc, tb=tb, ppa=ppa: e.matmul(
                            ps_a[ppa][:, :], wg_sb[wi % 2][:, dc * 128:(dc + 1) * 128],
                            xn_fm[:, dc * T + tb * TBW: dc * T + (tb + 1) * TBW],
                            start=(dc == 0), stop=(dc == DC - 1)),
                            deps=[t_wg, ev_gel.get(cnt_a - 2)], tag=f"ffA{wi}_{tb}_{dc}",
                            group=("ffa", wi, tb))
                    mtb = None
                    for dc in range(DC):
                        mtb = r.add("tensor", lambda e, wi=wi, dc=dc, tb=tb, ppa=ppa: e.matmul(
                            ps_b[ppa][:, :], w1s_sb[wi % 2][:, dc * 128:(dc + 1) * 128],
                            xn_fm[:, dc * T + tb * TBW: dc * T + (tb + 1) * TBW],
                            start=(dc == 0), stop=(dc == DC - 1)),
                            deps=[t_w1s, ev_yt.get(("b", cnt_a - 2))], tag=f"ffB{wi}_{tb}_{dc}",
                            group=("ffb", wi, tb))
                    bidx = fc * EXP + e_
                    ev_gel[cnt_a] = r.add("scalar", lambda e, bidx=bidx, ppa=ppa: e.activation(
                        gel_sb[ppa][:, :], ps_a[ppa][:, :], ACTF.Gelu,
                        bias=gbT[:, bidx:bidx + 1]),
                        deps=[mt, t_bias_j, ev_y.get(cnt_a - 2)], tag=f"gel{wi}_{tb}")
                    slot = ((esl * FC + fc) * NTB + tb) * TBW
                    warY = ev_bmmw.get(("g", g - 1))
                    if routed:
                        tyt = r.add("vector", lambda e, bidx=bidx, ppa=ppa, tb=tb: e.scalar_tensor_tensor(
                            yt_sb[ppa][:, :], ps_b[ppa][:, :], b1T[:, bidx:bidx + 1],
                            cb_sb[tb][:, :], OP.add, OP.mult),
                            deps=[mtb, t_cb[(e_, tb)], t_bias_j, ev_y.get(cnt_a - 2)],
                            tag=f"yt{wi}_{tb}")
                        ev_yt[("b", cnt_a)] = tyt
                        ev_yt[(e_, fc, tb)] = tyt
                        ev_y[cnt_a] = r.add("vector", lambda e, slot=slot, ppa=ppa: e.tensor_tensor(
                            y_sb[:, slot:slot + TBW], yt_sb[ppa][:, :], gel_sb[ppa][:, :], OP.mult),
                            deps=[tyt, ev_gel[cnt_a], warY], tag=f"y{wi}_{tb}")
                    else:
                        tyt = r.add("vector", lambda e, bidx=bidx, ppa=ppa, slot=slot: e.scalar_tensor_tensor(
                            y_sb[:, slot:slot + TBW], ps_b[ppa][:, :], b1T[:, bidx:bidx + 1],
                            gel_sb[ppa][:, :], OP.add, OP.mult),
                            deps=[mtb, ev_gel[cnt_a], t_bias_j, warY], tag=f"ysh{wi}_{tb}")
                        ev_yt[("b", cnt_a)] = tyt
                        ev_yt[(e_, fc, tb)] = tyt
                        ev_y[cnt_a] = tyt
                    ev_yfc[(wi, tb)] = ev_y[cnt_a]
        y_done = {tb: [ev_yfc[((g * GE + esl) * FC + fc, tb)]
                       for esl in range(GE) for fc in range(FC)] for tb in range(NTB)}
        bmms_all = []
        for dq in range(NDQ):
            w = g * NDQ + dq
            dmas = []
            for esl, e_ in enumerate(exps):
                dmas.append(r.add("gpsimd", lambda e, e_=e_, dq=dq, w=w, esl=esl: e.dma_start(
                    out=w2b_sb[w % 2][:, esl * FC * DQW:(esl + 1) * FC * DQW],
                    in_=w2_all.ap()[(e_ * NDQ + dq) * 128:(e_ * NDQ + dq + 1) * 128, :]),
                    dma=True, deps=[gate_p2, ev_bmmw.get(w - 2)], tag=f"w2{w}_{esl}"))
            t_w2b = r.join("gpsimd", dmas, tag=f"w2j{w}")
            bmms = []
            for tb in range(NTB):
                for tc in range(QS):
                    oi = (tb * QS + tc) * NDQ + dq
                    sb_cnt += 1
                    po = sb_cnt % 2
                    warO = ev_spill.get(sb_cnt - 2)
                    if warO is None and g > 0:
                        warO = t_spill_j[g - 1]
                    mt = None
                    nmm = GE * FC
                    im = 0
                    for esl in range(GE):
                        for fc in range(FC):
                            slot = ((esl * FC + fc) * NTB + tb) * TBW + tc * 128
                            mt = r.add("tensor", lambda e, slot=slot, esl=esl, fc=fc, po=po, im=im, nmm=nmm, w=w: e.matmul(
                                ps_o[po][:, :], y_sb[:, slot:slot + 128],
                                w2b_sb[w % 2][:, (esl * FC + fc) * DQW:(esl * FC + fc + 1) * DQW],
                                start=(im == 0), stop=(im == nmm - 1)),
                                deps=[t_w2b] + ([warO] + y_done[tb] if im == 0 else []),
                                tag=f"ffO{g}_{oi}_{im}", group=("ffo", g, sb_cnt))
                            im += 1
                    bmms.append(mt)
                    bmms_all.append(mt)
                    aslice = (tb * QS + tc) * D + dq * DQW
                    if g == 0:
                        ev_spill[sb_cnt] = r.add("vector", lambda e, aslice=aslice, po=po: e.tensor_copy(
                            acc[:, aslice:aslice + DQW], ps_o[po][:, :]),
                            deps=[mt], tag=f"sp{g}_{oi}")
                    else:
                        ev_spill[sb_cnt] = r.add("vector", lambda e, aslice=aslice, po=po: e.tensor_tensor(
                            acc[:, aslice:aslice + DQW], ps_o[po][:, :],
                            acc[:, aslice:aslice + DQW], OP.add),
                            deps=[mt, ev_spill.get(sb_cnt - TCH * NDQ)], tag=f"sp{g}_{oi}")
            ev_bmmw[w] = r.join("tensor", bmms, tag=f"bmmw{w}")
        ev_bmmw[("g", g)] = ev_bmmw[g * NDQ + NDQ - 1]
        t_spill_j[g] = r.join("vector", [ev_spill[sb_cnt - i] for i in range(TCH * NDQ)],
                              tag=f"spj{g}")

    t_ffn_j = t_spill_j[NG - 1]
    if debug:
        t_dbg_acc = r.add("sync", lambda e: e.dma_start(out=dbg_acc.ap(), in_=acc[:, :]),
                          dma=True, deps=[t_ffn_j], tag="dbgacc")
        t_ffn_j = r.join("vector", [t_ffn_j, t_dbg_acc], tag="ffnjd")
    pP2.close()
    p2.close()

    if phases < 4:
        t_fin = []
        for tch in range(TCH):
            t_fin.append(r.add("sync", lambda e, tch=tch: e.dma_start(
                out=out_loc.ap()[tch * 128:(tch + 1) * 128, :],
                in_=x_loc.ap()[tch * 128:(tch + 1) * 128, :]),
                dma=True, deps=[t_ffn_j], tag=f"dummyout{tch}"))
        r.join("sync", t_fin, tag="final")
        p2a.close()
        r.finalize_and_emit(nc, sems)
        es.close()
        return nc

    p4 = ExitStack()
    A4 = p4.enter_context
    CG = 4 if DC % 4 == 0 else 2
    ctx_gr = [A4(nc.sbuf_tensor(f"ctx_gr{i}", [128, CG * T], BF16)) for i in range(2)]
    wo_sb = [A4(nc.sbuf_tensor(f"wo_sb{i}", [128, CG * D], BF16)) for i in range(2)]
    x2_sb = [A4(nc.sbuf_tensor(f"x2_sb{i}", [128, D], F32)) for i in range(2)]
    xnT_sb = [A4(nc.sbuf_tensor(f"xnT_sb{i}", [128, D], BF16)) for i in range(2)]
    stg_sb = [A4(nc.sbuf_tensor(f"stg_sb{i}", [128, D], F32)) for i in range(2)]

    pP4 = ExitStack()
    ps_p = [pP4.enter_context(nc.psum_tensor(f"ps_p{i}", [128, DQW], F32)) for i in range(2)]
    ps_tx = [pP4.enter_context(nc.psum_tensor(f"ps_tx{i}", [128, 128], BF16)) for i in range(2)]

    gate_p4 = t_ffn_j

    ev_psp = {}
    pj_cnt = 0
    last_psp = {}
    for cg in range(DC // CG):
        ccs = [cg * CG + i for i in range(CG)]
        lds = [r.add("sync", lambda e, cg=cg: e.dma_start(
            out=bass.AP(ctx_gr[cg % 2][:].tensor, ctx_gr[cg % 2][:].offset,
                        [[CG * T, 128], [T, CG], [1, T]]),
            in_=dap(ctx_g, mcell["ctx"] + cg * CG * 128 * S,
                    [[S, 128], [128 * S, CG], [1, T]])),
            dma=True, deps=[t_ag2, gate_p4, last_psp.get(cg - 2)],
            tag=f"ctxg{cg}")]
        for cc in ccs:
            lds.append(r.add("gpsimd", lambda e, cc=cc, cg=cg: e.dma_start(
                out=wo_sb[cg % 2][:, (cc % CG) * D:(cc % CG + 1) * D],
                in_=wout_r.ap()[cc * 128:(cc + 1) * 128, :]),
                dma=True, deps=[gate_p4, last_psp.get(cg - 2)],
                tag=f"wog{cc}"))
        t_ldj = r.join("sync", lds, tag=f"ldj{cg}")
        for tch in range(TCH):
            for dq in range(NDQ):
                pj_cnt += 1
                pi = pj_cnt % 2
                mt = None
                for i, cc in enumerate(ccs):
                    mt = r.add("tensor", lambda e, cc=cc, i=i, tch=tch, dq=dq, pi=pi, cg=cg: e.matmul(
                        ps_p[pi][:, :],
                        ctx_gr[cg % 2][:, (cc % CG) * T + tch * 128:(cc % CG) * T + (tch + 1) * 128],
                        wo_sb[cg % 2][:, (cc % CG) * D + dq * DQW:(cc % CG) * D + (dq + 1) * DQW],
                        start=(i == 0), stop=(i == CG - 1)),
                        deps=[t_ldj] + ([ev_psp.get(pj_cnt - 2)] if i == 0 else []),
                        tag=f"prj{cg}_{tch}_{dq}_{i}", group=("prj", pj_cnt))
                aslice = tch * D + dq * DQW
                ev_psp[pj_cnt] = r.add("vector", lambda e, aslice=aslice, pi=pi: e.tensor_tensor(
                    acc[:, aslice:aslice + DQW], ps_p[pi][:, :],
                    acc[:, aslice:aslice + DQW], OP.add),
                    deps=[mt, t_ffn_j, ev_psp.get(pj_cnt - TCH * NDQ)], tag=f"prjsp{cg}_{tch}_{dq}")
        last_psp[cg] = r.join("vector", [ev_psp[pj_cnt - i] for i in range(TCH * NDQ)],
                              tag=f"pspj{cg}")

    t_proj_j = last_psp[DC // CG - 1]
    t_out = []
    ev_fin = {}
    ev_xnT = {}
    fin_cnt = 0
    tx_cnt = 0
    for tch in range(TCH):
        t_x2 = r.add("sync", lambda e, tch=tch: e.dma_start(
            out=x2_sb[tch % 2][:], in_=x_loc.ap()[tch * 128:(tch + 1) * 128, :]),
            dma=True, deps=[t_ag2, (t_out[tch - 2] if tch >= 2 else None)], tag=f"x2{tch}")
        xts = []
        for dc in range(DC):
            tx_cnt += 1
            pv = tx_cnt % 2
            t_tx = r.add("tensor", lambda e, dc=dc, tch=tch, pv=pv: e.transpose(
                ps_tx[pv][:, :], xn_fm[:, dc * T + tch * 128: dc * T + (tch + 1) * 128],
                id_bf[:, :]),
                deps=[t_idb, ev_xnT.get(tx_cnt - 2),
                      (t_out[tch - 2] if (tch >= 2 and dc < 2) else None),
                      (t_proj_j if (tch == 0 and dc < 2) else None)],
                tag=f"xtx{tch}_{dc}")
            ev_xnT[tx_cnt] = r.add("vector", lambda e, dc=dc, tch=tch, pv=pv: e.tensor_copy(
                xnT_sb[tch % 2][:, dc * 128:(dc + 1) * 128], ps_tx[pv][:, :]),
                deps=[t_tx, (t_out[tch - 2] if tch >= 2 else None)], tag=f"xte{tch}_{dc}")
            xts.append(ev_xnT[tx_cnt])
        for dq in range(NDQ):
            fin_cnt += 1
            pi = fin_cnt % 2
            m1 = r.add("tensor", lambda e, tch=tch, dq=dq, pi=pi: e.matmul(
                ps_p[pi][:, :], comb_b[:, tch * 128:(tch + 1) * 128],
                rb2_sb[:, dq * DQW:(dq + 1) * DQW], start=True, stop=False),
                deps=[t_comb_j, t_rb2, t_proj_j, ev_fin.get(fin_cnt - 2)],
                tag=f"fb2{tch}_{dq}", group=("fin", fin_cnt))
            m2 = r.add("tensor", lambda e, dq=dq, pi=pi: e.matmul(
                ps_p[pi][:, :], ones128b[0:SH, 0:128], sb2_sb[:, dq * DQW:(dq + 1) * DQW],
                start=False, stop=True),
                deps=[t_sb2, t_o128b], tag=f"fsb2{tch}_{dq}", group=("fin", fin_cnt))
            aslice = tch * D + dq * DQW
            f1 = r.add("vector", lambda e, aslice=aslice, pi=pi, tch=tch, dq=dq: e.tensor_tensor(
                stg_sb[tch % 2][:, dq * DQW:(dq + 1) * DQW], ps_p[pi][:, :],
                acc[:, aslice:aslice + DQW], OP.add),
                deps=[m2, t_proj_j,
                      (t_out[tch - 2] if tch >= 2 else None)], tag=f"f1_{tch}_{dq}")
            f2 = r.add("vector", lambda e, tch=tch, dq=dq: e.tensor_tensor(
                stg_sb[tch % 2][:, dq * DQW:(dq + 1) * DQW],
                stg_sb[tch % 2][:, dq * DQW:(dq + 1) * DQW],
                x2_sb[tch % 2][:, dq * DQW:(dq + 1) * DQW], OP.add),
                deps=[t_x2, f1])
            ev_fin[fin_cnt] = r.add("vector", lambda e, tch=tch, dq=dq: e.tensor_tensor(
                stg_sb[tch % 2][:, dq * DQW:(dq + 1) * DQW],
                stg_sb[tch % 2][:, dq * DQW:(dq + 1) * DQW],
                xnT_sb[tch % 2][:, dq * DQW:(dq + 1) * DQW], OP.add),
                deps=[f2] + xts, tag=f"fin{tch}_{dq}")
        t_out.append(r.add("sync", lambda e, tch=tch: e.dma_start(
            out=out_loc.ap()[tch * 128:(tch + 1) * 128, :], in_=stg_sb[tch % 2][:, :]),
            dma=True, deps=[ev_fin[fin_cnt - i] for i in range(NDQ)], tag=f"out{tch}"))

    r.join("sync", t_out, tag="final")

    pP4.close()
    p4.close()
    p2a.close()

    r.finalize_and_emit(nc, sems)
    es.close()
    return nc


def prep_inputs(cfg, inp):
    cf = derive(cfg)
    D, S, F, E, SH = cf["D"], cf["S"], cf["F"], cf["E"], cf["SH"]
    NC, DC, FC = cf["NC"], cf["DC"], cf["FC"]
    T, DQW, NDQ, EXP = cf["T"], cf["DQW"], cf["NDQ"], cf["EXP"]

    f32 = lambda a: np.ascontiguousarray(np.asarray(a, dtype=np.float32))
    x = f32(inp["L0_attn_x"]).reshape(S, D)
    Wqkv = f32(inp["L0_attn_Wqkv"])
    outW = f32(inp["L0_attn_out_W"])
    wa = f32(inp["L0_ffn_wa"])
    gq = f32(inp["L0_attn_q_ln_scale"])
    gk = f32(inp["L0_attn_k_ln_scale"])
    rg, rgb = f32(inp["L0_ffn_rg"]), f32(inp["L0_ffn_rgb"])
    rw1, rb1 = f32(inp["L0_ffn_rw1"]), f32(inp["L0_ffn_rb1"])
    rw2, rb2 = f32(inp["L0_ffn_rw2"]), f32(inp["L0_ffn_rb2"])
    sg, sgb = f32(inp["L0_ffn_sg"]), f32(inp["L0_ffn_sgb"])
    sw1, sb1 = f32(inp["L0_ffn_sw1"]), f32(inp["L0_ffn_sb1"])
    sw2, sb2 = f32(inp["L0_ffn_sw2"]), f32(inp["L0_ffn_sb2"])

    def qk_layout(w):
        return np.ascontiguousarray(
            w.reshape(DC, 128, DC, 128).transpose(2, 1, 0, 3).reshape(D, DC * 128))

    wqkv_r = np.ascontiguousarray(np.concatenate(
        [qk_layout(Wqkv[:, 0:D]), qk_layout(Wqkv[:, D:2 * D]), qk_layout(Wqkv[:, 2 * D:3 * D])],
        axis=0))
    wa_r = np.ascontiguousarray(wa.reshape(DC, 128, E).transpose(1, 0, 2).reshape(128, DC * E))
    gq_r = np.ascontiguousarray(gq.reshape(DC, 128).T)
    gk_r = np.ascontiguousarray(gk.reshape(DC, 128).T)

    gmat = np.concatenate([rg, sg], axis=0)
    w1mat = np.concatenate([rw1, sw1], axis=0)
    w2mat = np.concatenate([rw2, sw2], axis=0)
    g_allr = np.ascontiguousarray(
        gmat.reshape(EXP, DC, 128, FC, 128).transpose(0, 3, 2, 1, 4).reshape(EXP * FC * 128, DC * 128))
    w1_allr = np.ascontiguousarray(
        w1mat.reshape(EXP, DC, 128, FC, 128).transpose(0, 3, 2, 1, 4).reshape(EXP * FC * 128, DC * 128))
    w2_allr = np.ascontiguousarray(
        w2mat.reshape(EXP, FC, 128, NDQ, DQW).transpose(0, 3, 2, 1, 4).reshape(EXP * NDQ * 128, FC * DQW))
    gb_allr = np.ascontiguousarray(np.concatenate([rgb, sgb], axis=0))
    b1_allr = np.ascontiguousarray(np.concatenate([rb1, sb1], axis=0))

    shared = dict(wqkv_r=wqkv_r, wout_r=outW, wa_r=wa_r, gq_r=gq_r, gk_r=gk_r,
                  g_all=g_allr, w1_all=w1_allr, w2_all=w2_allr,
                  gb_all=gb_allr, b1_all=b1_allr, rb2_r=np.ascontiguousarray(rb2),
                  sb2_r=np.ascontiguousarray(sb2))
    in_maps = []
    for c in range(NC):
        m = dict(shared)
        m["x_loc"] = np.ascontiguousarray(x[c * T:(c + 1) * T])
        m["meta"] = np.array([[c, 0, 0, 0, 0, 0, 0, 0]], np.int32)
        in_maps.append(m)
    return in_maps


def kernel(**inputs):
    cfg = REAL_CFG
    cf = derive(cfg)
    nc = build_nc(cfg)
    in_maps = prep_inputs(cfg, inputs)
    res = run_bass_kernel_spmd(nc, in_maps, core_ids=list(range(cfg["NC"])))
    out = np.concatenate([res.results[i]["out_loc"] for i in range(cfg["NC"])], axis=0)
    return out.reshape(1, cfg["S"], cfg["D"]).astype(np.float32)


# version 16
# speedup vs baseline: 1.1967x; 1.0279x over previous
import os
import numpy as np
from itertools import zip_longest
from contextlib import ExitStack

import concourse.bass as bass
import concourse.mybir as mybir
from concourse.bass_utils import run_bass_kernel_spmd

F32 = mybir.dt.float32
BF16 = mybir.dt.bfloat16
I32 = mybir.dt.int32
AX = mybir.AxisListType.X
OP = mybir.AluOpType
ACTF = mybir.ActivationFunctionType

NEG_BIG = -1.0e30

REAL_CFG = dict(D=2048, S=8192, F=1024, E=16, SH=2, NC=8)


def derive(cfg):
    c = dict(cfg)
    c["H"] = c["D"] // 128
    assert c["H"] == 2 * c["NC"], "one head-pair per core"
    c["DC"] = c["D"] // 128
    c["FC"] = c["F"] // 128
    c["NT"] = 2 * c["NC"]
    c["TS"] = c["S"] // c["NT"]
    assert c["TS"] % 128 == 0 and c["TS"] <= 512
    c["T"] = 2 * c["TS"]
    c["TCH"] = c["T"] // 128
    c["TBW"] = c["TS"]
    c["QS"] = c["TBW"] // 128
    c["EXP"] = c["E"] + c["SH"]
    c["GE"] = 2
    assert c["EXP"] % c["GE"] == 0
    c["NG"] = c["EXP"] // c["GE"]
    c["DQW"] = min(512, c["D"])
    c["NDQ"] = c["D"] // c["DQW"]
    return c


ENGS = ("tensor", "vector", "scalar", "gpsimd", "sync")
FIFO_ENGS = set()


NCHAN = 8


class Tok:
    __slots__ = ("eng", "idx", "dma", "val", "chan")

    def __init__(self, eng, idx, dma, chan):
        self.eng = eng
        self.idx = idx
        self.dma = dma
        self.chan = chan
        self.val = None


class Rec:
    def __init__(self):
        self.ops = {e: [] for e in ENGS}
        self.used = set()
        self.dma_cnt = {e: 0 for e in ENGS}
        self.last_dma = {}

    def add(self, eng, fn, deps=(), dma=False, tag="", group=None):
        deps = [d for d in deps if d is not None]
        chan = None
        if dma:
            chan = self.dma_cnt[eng] % NCHAN
            self.dma_cnt[eng] += 1
            prev = self.last_dma.get((eng, chan))
            if prev is not None:
                deps = deps + [prev]
            self.last_dma[(eng, chan)] = None
        t = Tok(eng, len(self.ops[eng]), dma, chan)
        if dma:
            self.last_dma[(eng, chan)] = t
        self.ops[eng].append((fn, deps, t, tag, group))
        return t

    def join(self, eng, deps, tag="join"):
        return self.add(eng, lambda e: e.nop(nofuse=True), deps=deps, tag=tag)

    def finalize_and_emit(self, nc, sems):
        for eng in ENGS:
            cc = 0
            cd = {}
            ops = self.ops[eng]
            n = len(ops)
            i = 0
            while i < n:
                fn, deps, t, tag, group = ops[i]
                if t.dma:
                    cd[t.chan] = cd.get(t.chan, 0) + 16
                    t.val = cd[t.chan]
                    i += 1
                    continue
                j = i
                if group is not None:
                    while j + 1 < n and (not ops[j + 1][2].dma) and ops[j + 1][4] == group:
                        j += 1
                cc += 1
                for k in range(i, j + 1):
                    ops[k][2].val = cc
                i = j + 1
        rec = self

        def skey(d):
            return (d.eng, d.chan) if d.dma else (d.eng, None)

        def emit_for(eng_name):
            def closure(eng):
                waited = {}
                chain = 0
                ops = rec.ops[eng_name]
                n = len(ops)
                i = 0
                while i < n:
                    fn, deps, t, tag, group = ops[i]
                    j = i
                    if (not t.dma) and group is not None:
                        while j + 1 < n and (not ops[j + 1][2].dma) and ops[j + 1][4] == group:
                            j += 1
                    for k in range(i, j + 1):
                        fnk, depsk, tk, tagk, _g = ops[k]
                        need = {}
                        for d in depsk:
                            if d.val is None:
                                raise RuntimeError(f"unvalued dep used by {eng_name}:{tagk}")
                            kk = skey(d)
                            if need.get(kk, 0) < d.val:
                                need[kk] = d.val
                        attach = None
                        if not tk.dma:
                            need.pop((eng_name, None), None)
                            cpos = j if eng_name == "tensor" else i
                            if k == cpos and chain > 0:
                                attach = ((eng_name, None), chain)
                        else:
                            ck = (eng_name, tk.chan)
                            if ck in need:
                                attach = (ck, need.pop(ck))
                        stand = []
                        for kk, v in sorted(need.items(), key=lambda kv: str(kv[0])):
                            if waited.get(kk, 0) < v:
                                stand.append((kk, v))
                        if attach is None and stand:
                            attach = stand.pop()
                        for kk, v in stand:
                            eng.wait_ge(sems[kk], v)
                            waited[kk] = v
                        inst = fnk(eng)
                        if inst is None:
                            if attach is not None:
                                kk, v = attach
                                eng.wait_ge(sems[kk], v)
                                waited[kk] = max(waited.get(kk, 0), v)
                            continue
                        if attach is not None:
                            kk, v = attach
                            inst._wait_ge(sems[kk], v)
                            if waited.get(kk, 0) < v:
                                waited[kk] = v
                        if tk.dma:
                            inst.then_inc(sems[(eng_name, tk.chan)], 16)
                        elif k == j:
                            chain += 1
                            inst.then_inc(sems[(eng_name, None)], 1)
                    i = j + 1
            return closure

        with nc.Block() as blk:
            blk.tensor(emit_for("tensor"))
            blk.vector(emit_for("vector"))
            blk.scalar(emit_for("scalar"))
            blk.gpsimd(emit_for("gpsimd"))
            blk.sync(emit_for("sync"))


def build_nc(cfg, debug=False, phases=4):
    cf = derive(cfg)
    D, S, F, E, SH = cf["D"], cf["S"], cf["F"], cf["E"], cf["SH"]
    NC, H, DC, FC = cf["NC"], cf["H"], cf["DC"], cf["FC"]
    TS, T, TCH = cf["TS"], cf["T"], cf["TCH"]
    TBW, QS, EXP, GE, NG = cf["TBW"], cf["QS"], cf["EXP"], cf["GE"], cf["NG"]
    DQW, NDQ = cf["DQW"], cf["NDQ"]
    NTB = 2

    nc = bass.Bass("TRN2", target_bir_lowering=False, debug=debug)

    x_loc = nc.dram_tensor("x_loc", [T, D], F32, kind="ExternalInput")
    meta = nc.dram_tensor("meta", [1, 8], I32, kind="ExternalInput")
    wqkv_r = nc.dram_tensor("wqkv_r", [3 * D, DC * 128], F32, kind="ExternalInput")
    wout_r = nc.dram_tensor("wout_r", [D, D], F32, kind="ExternalInput")
    wa_r = nc.dram_tensor("wa_r", [128, DC * E], F32, kind="ExternalInput")
    gq_r = nc.dram_tensor("gq_r", [128, DC], F32, kind="ExternalInput")
    gk_r = nc.dram_tensor("gk_r", [128, DC], F32, kind="ExternalInput")
    g_all = nc.dram_tensor("g_all", [EXP * FC * 128, DC * 128], F32, kind="ExternalInput")
    w1_all = nc.dram_tensor("w1_all", [EXP * FC * 128, DC * 128], F32, kind="ExternalInput")
    w2_all = nc.dram_tensor("w2_all", [EXP * NDQ * 128, FC * DQW], F32, kind="ExternalInput")
    gb_all = nc.dram_tensor("gb_all", [EXP, F], F32, kind="ExternalInput")
    b1_all = nc.dram_tensor("b1_all", [EXP, F], F32, kind="ExternalInput")
    rb2_r = nc.dram_tensor("rb2_r", [E, D], F32, kind="ExternalInput")
    sb2_r = nc.dram_tensor("sb2_r", [SH, D], F32, kind="ExternalInput")
    out_loc = nc.dram_tensor("out_loc", [T, D], F32, kind="ExternalOutput")

    qkv_loc = nc.dram_tensor("qkv_loc", [3 * D, T], BF16)
    q_g = nc.dram_tensor("q_g", [NC * D, T], BF16,
                         addr_space="Shared" if NC > 4 else "Local")
    k_g = nc.dram_tensor("k_g", [NC * D, T], BF16,
                         addr_space="Shared" if NC > 4 else "Local")
    v_g = nc.dram_tensor("v_g", [NC * D, T], BF16,
                         addr_space="Shared" if NC > 4 else "Local")
    ctx_loc = nc.dram_tensor("ctx_loc", [256, S], BF16)
    ctx_g = nc.dram_tensor("ctx_g", [NC * 256, S], BF16,
                           addr_space="Shared" if NC > 4 else "Local")

    es = ExitStack()
    A = es.enter_context
    r = Rec()
    sems = {}
    for e in ENGS:
        sems[(e, None)] = A(nc.semaphore(f"sc_{e}"))
        for ch in range(NCHAN):
            sems[(e, ch)] = A(nc.semaphore(f"sd_{e}{ch}"))

    def dap(t, off, dims):
        return bass.AP(t.ap().tensor, off, [[int(a), int(b)] for a, b in dims])

    xn_fm = A(nc.sbuf_tensor("xn_fm", [128, DC * T], BF16))
    comb_b = A(nc.sbuf_tensor("comb_b", [E, T], BF16))
    gq2_sb = A(nc.sbuf_tensor("gq2_sb", [128, DC], F32))
    gk_sb = A(nc.sbuf_tensor("gk_sb", [128, DC], F32))
    gbT = A(nc.sbuf_tensor("gbT", [128, FC * EXP], F32))
    b1T = A(nc.sbuf_tensor("b1T", [128, FC * EXP], F32))
    rb2_sb = A(nc.sbuf_tensor("rb2_sb", [E, D], BF16))
    sb2_sb = A(nc.sbuf_tensor("sb2_sb", [SH, D], BF16))
    id_bf = A(nc.sbuf_tensor("id_bf", [128, 128], BF16))
    id_f32 = A(nc.sbuf_tensor("id_f32", [128, 128], F32))
    ones128f = A(nc.sbuf_tensor("ones128f", [128, 128], F32))
    ones128b = A(nc.sbuf_tensor("ones128b", [128, 128], BF16))
    ones_c = A(nc.sbuf_tensor("ones_c", [128, 1], BF16))
    ones_rb = A(nc.sbuf_tensor("ones_rb", [1, 128], BF16))
    ones_rf = A(nc.sbuf_tensor("ones_rf", [1, 128], F32))
    onehot = A(nc.sbuf_tensor("onehot", [E, E * 128], BF16))
    wa_sb = A(nc.sbuf_tensor("wa_sb", [128, DC * E], F32))
    meta_sb = A(nc.sbuf_tensor("meta_sb", [1, 8], I32))

    mcell = {}

    if debug:
        dbg_xn = nc.dram_tensor("dbg_xn", [128, DC * T], BF16, kind="ExternalOutput")
        dbg_comb = nc.dram_tensor("dbg_comb", [E, T], BF16, kind="ExternalOutput")
        dbg_acc = nc.dram_tensor("dbg_acc", [128, TCH * D], F32, kind="ExternalOutput")

    t_meta = r.add("sync", lambda e: e.dma_start(out=meta_sb[:], in_=meta.ap()), dma=True, tag="meta")

    r.join("sync", [t_meta], tag="metaj")

    def loadv(e):
        c = e.value_load(meta_sb[0:1, 0:1])
        mcell["c"] = c
        mcell["qk"] = e.snap(c * (256 * T))
        mcell["v"] = e.snap(c * (TCH * 128 * 256))
        mcell["ctx"] = e.snap(c * T)
        return e.nop(nofuse=True)
    r.add("sync", loadv, tag="loadv")

    t_gq = r.add("sync", lambda e: e.dma_start(out=gq2_sb[:], in_=gq_r.ap()), dma=True)
    t_gk = r.add("sync", lambda e: e.dma_start(out=gk_sb[:], in_=gk_r.ap()), dma=True)
    t_wa = r.add("sync", lambda e: e.dma_start(out=wa_sb[:], in_=wa_r.ap()), dma=True)
    t_rb2 = r.add("gpsimd", lambda e: e.dma_start(out=rb2_sb[:], in_=rb2_r.ap()), dma=True)
    t_sb2 = r.add("gpsimd", lambda e: e.dma_start(out=sb2_sb[:], in_=sb2_r.ap()), dma=True)

    t_o128f = r.add("gpsimd", lambda e: e.memset(ones128f[:], 1.0))
    t_o128b = r.add("gpsimd", lambda e: e.memset(ones128b[:], 1.0))
    t_onc = r.add("gpsimd", lambda e: e.memset(ones_c[:], 1.0))
    t_onrb = r.add("gpsimd", lambda e: e.memset(ones_rb[:], 1.0))
    t_onrf = r.add("gpsimd", lambda e: e.memset(ones_rf[:], 1.0))
    t_oh0 = r.add("gpsimd", lambda e: e.memset(onehot[:], 1.0))
    t_idb = r.add("gpsimd", lambda e: e.affine_select(
        out=id_bf[:], in_=ones128b[:], compare_op=OP.is_equal, fill=0.0,
        base=0, pattern=[[-1, 128]], channel_multiplier=1), deps=[t_o128b])
    t_idf = r.add("gpsimd", lambda e: e.affine_select(
        out=id_f32[:], in_=ones128f[:], compare_op=OP.is_equal, fill=0.0,
        base=0, pattern=[[-1, 128]], channel_multiplier=1), deps=[t_o128f])
    t_oh = []
    for ee in range(E):
        t_oh.append(r.add("gpsimd", lambda e, ee=ee: e.affine_select(
            out=onehot[:, ee * 128:(ee + 1) * 128], in_=onehot[:, ee * 128:(ee + 1) * 128],
            compare_op=OP.is_equal, fill=0.0,
            base=-ee, pattern=[[0, 128]], channel_multiplier=1), deps=[t_oh0]))
    t_oh_j = r.join("gpsimd", t_oh, tag="ohj")

    t_gq2 = r.add("vector", lambda e: e.tensor_scalar_mul(
        gq2_sb[:], gq2_sb[:], float(1.0 / np.sqrt(128.0))), deps=[t_gq])

    p1 = ExitStack()
    A1 = p1.enter_context
    bias_raw = A1(nc.sbuf_tensor("bias_raw", [EXP, 2 * F], F32))
    x_sb = [A1(nc.sbuf_tensor(f"x_sb{i}", [128, D], F32)) for i in range(2)]
    xnf = A1(nc.sbuf_tensor("xnf", [128, D], F32))
    sqf = A1(nc.sbuf_tensor("sqf", [128, D], F32))
    xnT_f = [A1(nc.sbuf_tensor(f"xnT_f{i}", [128, 128], F32)) for i in range(2)]
    stat_c = A1(nc.sbuf_tensor("stat_c", [128, 8], F32))
    sg_sb = A1(nc.sbuf_tensor("sg_sb", [128, E], F32))
    tp_sc = A1(nc.sbuf_tensor("tp_sc", [128, 2 * E], F32))
    qraw = A1(nc.sbuf_tensor("qraw", [128, DC * T], BF16))
    kraw = A1(nc.sbuf_tensor("kraw", [128, DC * T], BF16))
    sq_bf = A1(nc.sbuf_tensor("sq_bf", [128, T], BF16))
    wqk_sb = [A1(nc.sbuf_tensor(f"wqk_sb{i}", [128, DC * 128], BF16)) for i in range(2)]
    vtmp = [A1(nc.sbuf_tensor(f"vtmp{i}", [128, TBW], BF16)) for i in range(2)]
    vt_st = [A1(nc.sbuf_tensor(f"vt_st{i}", [128, QS * 128], BF16)) for i in range(2)]
    qn_sc = [A1(nc.sbuf_tensor(f"qn_sc{i}", [128, TBW], F32)) for i in range(2)]
    qn_st = [A1(nc.sbuf_tensor(f"qn_st{i}", [128, TBW], BF16)) for i in range(2)]
    rows = A1(nc.sbuf_tensor("rows", [1, 3 * T], F32))

    t_biasg = r.add("sync", lambda e: e.dma_start(out=bias_raw[:, 0:F], in_=gb_all.ap()), dma=True)
    t_biasb = r.add("sync", lambda e: e.dma_start(out=bias_raw[:, F:2 * F], in_=b1_all.ap()), dma=True)

    pA = ExitStack()
    ps_tr = [pA.enter_context(nc.psum_tensor(f"ps_tr{i}", [128, 512], F32)) for i in range(2)]
    ps_aff = [pA.enter_context(nc.psum_tensor(f"ps_aff{i}", [128, 512], F32)) for i in range(2)]

    ev_bias = {}
    for j, (soff, dst) in enumerate([(0, gbT), (F, b1T)]):
        for fc in range(FC):
            i = j * FC + fc
            tt_ = r.add("tensor", lambda e, soff=soff, fc=fc, i=i: e.transpose(
                ps_tr[i % 2][:, 0:EXP],
                bias_raw[0:EXP, soff + fc * 128: soff + (fc + 1) * 128],
                id_f32[0:EXP, 0:EXP]),
                deps=[t_biasg, t_biasb, t_idf, ev_bias.get(i - 2)], tag=f"biasT{i}")
            ev_bias[i] = r.add("vector", lambda e, dst=dst, fc=fc, i=i: e.tensor_copy(
                dst[:, fc * EXP:(fc + 1) * EXP], ps_tr[i % 2][:, 0:EXP]),
                deps=[tt_], tag=f"biasE{i}")
    t_bias_j = r.join("vector", list(ev_bias.values()), tag="biasj")

    if phases < 0.18:
        t_fin = []
        for tch in range(TCH):
            t_fin.append(r.add("sync", lambda e, tch=tch: e.dma_start(
                out=out_loc.ap()[tch * 128:(tch + 1) * 128, :],
                in_=x_loc.ap()[tch * 128:(tch + 1) * 128, :]),
                dma=True, deps=[t_bias_j, t_oh_j, t_gq2, t_rb2, t_sb2, t_idb], tag=f"dummyout{tch}"))
        r.join("sync", t_fin, tag="final")
        r.finalize_and_emit(nc, sems)
        p1.close()
        es.close()
        return nc


    ev_xn = {}
    ev_comb = {}
    ev_combT = {}
    ev_tr_last = {}
    t_sg_tok = {}
    last_xn_apply = {}
    TCHrun = TCH if phases >= 0.22 else 1
    for tch in range(TCHrun):
        t_x = r.add("sync", lambda e, tch=tch: e.dma_start(
            out=x_sb[tch % 2][:], in_=x_loc.ap()[tch * 128:(tch + 1) * 128, :]),
            dma=True, deps=[last_xn_apply.get(tch - 2)], tag=f"x{tch}")
        t_sq = r.add("scalar", lambda e, tch=tch: e.activation(
            sqf[:], x_sb[tch % 2][:], ACTF.Square),
            deps=[t_x, last_xn_apply.get(tch - 1)], tag=f"sq{tch}")
        c1 = r.add("vector", lambda e, tch=tch: e.reduce_sum(
            stat_c[:, 0:1], x_sb[tch % 2][:], axis=AX), deps=[t_x])
        c2 = r.add("vector", lambda e: e.reduce_sum(stat_c[:, 1:2], sqf[:], axis=AX),
                   deps=[t_sq])
        c3 = r.add("vector", lambda e: e.tensor_scalar_mul(stat_c[:, 0:1], stat_c[:, 0:1], 1.0 / D), deps=[c1])
        c4 = r.add("vector", lambda e: e.tensor_scalar_mul(stat_c[:, 1:2], stat_c[:, 1:2], 1.0 / D), deps=[c2])
        c5 = r.add("vector", lambda e: e.tensor_tensor(stat_c[:, 2:3], stat_c[:, 0:1], stat_c[:, 0:1], OP.mult), deps=[c3])
        c6 = r.add("vector", lambda e: e.tensor_tensor(stat_c[:, 3:4], stat_c[:, 1:2], stat_c[:, 2:3], OP.subtract), deps=[c4, c5])
        t_var = r.add("vector", lambda e: e.tensor_scalar_add(stat_c[:, 3:4], stat_c[:, 3:4], 1e-5), deps=[c6])
        t_sqrt = r.add("scalar", lambda e: e.activation(stat_c[:, 4:5], stat_c[:, 3:4], ACTF.Sqrt),
                       deps=[t_var], tag=f"sqrt{tch}")
        t_rec = r.add("vector", lambda e: e.reciprocal(stat_c[:, 5:6], stat_c[:, 4:5]),
                      deps=[t_sqrt])
        last_xn_apply[tch] = r.add("vector", lambda e, tch=tch: e.tensor_scalar(
            xnf[:], x_sb[tch % 2][:], stat_c[:, 0:1], stat_c[:, 5:6],
            OP.subtract, OP.mult),
            deps=[t_rec, c3, ev_tr_last.get(tch - 1)], tag=f"xn{tch}")
        SKIP = os.environ.get("K_SKIP", "")
        if "ln_only" in SKIP:
            ev_combT[tch] = last_xn_apply[tch]
            continue
        ev_evf = {}
        last_router_mm = {}
        for dc in range(DC):
            war = [ev_xn.get((dc - 2, tch)), ev_evf.get(dc - 2)]
            if dc < 2:
                war.append(ev_combT.get(tch - 1))
                if tch == 0:
                    war.append(t_bias_j)
            t_tr = r.add("tensor", lambda e, dc=dc: e.transpose(
                ps_tr[dc % 2][:, 0:128], xnf[:, dc * 128:(dc + 1) * 128], id_f32[:, :]),
                deps=[last_xn_apply[tch], t_idf] + war, tag=f"tr{tch}_{dc}")
            ev_evf[dc] = r.add("vector", lambda e, dc=dc: e.tensor_copy(
                xnT_f[dc % 2][:], ps_tr[dc % 2][:, 0:128]),
                deps=[t_tr, last_router_mm.get(dc - 2)], tag=f"evf{tch}_{dc}")
            ev_xn[(dc, tch)] = r.add("scalar", lambda e, dc=dc, tch=tch: e.activation(
                xn_fm[:, dc * T + tch * 128: dc * T + (tch + 1) * 128],
                ps_tr[dc % 2][:, 0:128], ACTF.Copy),
                deps=[t_tr, ev_evf[dc]], tag=f"xnb{tch}_{dc}")
            if "no_aff" in SKIP:
                last_router_mm[dc] = ev_evf[dc]
            else:
                last_router_mm[dc] = r.add("tensor", lambda e, dc=dc, tch=tch: e.matmul(
                    ps_aff[tch % 2][:, 0:E], xnT_f[dc % 2][:], wa_sb[:, dc * E:(dc + 1) * E],
                    start=(dc == 0), stop=(dc == DC - 1)),
                    deps=[ev_evf[dc], t_wa, t_sg_tok.get(tch - 2)], tag=f"aff{tch}_{dc}")
        ev_tr_last[tch] = last_router_mm[DC - 1]
        if "no_router_tail" in SKIP:
            ev_combT[tch] = r.join("scalar", [ev_xn[(dc, tch)] for dc in range(DC)], tag=f"sk{tch}")
            t_sg_tok[tch] = ev_combT[tch]
            ev_comb[tch] = ev_combT[tch]
            continue
        t_sg = r.add("scalar", lambda e, tch=tch: e.activation(
            sg_sb[:], ps_aff[tch % 2][:, 0:E], ACTF.Sigmoid),
            deps=[last_router_mm[DC - 1], ev_comb.get(tch - 1)], tag=f"sig{tch}")
        t_sg_tok[tch] = t_sg
        u1 = r.add("vector", lambda e: e.reduce_max(stat_c[:, 6:7], sg_sb[:], axis=AX), deps=[t_sg])
        u2 = r.add("vector", lambda e: e.tensor_single_scalar(
            tp_sc[:, 0:E], sg_sb[:], stat_c[:, 6:7], OP.is_ge), deps=[u1])
        u3 = r.add("vector", lambda e: e.scalar_tensor_tensor(
            tp_sc[:, E:2 * E], tp_sc[:, 0:E], NEG_BIG, sg_sb[:], OP.mult, OP.add), deps=[u2])
        u4 = r.add("vector", lambda e: e.reduce_max(stat_c[:, 7:8], tp_sc[:, E:2 * E], axis=AX), deps=[u3])
        u5 = r.add("vector", lambda e: e.tensor_single_scalar(
            tp_sc[:, E:2 * E], tp_sc[:, E:2 * E], stat_c[:, 7:8], OP.is_ge), deps=[u4])
        u6 = r.add("vector", lambda e: e.tensor_tensor(
            tp_sc[:, 0:E], tp_sc[:, 0:E], tp_sc[:, E:2 * E], OP.add), deps=[u5])
        u7 = r.add("vector", lambda e: e.tensor_tensor(
            stat_c[:, 6:7], stat_c[:, 6:7], stat_c[:, 7:8], OP.add), deps=[u4, u6])
        t8 = r.add("vector", lambda e: e.reciprocal(stat_c[:, 7:8], stat_c[:, 6:7]), deps=[u7])
        ev_comb[tch] = r.add("vector", lambda e: e.scalar_tensor_tensor(
            tp_sc[:, E:2 * E], sg_sb[:], stat_c[:, 7:8], tp_sc[:, 0:E],
            OP.mult, OP.mult), deps=[t8, u6], tag=f"comb{tch}")
        if "no_combT" in SKIP:
            ev_combT[tch] = ev_comb[tch]
            continue
        t_ct = r.add("tensor", lambda e, tch=tch: e.transpose(
            ps_tr[tch % 2][0:E, 0:128], tp_sc[:, E:2 * E], id_f32[:, :]),
            deps=[ev_comb[tch],
                  ev_xn.get((DC - 2, tch)), ev_xn.get((DC - 1, tch)),
                  ev_evf.get(DC - 2), ev_evf.get(DC - 1)],
            tag=f"combT{tch}")
        ev_combT[tch] = r.add("scalar", lambda e, tch=tch: e.activation(
            comb_b[:, tch * 128:(tch + 1) * 128], ps_tr[tch % 2][0:E, 0:128], ACTF.Copy),
            deps=[t_ct], tag=f"combE{tch}")

    t_comb_j = r.join("scalar", [ev_combT[t] for t in range(TCHrun)], tag="combj")
    pA.close()

    if phases < 0.3:
        t_fin = []
        for tch in range(TCH):
            t_fin.append(r.add("sync", lambda e, tch=tch: e.dma_start(
                out=out_loc.ap()[tch * 128:(tch + 1) * 128, :],
                in_=x_loc.ap()[tch * 128:(tch + 1) * 128, :]),
                dma=True, deps=[t_comb_j], tag=f"dummyout{tch}"))
        r.join("sync", t_fin, tag="final")
        r.finalize_and_emit(nc, sems)
        p1.close()
        es.close()
        return nc


    pB = ExitStack()
    ps_qkv = [pB.enter_context(nc.psum_tensor(f"ps_qkv{i}", [128, TBW], F32)) for i in range(2)]
    ps_vtr = [pB.enter_context(nc.psum_tensor(f"ps_vtr{i}", [128, 128], BF16)) for i in range(2)]

    t_wqk = {}
    ev_epi = {}
    mm_last = {}
    t_vb = {}
    ev_vtr = {}
    vtr_cnt = 0
    for oc in range(3 * DC):
        t_wqk[oc] = r.add("gpsimd", lambda e, oc=oc: e.dma_start(
            out=wqk_sb[oc % 2][:], in_=wqkv_r.ap()[oc * 128:(oc + 1) * 128, :]),
            dma=True, deps=[mm_last.get((oc - 2, NTB - 1))], tag=f"wqkv{oc}")
        for tb in range(NTB):
            gi = oc * NTB + tb
            pp = gi % 2
            mt = None
            for dc in range(DC):
                deps = [t_wqk[oc], ev_epi.get(gi - 2)]
                if oc == 0:
                    deps += [ev_xn[(dc, tc2)] for tc2 in range(tb * QS, (tb + 1) * QS)]
                if gi < 2 and dc == 0:
                    deps += [t_comb_j, t_sg_tok[TCH - 1], ev_tr_last[TCH - 1],
                             ev_combT[TCH - 1]]
                mt = r.add("tensor", lambda e, oc=oc, tb=tb, dc=dc, pp=pp: e.matmul(
                    ps_qkv[pp][:, :], wqk_sb[oc % 2][:, dc * 128:(dc + 1) * 128],
                    xn_fm[:, dc * T + tb * TBW: dc * T + (tb + 1) * TBW],
                    start=(dc == 0), stop=(dc == DC - 1)),
                    deps=deps, tag=f"qkvmm{oc}_{tb}_{dc}", group=("qkv", oc, tb))
            mm_last[(oc, tb)] = mt
            if oc < 2 * DC:
                dst = qraw if oc < DC else kraw
                c0 = (oc % DC) * T + tb * TBW
                ev_epi[gi] = r.add("scalar", lambda e, dst=dst, c0=c0, pp=pp: e.activation(
                    dst[:, c0:c0 + TBW], ps_qkv[pp][:, :], ACTF.Copy),
                    deps=[mt], tag=f"qke{oc}_{tb}")
            else:
                vc = oc - 2 * DC
                t_ev = r.add("scalar", lambda e, pp=pp, tb=tb: e.activation(
                    vtmp[tb % 2][:, :], ps_qkv[pp][:, :], ACTF.Copy),
                    deps=[mt, t_vb.get((vc - 1, tb))], tag=f"vev{vc}_{tb}")
                ev_epi[gi] = t_ev
                for s in range(QS):
                    vtr_cnt += 1
                    vv = vtr_cnt % 2
                    t_vt = r.add("tensor", lambda e, s=s, tb=tb, vv=vv: e.transpose(
                        ps_vtr[vv][:, :], vtmp[tb % 2][:, s * 128:(s + 1) * 128], id_bf[:, :]),
                        deps=[t_ev, t_idb, ev_vtr.get(vtr_cnt - 2)] +
                             ([t_comb_j, t_sg_tok[TCH - 1]] if vtr_cnt <= 2 else []),
                        tag=f"vtr{vc}_{tb}_{s}")
                    ev_vtr[vtr_cnt] = r.add("vector", lambda e, s=s, tb=tb, vv=vv: e.tensor_copy(
                        vt_st[tb % 2][:, s * 128:(s + 1) * 128], ps_vtr[vv][:, :]),
                        deps=[t_vt, t_vb.get((vc - 1, tb))], tag=f"vte{vc}_{tb}_{s}")
                base = 2 * D * T + (vc // 2) * (TCH * 128 * 256) + (vc % 2) * 128 \
                    + tb * QS * 128 * 256
                t_vb[(vc, tb)] = r.add("sync", lambda e, base=base, tb=tb: e.dma_start(
                    out=dap(qkv_loc, base, [[256, 128], [128 * 256, QS], [1, 128]]),
                    in_=bass.AP(vt_st[tb % 2][:].tensor, vt_st[tb % 2][:].offset,
                                [[QS * 128, 128], [128, QS], [1, 128]])),
                    dma=True, deps=[ev_vtr[vtr_cnt - s2] for s2 in range(QS)], tag=f"vb{vc}_{tb}")

    t_qk_j = r.join("scalar", [ev_epi[oc * NTB + tb] for oc in range(2 * DC) for tb in range(NTB)],
                    tag="qkj")
    t_vtr_j = r.join("vector", [ev_vtr[i + 1] for i in range(vtr_cnt)], tag="vtrj")
    pB.close()

    if phases < 0.6:
        t_fin = []
        for tch in range(TCH):
            t_fin.append(r.add("sync", lambda e, tch=tch: e.dma_start(
                out=out_loc.ap()[tch * 128:(tch + 1) * 128, :],
                in_=x_loc.ap()[tch * 128:(tch + 1) * 128, :]),
                dma=True, deps=[t_qk_j, t_vtr_j] + list(t_vb.values()), tag=f"dummyout{tch}"))
        r.join("sync", t_fin, tag="final")
        r.finalize_and_emit(nc, sems)
        p1.close()
        es.close()
        return nc


    def qk_ln(raw, gamma, row_base, tag, gates, prev_outs=None):
        pC = ExitStack()
        ps_s1 = pC.enter_context(nc.psum_tensor(f"ps_s1{tag}", [1, T], F32))
        ps_s2 = pC.enter_context(nc.psum_tensor(f"ps_s2{tag}", [1, T], F32))
        mm1 = mm2 = None
        prev_mm2 = None
        for qc in range(DC):
            t_sqx = r.add("scalar", lambda e, qc=qc: e.activation(
                sq_bf[:, :], raw[:, qc * T:(qc + 1) * T], ACTF.Square),
                deps=[t_qk_j, prev_mm2], tag=f"lnsq{tag}{qc}")
            for tb in range(NTB):
                mm1 = r.add("tensor", lambda e, qc=qc, tb=tb: e.matmul(
                    ps_s1[0:1, tb * TBW:(tb + 1) * TBW], ones_c[:, 0:1],
                    raw[:, qc * T + tb * TBW: qc * T + (tb + 1) * TBW],
                    start=(qc == 0), stop=(qc == DC - 1)),
                    deps=[t_qk_j, t_onc] + (gates if qc == 0 else []), tag=f"lns1{tag}{qc}_{tb}")
                mm2 = r.add("tensor", lambda e, qc=qc, tb=tb: e.matmul(
                    ps_s2[0:1, tb * TBW:(tb + 1) * TBW], ones_c[:, 0:1],
                    sq_bf[:, tb * TBW:(tb + 1) * TBW],
                    start=(qc == 0), stop=(qc == DC - 1)),
                    deps=[t_sqx, t_onc] + (gates if qc == 0 else []), tag=f"lns2{tag}{qc}_{tb}")
            prev_mm2 = mm2
        t_m = r.add("vector", lambda e: e.tensor_scalar_mul(rows[0:1, 0:T], ps_s1[0:1, :], 1.0 / D),
                    deps=[mm1], tag=f"lnm{tag}")
        t_s = r.add("vector", lambda e: e.tensor_scalar_mul(rows[0:1, 2 * T:3 * T], ps_s2[0:1, :], 1.0 / D),
                    deps=[mm2], tag=f"lns{tag}")
        w1_ = r.add("vector", lambda e: e.tensor_tensor(rows[0:1, T:2 * T], rows[0:1, 0:T], rows[0:1, 0:T], OP.mult), deps=[t_m])
        w2_ = r.add("vector", lambda e: e.tensor_tensor(rows[0:1, 2 * T:3 * T], rows[0:1, 2 * T:3 * T], rows[0:1, T:2 * T], OP.subtract), deps=[t_s, w1_])
        t_v = r.add("vector", lambda e: e.tensor_scalar_add(rows[0:1, 2 * T:3 * T], rows[0:1, 2 * T:3 * T], 1e-6), deps=[w2_])
        t_sq2 = r.add("scalar", lambda e: e.activation(rows[0:1, T:2 * T], rows[0:1, 2 * T:3 * T], ACTF.Sqrt),
                      deps=[t_v], tag=f"lnsqrt{tag}")
        t_r = r.add("vector", lambda e: e.reciprocal(rows[0:1, T:2 * T], rows[0:1, T:2 * T]),
                    deps=[t_sq2])
        pC.close()
        pD = ExitStack()
        ps_bm = pD.enter_context(nc.psum_tensor(f"ps_bm{tag}", [128, TBW], F32))
        ps_br = pD.enter_context(nc.psum_tensor(f"ps_br{tag}", [128, TBW], F32))
        outs = []
        appl = []
        for tb in range(NTB):
            bgate = [t_m, t_s, t_r] + appl[-2:]
            bm = r.add("tensor", lambda e, tb=tb: e.matmul(
                ps_bm[:, :], ones_rf[0:1, :], rows[0:1, tb * TBW:(tb + 1) * TBW],
                start=True, stop=True),
                deps=[t_m, t_onrf] + bgate, tag=f"lnbm{tag}{tb}")
            br = r.add("tensor", lambda e, tb=tb: e.matmul(
                ps_br[:, :], ones_rf[0:1, :], rows[0:1, T + tb * TBW: T + (tb + 1) * TBW],
                start=True, stop=True),
                deps=[t_r, t_onrf] + bgate, tag=f"lnbr{tag}{tb}")
            for qc in range(DC):
                i = tb * DC + qc
                pp = i % 2
                war_st = outs[i - 2] if i >= 2 else (prev_outs[i - 2] if prev_outs else None)
                tA = r.add("vector", lambda e, qc=qc, tb=tb, pp=pp: e.tensor_tensor(
                    qn_sc[pp][:, :], raw[:, qc * T + tb * TBW: qc * T + (tb + 1) * TBW],
                    ps_bm[:, :], OP.subtract),
                    deps=[bm, war_st], tag=f"lnap{tag}{i}")
                tB = r.add("vector", lambda e, qc=qc, pp=pp: e.scalar_tensor_tensor(
                    qn_st[pp][:, :], qn_sc[pp][:, :], gamma[:, qc:qc + 1], ps_br[:, :],
                    OP.mult, OP.mult),
                    deps=[tA, br, t_gq2, t_gk, war_st], tag=f"lnap2{tag}{i}")
                appl.append(tB)
                td = r.add("sync", lambda e, qc=qc, tb=tb, pp=pp, row_base=row_base: e.dma_start(
                    out=qkv_loc.ap()[row_base + qc * 128: row_base + (qc + 1) * 128,
                                     tb * TBW:(tb + 1) * TBW],
                    in_=qn_st[pp][:, :]),
                    dma=True, deps=[tB], tag=f"lnb{tag}{i}")
                outs.append(td)
        pD.close()
        gate_next = r.join("vector", appl[-4:] + [t_m, t_s, t_r], tag=f"lng{tag}")
        return outs, gate_next

    qb_outs, gate_q = qk_ln(qraw, gq2_sb, 0, "q", gates=[t_qk_j, t_vtr_j])
    kb_outs, gate_k = qk_ln(kraw, gk_sb, D, "k", gates=[gate_q], prev_outs=qb_outs)

    if phases < 0.9:
        t_fin = []
        for tch in range(TCH):
            t_fin.append(r.add("sync", lambda e, tch=tch: e.dma_start(
                out=out_loc.ap()[tch * 128:(tch + 1) * 128, :],
                in_=x_loc.ap()[tch * 128:(tch + 1) * 128, :]),
                dma=True, deps=qb_outs + kb_outs + list(t_vb.values()), tag=f"dummyout{tch}"))
        r.join("sync", t_fin, tag="final")
        r.finalize_and_emit(nc, sems)
        p1.close()
        es.close()
        return nc


    t_agv = r.add("gpsimd", lambda e: e.collective_compute(
        "AllGather", OP.bypass, replica_groups=[list(range(NC))],
        ins=[qkv_loc.ap()[2 * D:3 * D, :].opt()], outs=[v_g.ap().opt()]),
        deps=list(t_vb.values()), tag="agv")
    t_agq = r.add("gpsimd", lambda e: e.collective_compute(
        "AllGather", OP.bypass, replica_groups=[list(range(NC))],
        ins=[qkv_loc.ap()[0:D, :].opt()], outs=[q_g.ap().opt()]),
        deps=qb_outs, tag="agq")
    t_agk = r.add("gpsimd", lambda e: e.collective_compute(
        "AllGather", OP.bypass, replica_groups=[list(range(NC))],
        ins=[qkv_loc.ap()[D:2 * D, :].opt()], outs=[k_g.ap().opt()]),
        deps=kb_outs, tag="agk")
    t_ag1 = t_agk

    if debug:
        r.add("sync", lambda e: e.dma_start(out=dbg_xn.ap(), in_=xn_fm[:, :]),
              dma=True, deps=[t_qk_j], tag="dbgxn")
        r.add("sync", lambda e: e.dma_start(out=dbg_comb.ap(), in_=comb_b[:, :]),
              dma=True, deps=[t_comb_j], tag="dbgcomb")

    p1.close()

    if phases < 2:
        t_fin = []
        for tch in range(TCH):
            t_fin.append(r.add("sync", lambda e, tch=tch: e.dma_start(
                out=out_loc.ap()[tch * 128:(tch + 1) * 128, :],
                in_=x_loc.ap()[tch * 128:(tch + 1) * 128, :]),
                dma=True, deps=[t_ag1], tag=f"dummyout{tch}"))
        r.join("sync", t_fin, tag="final")
        r.finalize_and_emit(nc, sems)
        es.close()
        return nc

    p3 = ExitStack()
    A3 = p3.enter_context
    q2_sb = A3(nc.sbuf_tensor("q2_sb", [128, 2 * S], BF16))
    k2_sb = A3(nc.sbuf_tensor("k2_sb", [128, 2 * S], BF16))
    v2_sb = A3(nc.sbuf_tensor("v2_sb", [128, 2 * S], BF16))
    maskA = A3(nc.sbuf_tensor("maskA", [128, QS * TBW], BF16))
    maskB = A3(nc.sbuf_tensor("maskB", [128, QS * TBW], BF16))
    sA_sc = [A3(nc.sbuf_tensor(f"sA_sc{i}", [128, TBW], F32)) for i in range(2)]
    sB_sc = [A3(nc.sbuf_tensor(f"sB_sc{i}", [128, TBW], F32)) for i in range(3)]
    pt_sb = [A3(nc.sbuf_tensor(f"pt_sb{i}", [128, TBW], BF16)) for i in range(4)]
    nm_sb = A3(nc.sbuf_tensor("nm_sb", [128, 2 * QS + 2], F32))
    nmrow = [A3(nc.sbuf_tensor(f"nmrow{i}", [1, TBW], BF16)) for i in range(2)]
    l_row = [A3(nc.sbuf_tensor(f"l_row{i}", [1, TBW], F32)) for i in range(2)]
    linv_sb = [A3(nc.sbuf_tensor(f"linv_sb{i}", [128, TBW], F32)) for i in range(2)]
    ctx_st = [A3(nc.sbuf_tensor(f"ctx_st{i}", [128, TBW], BF16)) for i in range(2)]

    pP3 = ExitStack()
    ps_sa = [pP3.enter_context(nc.psum_tensor(f"ps_sa{i}", [128, TBW], F32)) for i in range(4)]
    ps_sb_ = [pP3.enter_context(nc.psum_tensor(f"ps_sb{i}", [128, TBW], F32)) for i in range(2)]
    ps_ctx = [pP3.enter_context(nc.psum_tensor("ps_ctx0", [128, TBW], F32))]
    ps_lb = pP3.enter_context(nc.psum_tensor("ps_lb", [128, TBW], F32))
    ps_l = ps_lb

    t_mA0 = r.add("gpsimd", lambda e: e.memset(maskA[:], 0.0), deps=[t_agv])
    t_mB0 = r.add("gpsimd", lambda e: e.memset(maskB[:], 0.0), deps=[t_agv])
    t_mA = []
    t_mB = []
    for qs in range(QS):
        t_mA.append(r.add("gpsimd", lambda e, qs=qs: e.affine_select(
            out=maskA[:, qs * TBW:(qs + 1) * TBW], in_=maskA[:, qs * TBW:(qs + 1) * TBW],
            compare_op=OP.is_ge, fill=NEG_BIG,
            base=qs * 128, pattern=[[-1, TBW]], channel_multiplier=1), deps=[t_mA0]))
    for kc in range(QS):
        t_mB.append(r.add("gpsimd", lambda e, kc=kc: e.affine_select(
            out=maskB[:, kc * TBW:(kc + 1) * TBW], in_=maskB[:, kc * TBW:(kc + 1) * TBW],
            compare_op=OP.is_ge, fill=NEG_BIG,
            base=-kc * 128, pattern=[[1, TBW]], channel_multiplier=-1), deps=[t_mB0]))
    t_mA_j = r.join("gpsimd", t_mA, tag="mAj")
    t_mB_j = r.join("gpsimd", t_mB, tag="mBj")

    t_qld_all, t_kld_all = [], []
    for hh in range(2):
        t_qld_all.append(r.add("sync", lambda e, hh=hh: e.dma_start(
            out=bass.AP(q2_sb[:].tensor, q2_sb[:].offset + hh * S,
                        [[2 * S, 128], [T, NC], [1, T]]),
            in_=dap(q_g, mcell["qk"] + hh * 128 * T,
                    [[T, 128], [D * T, NC], [1, T]])),
            dma=True, deps=[t_agq], tag=f"qld{hh}"))
        t_kld_all.append(r.add("sync", lambda e, hh=hh: e.dma_start(
            out=bass.AP(k2_sb[:].tensor, k2_sb[:].offset + hh * S,
                        [[2 * S, 128], [T, NC], [1, T]]),
            in_=dap(k_g, mcell["qk"] + hh * 128 * T,
                    [[T, 128], [D * T, NC], [1, T]])),
            dma=True, deps=[t_agk], tag=f"kld{hh}"))
    t_vld = {}
    for j in range(NC):
        t_vld[j] = r.add("sync", lambda e, j=j: e.dma_start(
            out=v2_sb[:, j * (TCH * 256): (j + 1) * (TCH * 256)],
            in_=dap(v_g, mcell["v"] + j * D * T,
                    [[256, 128], [128 * 256, TCH], [1, 256]])),
            dma=True, deps=[t_agv], tag=f"vld{j}")
    t_qld = {k_: t_qld_all[k_[1]] for k_ in [(j, hh) for j in range(NC) for hh in range(2)]}
    t_kld = {k_: t_kld_all[k_[1]] for k_ in [(j, hh) for j in range(NC) for hh in range(2)]}

    NQT = S // TBW
    seq = [(hh, qt) for hh in range(2) for qt in range(NQT)]
    ev_red = {}
    ev_exp = {}
    ev_ptuse = {}
    ev_norm = {}
    ev_nmT = {}
    ev_nmrow = {}
    ev_lastB = {}
    ev_linv = {}
    t_ctxst = []

    Acons = {}
    t_nmdma = {}
    t_nmj = {}
    arot_box = [0]

    def pass_A(i):
        hh, qt = seq[i]
        scr = 2 * QS
        for kt in range(qt + 1):
            for half in range(2):
                arot = arot_box[0]
                grp = ("pa", i, kt, half)
                cons = []
                war = Acons.get(arot - 2, [])
                for t2 in range(2):
                    qs = 2 * half + t2
                    c0 = 2 * qs
                    bk = (2 * arot + t2) % 4
                    diag = (kt == qt)
                    mm = r.add("tensor", lambda e, hh=hh, qt=qt, qs=qs, kt=kt, bk=bk: e.matmul(
                        ps_sa[bk][:, :],
                        q2_sb[:, hh * S + qt * TBW + qs * 128: hh * S + qt * TBW + (qs + 1) * 128],
                        k2_sb[:, hh * S + kt * TBW: hh * S + (kt + 1) * TBW],
                        start=True, stop=True),
                        deps=[t_qld[((qt * TBW) // T, hh)], t_kld[((kt * TBW) // T, hh)]] + war,
                        tag=f"pA{i}_{qs}_{kt}", group=grp)
                    war = []
                    src_ = ps_sa[bk]
                    dep_m = [mm]
                    if diag:
                        tmk = r.add("vector", lambda e, qs=qs, bk=bk: e.tensor_tensor(
                            sA_sc[qs % 2][:, :], ps_sa[bk][:, :],
                            maskA[:, qs * TBW:(qs + 1) * TBW], OP.add),
                            deps=[mm, t_mA_j], tag=f"pAm{i}_{qs}")
                        src_ = sA_sc[qs % 2]
                        dep_m = [tmk]
                        cons.append(tmk)
                    if kt == 0:
                        ev_red[(i, qs, kt)] = r.add("vector", lambda e, src_=src_, c0=c0: e.reduce_max(
                            nm_sb[:, c0:c0 + 1], src_[:, :], axis=AX, negate=True),
                            deps=dep_m + [t_nmdma.get((i - 1, qs)), t_nmdma.get((i - 2, qs))],
                            tag=f"pAr{i}_{qs}_{kt}")
                    else:
                        t_rm = r.add("vector", lambda e, src_=src_: e.reduce_max(
                            nm_sb[:, scr:scr + 1], src_[:, :], axis=AX, negate=True),
                            deps=dep_m + [ev_red[(i, qs, kt - 1)]], tag=f"pAr{i}_{qs}_{kt}")
                        ev_red[(i, qs, kt)] = r.add("vector", lambda e, kt=kt, c0=c0: e.tensor_tensor(
                            nm_sb[:, c0 + kt % 2:c0 + kt % 2 + 1],
                            nm_sb[:, c0 + (kt - 1) % 2:c0 + (kt - 1) % 2 + 1],
                            nm_sb[:, scr:scr + 1], OP.min),
                            deps=[t_rm, ev_red[(i, qs, kt - 1)]])
                    if not diag:
                        cons.append(ev_red[(i, qs, kt)])
                Acons[arot] = cons
                arot_box[0] = arot + 1
                if kt == qt:
                    for qs in (2 * half, 2 * half + 1):
                        fin = 2 * qs + qt % 2
                        t_nmdma[(i, qs)] = r.add("gpsimd", lambda e, i=i, qs=qs, fin=fin: e.dma_start(
                            out=nmrow[i % 2][0:1, qs * 128:(qs + 1) * 128],
                            in_=nm_sb[:, fin:fin + 1]),
                            dma=True, deps=[ev_red[(i, qs, qt)], ev_lastB.get(i - 2)],
                            tag=f"nmD{i}_{qs}")
                yield
        t_nmj[i] = r.join("gpsimd", [t_nmdma[(i, qs)] for qs in range(QS)], tag=f"nmJ{i}")
        yield

    Bexp = []
    Pgrp = {}
    Qgrp = {}
    Bgrp_base = {}
    sB_last = {}
    CH = 2

    def emit_exp(i, kc):
        hh, qt = seq[i]
        g = Bgrp_base[i] + kc
        bb = g % CH
        pt4 = g % 4
        diag = (kc >= qt * QS)
        if diag:
            kcd = kc - qt * QS
            tmk = r.add("vector", lambda e, kcd=kcd, bb=bb: e.tensor_tensor(
                sB_sc[bb][:, :], ps_sb_[bb][:, :],
                maskB[:, kcd * TBW:(kcd + 1) * TBW], OP.add),
                deps=[Pgrp[g // CH], Qgrp.get(g // CH - 2), t_mB_j,
                      sB_last.get(bb)], tag=f"pBm{i}_{kc}")
            te = r.add("scalar", lambda e, bb=bb, pt4=pt4: e.activation(
                pt_sb[pt4][:, :], sB_sc[bb][:, :], ACTF.Exp),
                deps=[tmk], tag=f"pBe{i}_{kc}")
            sB_last[bb] = te
        else:
            te = r.add("scalar", lambda e, bb=bb, pt4=pt4: e.activation(
                pt_sb[pt4][:, :], ps_sb_[bb][:, :], ACTF.Exp),
                deps=[Pgrp[g // CH], Qgrp.get(g // CH - 2)], tag=f"pBe{i}_{kc}")
        assert len(Bexp) == g
        Bexp.append(te)

    def emit_P(i, c):
        hh, qt = seq[i]
        gc = (Bgrp_base[i] + c * CH) // CH
        grp = ("pbP", i, c)
        war = Bexp[CH * gc - 1] if CH * gc >= 1 else None
        tok = None
        for j in range(CH):
            kc = c * CH + j
            tok = r.add("tensor", lambda e, hh=hh, qt=qt, kc=kc, j=j: e.matmul(
                ps_sb_[j][:, :],
                k2_sb[:, hh * S + kc * 128: hh * S + (kc + 1) * 128],
                q2_sb[:, hh * S + qt * TBW: hh * S + (qt + 1) * TBW],
                start=True, stop=False),
                deps=[war], tag=f"pB{i}_{kc}", group=grp)
            war = None
            tok = r.add("tensor", lambda e, i=i, j=j: e.matmul(
                ps_sb_[j][:, :], ones_rb[0:1, :], nmrow[i % 2][0:1, :],
                start=False, stop=True),
                deps=[t_nmj[i], t_onrb], tag=f"pBr{i}_{kc}", group=grp)
        Pgrp[gc] = tok

    def emit_Q(i, c):
        hh, qt = seq[i]
        nkc = (qt + 1) * QS
        gc = (Bgrp_base[i] + c * CH) // CH
        grp = ("pbQ", i, c)
        war = Bexp[CH * gc + CH - 1]
        tok = None
        for j in range(CH):
            kt_ = c * CH + j
            pt4 = (CH * gc + j) % 4
            tok = r.add("tensor", lambda e, kt_=kt_, pt4=pt4, nkc=nkc: e.matmul(
                ps_l[0:1, :], ones_c[:, 0:1], pt_sb[pt4][:, :],
                start=(kt_ == 0), stop=(kt_ == nkc - 1)),
                deps=[war, t_onc,
                      (ev_linv.get(i - 1) if kt_ == 0 else None)],
                tag=f"pBl{i}_{kt_}", group=grp)
            war = None
            tok = r.add("tensor", lambda e, hh=hh, kt_=kt_, pt4=pt4, nkc=nkc, i=i: e.matmul(
                ps_ctx[0][:, :],
                v2_sb[:, kt_ * 256 + hh * 128: kt_ * 256 + (hh + 1) * 128],
                pt_sb[pt4][:, :],
                start=(kt_ == 0), stop=(kt_ == nkc - 1)),
                deps=[war, t_vld[(kt_ * 128) // T],
                      (ev_norm.get(i - 1) if kt_ == 0 else None)],
                tag=f"pBc{i}_{kt_}", group=grp)
            ev_ptuse[(i, kt_)] = tok
        Qgrp[gc] = tok

    def pass_B(i):
        hh, qt = seq[i]
        nkc = (qt + 1) * QS
        nch = nkc // CH
        Bgrp_base[i] = len(Bexp)
        for c in range(nch):
            if c >= 1:
                for j in range(CH):
                    emit_exp(i, CH * (c - 1) + j)
            emit_P(i, c)
            if c >= 1:
                emit_Q(i, c - 1)
            yield
        for j in range(CH):
            emit_exp(i, CH * (nch - 1) + j)
        yield
        emit_Q(i, nch - 1)
        ev_lastB[i] = ev_ptuse[(i, nkc - 1)]
        t_lev = r.add("vector", lambda e, i=i: e.tensor_copy(l_row[i % 2][0:1, :], ps_l[0:1, 0:TBW]),
                      deps=[ev_ptuse[(i, nkc - 1)]], tag=f"lev{i}")
        t_lrec = r.add("vector", lambda e, i=i: e.reciprocal(l_row[i % 2][0:1, :], l_row[i % 2][0:1, :]),
                       deps=[t_lev])
        for _ in range(5):
            yield
        t_lb = r.add("tensor", lambda e, i=i: e.matmul(
            ps_lb[:, :], ones_rf[0:1, :], l_row[i % 2][0:1, :], start=True, stop=True),
            deps=[t_lrec, t_onrf, ev_linv.get(i - 1)], tag=f"lb{i}")
        ev_linv[i] = r.add("vector", lambda e, i=i: e.tensor_copy(linv_sb[i % 2][:, :], ps_lb[:, :]),
                           deps=[t_lb], tag=f"linv{i}")
        ev_norm[i] = r.add("vector", lambda e, i=i: e.tensor_tensor(
            ctx_st[i % 2][:, :], ps_ctx[0][:, :], linv_sb[i % 2][:, :], OP.mult),
            deps=[ev_linv[i], ev_ptuse[(i, nkc - 1)],
                  (t_ctxst[i - 2] if i >= 2 else None)], tag=f"norm{i}")
        t_ctxst.append(r.add("sync", lambda e, hh=hh, qt=qt, i=i: e.dma_start(
            out=ctx_loc.ap()[hh * 128:(hh + 1) * 128, qt * TBW:(qt + 1) * TBW],
            in_=ctx_st[i % 2][:, :]),
            dma=True, deps=[ev_norm[i]], tag=f"ctxst{i}"))
        yield

    SENT = object()
    for i in range(len(seq)):
        ga = pass_A(i)
        gb = pass_B(i - 1) if i > 0 else None
        a_done = False
        b_done = gb is None
        for _ in range(5):
            if not a_done:
                a_done = next(ga, SENT) is SENT
        while not (a_done and b_done):
            if not b_done:
                b_done = next(gb, SENT) is SENT
            if not a_done:
                a_done = next(ga, SENT) is SENT
    for _ in pass_B(len(seq) - 1):
        pass

    t_ag2 = r.add("gpsimd", lambda e: e.collective_compute(
        "AllGather", OP.bypass, replica_groups=[list(range(NC))],
        ins=[ctx_loc.ap().opt()], outs=[ctx_g.ap().opt()]),
        deps=t_ctxst, tag="ag2")

    pP3.close()
    p3.close()

    if phases < 3:
        t_fin = []
        for tch in range(TCH):
            t_fin.append(r.add("sync", lambda e, tch=tch: e.dma_start(
                out=out_loc.ap()[tch * 128:(tch + 1) * 128, :],
                in_=x_loc.ap()[tch * 128:(tch + 1) * 128, :]),
                dma=True, deps=[t_ag2], tag=f"dummyout{tch}"))
        r.join("sync", t_fin, tag="final")
        r.finalize_and_emit(nc, sems)
        es.close()
        return nc

    p2a = ExitStack()
    acc = p2a.enter_context(nc.sbuf_tensor("acc", [128, TCH * D], F32))
    p2 = ExitStack()
    A2 = p2.enter_context
    y_sb = A2(nc.sbuf_tensor("y_sb", [128, GE * FC * NTB * TBW], BF16))
    wg_sb = [A2(nc.sbuf_tensor(f"wg_sb{i}", [128, DC * 128], BF16)) for i in range(2)]
    w1s_sb = [A2(nc.sbuf_tensor(f"w1s_sb{i}", [128, DC * 128], BF16)) for i in range(2)]
    w2b_sb = [A2(nc.sbuf_tensor(f"w2b_sb{i}", [128, GE * FC * DQW], BF16)) for i in range(2)]
    gel_sb = [A2(nc.sbuf_tensor(f"gel_sb{i}", [128, TBW], F32)) for i in range(2)]
    cb_sb = [A2(nc.sbuf_tensor(f"cb_sb{i}", [128, TBW], F32)) for i in range(2)]
    yt_sb = [A2(nc.sbuf_tensor(f"yt_sb{i}", [128, TBW], F32)) for i in range(2)]

    pP2 = ExitStack()
    ps_a = [pP2.enter_context(nc.psum_tensor(f"ps_a{i}", [128, TBW], F32)) for i in range(2)]
    ps_b = [pP2.enter_context(nc.psum_tensor(f"ps_b{i}", [128, TBW], F32)) for i in range(2)]
    ps_cb = [pP2.enter_context(nc.psum_tensor(f"ps_cb{i}", [128, TBW], F32)) for i in range(2)]
    ps_o = [pP2.enter_context(nc.psum_tensor(f"ps_o{i}", [128, DQW], F32)) for i in range(2)]

    gate_p2 = r.join("vector", [ev_norm[len(seq) - 1], ev_lastB[len(seq) - 1],
                                t_ctxst[-1], t_ctxst[-2]], tag="p3done")

    ev_gel = {}
    ev_yt = {}
    ev_y = {}
    ev_yfc = {}
    t_cb = {}
    ev_spill = {}
    t_spill_j = {}
    ev_bmmw = {}
    cnt_a = 0
    sb_cnt = 0
    for g in range(NG):
        exps = [g * GE + i for i in range(GE)]
        for esl, e_ in enumerate(exps):
            routed = e_ < E
            for fc in range(FC):
                wi = (g * GE + esl) * FC + fc
                t_wg = r.add("gpsimd", lambda e, e_=e_, fc=fc, wi=wi: e.dma_start(
                    out=wg_sb[wi % 2][:],
                    in_=g_all.ap()[(e_ * FC + fc) * 128:(e_ * FC + fc + 1) * 128, :]),
                    dma=True, deps=[gate_p2, ev_yfc.get((wi - 2, NTB - 1))], tag=f"wg{wi}")
                t_w1s = r.add("gpsimd", lambda e, e_=e_, fc=fc, wi=wi: e.dma_start(
                    out=w1s_sb[wi % 2][:],
                    in_=w1_all.ap()[(e_ * FC + fc) * 128:(e_ * FC + fc + 1) * 128, :]),
                    dma=True, deps=[gate_p2, ev_yfc.get((wi - 2, NTB - 1))], tag=f"w1{wi}")
                for tb in range(NTB):
                    if routed and fc == 0:
                        t_cbm = r.add("tensor", lambda e, e_=e_, tb=tb: e.matmul(
                            ps_cb[tb][:, :], onehot[:, e_ * 128:(e_ + 1) * 128],
                            comb_b[:, tb * TBW:(tb + 1) * TBW], start=True, stop=True),
                            deps=[t_comb_j, t_oh_j,
                                  t_cb.get((e_ - 1, tb))],
                            tag=f"cb{e_}_{tb}")
                        t_cb[(e_, tb)] = r.add("vector", lambda e, tb=tb: e.tensor_copy(
                            cb_sb[tb][:, :], ps_cb[tb][:, :]),
                            deps=[t_cbm, ev_yt.get((e_ - 1, FC - 1, tb))],
                            tag=f"cbe{e_}_{tb}")
                    cnt_a += 1
                    ppa = cnt_a % 2
                    mt = None
                    for dc in range(DC):
                        mt = r.add("tensor", lambda e, wi=wi, dc=dc, tb=tb, ppa=ppa: e.matmul(
                            ps_a[ppa][:, :], wg_sb[wi % 2][:, dc * 128:(dc + 1) * 128],
                            xn_fm[:, dc * T + tb * TBW: dc * T + (tb + 1) * TBW],
                            start=(dc == 0), stop=(dc == DC - 1)),
                            deps=[t_wg, ev_gel.get(cnt_a - 2)], tag=f"ffA{wi}_{tb}_{dc}",
                            group=("ffa", wi, tb))
                    mtb = None
                    for dc in range(DC):
                        mtb = r.add("tensor", lambda e, wi=wi, dc=dc, tb=tb, ppa=ppa: e.matmul(
                            ps_b[ppa][:, :], w1s_sb[wi % 2][:, dc * 128:(dc + 1) * 128],
                            xn_fm[:, dc * T + tb * TBW: dc * T + (tb + 1) * TBW],
                            start=(dc == 0), stop=(dc == DC - 1)),
                            deps=[t_w1s, ev_yt.get(("b", cnt_a - 2))], tag=f"ffB{wi}_{tb}_{dc}",
                            group=("ffb", wi, tb))
                    bidx = fc * EXP + e_
                    ev_gel[cnt_a] = r.add("scalar", lambda e, bidx=bidx, ppa=ppa: e.activation(
                        gel_sb[ppa][:, :], ps_a[ppa][:, :], ACTF.Gelu,
                        bias=gbT[:, bidx:bidx + 1]),
                        deps=[mt, t_bias_j, ev_y.get(cnt_a - 2)], tag=f"gel{wi}_{tb}")
                    slot = ((esl * FC + fc) * NTB + tb) * TBW
                    warY = ev_bmmw.get(("g", g - 1))
                    if routed:
                        tyt = r.add("vector", lambda e, bidx=bidx, ppa=ppa, tb=tb: e.scalar_tensor_tensor(
                            yt_sb[ppa][:, :], ps_b[ppa][:, :], b1T[:, bidx:bidx + 1],
                            cb_sb[tb][:, :], OP.add, OP.mult),
                            deps=[mtb, t_cb[(e_, tb)], t_bias_j, ev_y.get(cnt_a - 2)],
                            tag=f"yt{wi}_{tb}")
                        ev_yt[("b", cnt_a)] = tyt
                        ev_yt[(e_, fc, tb)] = tyt
                        ev_y[cnt_a] = r.add("vector", lambda e, slot=slot, ppa=ppa: e.tensor_tensor(
                            y_sb[:, slot:slot + TBW], yt_sb[ppa][:, :], gel_sb[ppa][:, :], OP.mult),
                            deps=[tyt, ev_gel[cnt_a], warY], tag=f"y{wi}_{tb}")
                    else:
                        tyt = r.add("vector", lambda e, bidx=bidx, ppa=ppa, slot=slot: e.scalar_tensor_tensor(
                            y_sb[:, slot:slot + TBW], ps_b[ppa][:, :], b1T[:, bidx:bidx + 1],
                            gel_sb[ppa][:, :], OP.add, OP.mult),
                            deps=[mtb, ev_gel[cnt_a], t_bias_j, warY], tag=f"ysh{wi}_{tb}")
                        ev_yt[("b", cnt_a)] = tyt
                        ev_yt[(e_, fc, tb)] = tyt
                        ev_y[cnt_a] = tyt
                    ev_yfc[(wi, tb)] = ev_y[cnt_a]
        y_done = {tb: [ev_yfc[((g * GE + esl) * FC + fc, tb)]
                       for esl in range(GE) for fc in range(FC)] for tb in range(NTB)}
        bmms_all = []
        for dq in range(NDQ):
            w = g * NDQ + dq
            dmas = []
            for esl, e_ in enumerate(exps):
                dmas.append(r.add("gpsimd", lambda e, e_=e_, dq=dq, w=w, esl=esl: e.dma_start(
                    out=w2b_sb[w % 2][:, esl * FC * DQW:(esl + 1) * FC * DQW],
                    in_=w2_all.ap()[(e_ * NDQ + dq) * 128:(e_ * NDQ + dq + 1) * 128, :]),
                    dma=True, deps=[gate_p2, ev_bmmw.get(w - 2)], tag=f"w2{w}_{esl}"))
            t_w2b = r.join("gpsimd", dmas, tag=f"w2j{w}")
            bmms = []
            for tb in range(NTB):
                for tc in range(QS):
                    oi = (tb * QS + tc) * NDQ + dq
                    sb_cnt += 1
                    po = sb_cnt % 2
                    warO = ev_spill.get(sb_cnt - 2)
                    if warO is None and g > 0:
                        warO = t_spill_j[g - 1]
                    mt = None
                    nmm = GE * FC
                    im = 0
                    for esl in range(GE):
                        for fc in range(FC):
                            slot = ((esl * FC + fc) * NTB + tb) * TBW + tc * 128
                            mt = r.add("tensor", lambda e, slot=slot, esl=esl, fc=fc, po=po, im=im, nmm=nmm, w=w: e.matmul(
                                ps_o[po][:, :], y_sb[:, slot:slot + 128],
                                w2b_sb[w % 2][:, (esl * FC + fc) * DQW:(esl * FC + fc + 1) * DQW],
                                start=(im == 0), stop=(im == nmm - 1)),
                                deps=[t_w2b] + ([warO] + y_done[tb] if im == 0 else []),
                                tag=f"ffO{g}_{oi}_{im}", group=("ffo", g, sb_cnt))
                            im += 1
                    bmms.append(mt)
                    bmms_all.append(mt)
                    aslice = (tb * QS + tc) * D + dq * DQW
                    if g == 0:
                        ev_spill[sb_cnt] = r.add("vector", lambda e, aslice=aslice, po=po: e.tensor_copy(
                            acc[:, aslice:aslice + DQW], ps_o[po][:, :]),
                            deps=[mt], tag=f"sp{g}_{oi}")
                    else:
                        ev_spill[sb_cnt] = r.add("vector", lambda e, aslice=aslice, po=po: e.tensor_tensor(
                            acc[:, aslice:aslice + DQW], ps_o[po][:, :],
                            acc[:, aslice:aslice + DQW], OP.add),
                            deps=[mt, ev_spill.get(sb_cnt - TCH * NDQ)], tag=f"sp{g}_{oi}")
            ev_bmmw[w] = r.join("tensor", bmms, tag=f"bmmw{w}")
        ev_bmmw[("g", g)] = ev_bmmw[g * NDQ + NDQ - 1]
        t_spill_j[g] = r.join("vector", [ev_spill[sb_cnt - i] for i in range(TCH * NDQ)],
                              tag=f"spj{g}")

    t_ffn_j = t_spill_j[NG - 1]
    if debug:
        t_dbg_acc = r.add("sync", lambda e: e.dma_start(out=dbg_acc.ap(), in_=acc[:, :]),
                          dma=True, deps=[t_ffn_j], tag="dbgacc")
        t_ffn_j = r.join("vector", [t_ffn_j, t_dbg_acc], tag="ffnjd")
    pP2.close()
    p2.close()

    if phases < 4:
        t_fin = []
        for tch in range(TCH):
            t_fin.append(r.add("sync", lambda e, tch=tch: e.dma_start(
                out=out_loc.ap()[tch * 128:(tch + 1) * 128, :],
                in_=x_loc.ap()[tch * 128:(tch + 1) * 128, :]),
                dma=True, deps=[t_ffn_j], tag=f"dummyout{tch}"))
        r.join("sync", t_fin, tag="final")
        p2a.close()
        r.finalize_and_emit(nc, sems)
        es.close()
        return nc

    p4 = ExitStack()
    A4 = p4.enter_context
    CG = 4 if DC % 4 == 0 else 2
    ctx_gr = [A4(nc.sbuf_tensor(f"ctx_gr{i}", [128, CG * T], BF16)) for i in range(2)]
    wo_sb = [A4(nc.sbuf_tensor(f"wo_sb{i}", [128, CG * D], BF16)) for i in range(2)]
    x2_sb = [A4(nc.sbuf_tensor(f"x2_sb{i}", [128, D], F32)) for i in range(2)]
    xnT_sb = [A4(nc.sbuf_tensor(f"xnT_sb{i}", [128, D], BF16)) for i in range(2)]
    stg_sb = [A4(nc.sbuf_tensor(f"stg_sb{i}", [128, D], F32)) for i in range(2)]

    pP4 = ExitStack()
    ps_p = [pP4.enter_context(nc.psum_tensor(f"ps_p{i}", [128, DQW], F32)) for i in range(2)]
    ps_tx = [pP4.enter_context(nc.psum_tensor(f"ps_tx{i}", [128, 128], BF16)) for i in range(2)]

    gate_p4 = t_ffn_j

    ev_psp = {}
    pj_cnt = 0
    last_psp = {}
    for cg in range(DC // CG):
        ccs = [cg * CG + i for i in range(CG)]
        lds = [r.add("sync", lambda e, cg=cg: e.dma_start(
            out=bass.AP(ctx_gr[cg % 2][:].tensor, ctx_gr[cg % 2][:].offset,
                        [[CG * T, 128], [T, CG], [1, T]]),
            in_=dap(ctx_g, mcell["ctx"] + cg * CG * 128 * S,
                    [[S, 128], [128 * S, CG], [1, T]])),
            dma=True, deps=[t_ag2, gate_p4, last_psp.get(cg - 2)],
            tag=f"ctxg{cg}")]
        for cc in ccs:
            lds.append(r.add("gpsimd", lambda e, cc=cc, cg=cg: e.dma_start(
                out=wo_sb[cg % 2][:, (cc % CG) * D:(cc % CG + 1) * D],
                in_=wout_r.ap()[cc * 128:(cc + 1) * 128, :]),
                dma=True, deps=[gate_p4, last_psp.get(cg - 2)],
                tag=f"wog{cc}"))
        t_ldj = r.join("sync", lds, tag=f"ldj{cg}")
        for tch in range(TCH):
            for dq in range(NDQ):
                pj_cnt += 1
                pi = pj_cnt % 2
                mt = None
                for i, cc in enumerate(ccs):
                    mt = r.add("tensor", lambda e, cc=cc, i=i, tch=tch, dq=dq, pi=pi, cg=cg: e.matmul(
                        ps_p[pi][:, :],
                        ctx_gr[cg % 2][:, (cc % CG) * T + tch * 128:(cc % CG) * T + (tch + 1) * 128],
                        wo_sb[cg % 2][:, (cc % CG) * D + dq * DQW:(cc % CG) * D + (dq + 1) * DQW],
                        start=(i == 0), stop=(i == CG - 1)),
                        deps=[t_ldj] + ([ev_psp.get(pj_cnt - 2)] if i == 0 else []),
                        tag=f"prj{cg}_{tch}_{dq}_{i}", group=("prj", pj_cnt))
                aslice = tch * D + dq * DQW
                ev_psp[pj_cnt] = r.add("vector", lambda e, aslice=aslice, pi=pi: e.tensor_tensor(
                    acc[:, aslice:aslice + DQW], ps_p[pi][:, :],
                    acc[:, aslice:aslice + DQW], OP.add),
                    deps=[mt, t_ffn_j, ev_psp.get(pj_cnt - TCH * NDQ)], tag=f"prjsp{cg}_{tch}_{dq}")
        last_psp[cg] = r.join("vector", [ev_psp[pj_cnt - i] for i in range(TCH * NDQ)],
                              tag=f"pspj{cg}")

    t_proj_j = last_psp[DC // CG - 1]
    t_out = []
    ev_fin = {}
    ev_xnT = {}
    fin_cnt = 0
    tx_cnt = 0
    for tch in range(TCH):
        t_x2 = r.add("sync", lambda e, tch=tch: e.dma_start(
            out=x2_sb[tch % 2][:], in_=x_loc.ap()[tch * 128:(tch + 1) * 128, :]),
            dma=True, deps=[t_ag2, (t_out[tch - 2] if tch >= 2 else None)], tag=f"x2{tch}")
        xts = []
        for dc in range(DC):
            tx_cnt += 1
            pv = tx_cnt % 2
            t_tx = r.add("tensor", lambda e, dc=dc, tch=tch, pv=pv: e.transpose(
                ps_tx[pv][:, :], xn_fm[:, dc * T + tch * 128: dc * T + (tch + 1) * 128],
                id_bf[:, :]),
                deps=[t_idb, ev_xnT.get(tx_cnt - 2),
                      (t_out[tch - 2] if (tch >= 2 and dc < 2) else None),
                      (t_proj_j if (tch == 0 and dc < 2) else None)],
                tag=f"xtx{tch}_{dc}")
            ev_xnT[tx_cnt] = r.add("vector", lambda e, dc=dc, tch=tch, pv=pv: e.tensor_copy(
                xnT_sb[tch % 2][:, dc * 128:(dc + 1) * 128], ps_tx[pv][:, :]),
                deps=[t_tx, (t_out[tch - 2] if tch >= 2 else None)], tag=f"xte{tch}_{dc}")
            xts.append(ev_xnT[tx_cnt])
        for dq in range(NDQ):
            fin_cnt += 1
            pi = fin_cnt % 2
            m1 = r.add("tensor", lambda e, tch=tch, dq=dq, pi=pi: e.matmul(
                ps_p[pi][:, :], comb_b[:, tch * 128:(tch + 1) * 128],
                rb2_sb[:, dq * DQW:(dq + 1) * DQW], start=True, stop=False),
                deps=[t_comb_j, t_rb2, t_proj_j, ev_fin.get(fin_cnt - 2)],
                tag=f"fb2{tch}_{dq}", group=("fin", fin_cnt))
            m2 = r.add("tensor", lambda e, dq=dq, pi=pi: e.matmul(
                ps_p[pi][:, :], ones128b[0:SH, 0:128], sb2_sb[:, dq * DQW:(dq + 1) * DQW],
                start=False, stop=True),
                deps=[t_sb2, t_o128b], tag=f"fsb2{tch}_{dq}", group=("fin", fin_cnt))
            aslice = tch * D + dq * DQW
            f1 = r.add("vector", lambda e, aslice=aslice, pi=pi, tch=tch, dq=dq: e.tensor_tensor(
                stg_sb[tch % 2][:, dq * DQW:(dq + 1) * DQW], ps_p[pi][:, :],
                acc[:, aslice:aslice + DQW], OP.add),
                deps=[m2, t_proj_j,
                      (t_out[tch - 2] if tch >= 2 else None)], tag=f"f1_{tch}_{dq}")
            f2 = r.add("vector", lambda e, tch=tch, dq=dq: e.tensor_tensor(
                stg_sb[tch % 2][:, dq * DQW:(dq + 1) * DQW],
                stg_sb[tch % 2][:, dq * DQW:(dq + 1) * DQW],
                x2_sb[tch % 2][:, dq * DQW:(dq + 1) * DQW], OP.add),
                deps=[t_x2, f1])
            ev_fin[fin_cnt] = r.add("vector", lambda e, tch=tch, dq=dq: e.tensor_tensor(
                stg_sb[tch % 2][:, dq * DQW:(dq + 1) * DQW],
                stg_sb[tch % 2][:, dq * DQW:(dq + 1) * DQW],
                xnT_sb[tch % 2][:, dq * DQW:(dq + 1) * DQW], OP.add),
                deps=[f2] + xts, tag=f"fin{tch}_{dq}")
        t_out.append(r.add("sync", lambda e, tch=tch: e.dma_start(
            out=out_loc.ap()[tch * 128:(tch + 1) * 128, :], in_=stg_sb[tch % 2][:, :]),
            dma=True, deps=[ev_fin[fin_cnt - i] for i in range(NDQ)], tag=f"out{tch}"))

    r.join("sync", t_out, tag="final")

    pP4.close()
    p4.close()
    p2a.close()

    r.finalize_and_emit(nc, sems)
    es.close()
    return nc


def prep_inputs(cfg, inp):
    cf = derive(cfg)
    D, S, F, E, SH = cf["D"], cf["S"], cf["F"], cf["E"], cf["SH"]
    NC, DC, FC = cf["NC"], cf["DC"], cf["FC"]
    T, DQW, NDQ, EXP = cf["T"], cf["DQW"], cf["NDQ"], cf["EXP"]

    f32 = lambda a: np.ascontiguousarray(np.asarray(a, dtype=np.float32))
    x = f32(inp["L0_attn_x"]).reshape(S, D)
    Wqkv = f32(inp["L0_attn_Wqkv"])
    outW = f32(inp["L0_attn_out_W"])
    wa = f32(inp["L0_ffn_wa"])
    gq = f32(inp["L0_attn_q_ln_scale"])
    gk = f32(inp["L0_attn_k_ln_scale"])
    rg, rgb = f32(inp["L0_ffn_rg"]), f32(inp["L0_ffn_rgb"])
    rw1, rb1 = f32(inp["L0_ffn_rw1"]), f32(inp["L0_ffn_rb1"])
    rw2, rb2 = f32(inp["L0_ffn_rw2"]), f32(inp["L0_ffn_rb2"])
    sg, sgb = f32(inp["L0_ffn_sg"]), f32(inp["L0_ffn_sgb"])
    sw1, sb1 = f32(inp["L0_ffn_sw1"]), f32(inp["L0_ffn_sb1"])
    sw2, sb2 = f32(inp["L0_ffn_sw2"]), f32(inp["L0_ffn_sb2"])

    def qk_layout(w):
        return np.ascontiguousarray(
            w.reshape(DC, 128, DC, 128).transpose(2, 1, 0, 3).reshape(D, DC * 128))

    wqkv_r = np.ascontiguousarray(np.concatenate(
        [qk_layout(Wqkv[:, 0:D]), qk_layout(Wqkv[:, D:2 * D]), qk_layout(Wqkv[:, 2 * D:3 * D])],
        axis=0))
    wa_r = np.ascontiguousarray(wa.reshape(DC, 128, E).transpose(1, 0, 2).reshape(128, DC * E))
    gq_r = np.ascontiguousarray(gq.reshape(DC, 128).T)
    gk_r = np.ascontiguousarray(gk.reshape(DC, 128).T)

    gmat = np.concatenate([rg, sg], axis=0)
    w1mat = np.concatenate([rw1, sw1], axis=0)
    w2mat = np.concatenate([rw2, sw2], axis=0)
    g_allr = np.ascontiguousarray(
        gmat.reshape(EXP, DC, 128, FC, 128).transpose(0, 3, 2, 1, 4).reshape(EXP * FC * 128, DC * 128))
    w1_allr = np.ascontiguousarray(
        w1mat.reshape(EXP, DC, 128, FC, 128).transpose(0, 3, 2, 1, 4).reshape(EXP * FC * 128, DC * 128))
    w2_allr = np.ascontiguousarray(
        w2mat.reshape(EXP, FC, 128, NDQ, DQW).transpose(0, 3, 2, 1, 4).reshape(EXP * NDQ * 128, FC * DQW))
    gb_allr = np.ascontiguousarray(np.concatenate([rgb, sgb], axis=0))
    b1_allr = np.ascontiguousarray(np.concatenate([rb1, sb1], axis=0))

    shared = dict(wqkv_r=wqkv_r, wout_r=outW, wa_r=wa_r, gq_r=gq_r, gk_r=gk_r,
                  g_all=g_allr, w1_all=w1_allr, w2_all=w2_allr,
                  gb_all=gb_allr, b1_all=b1_allr, rb2_r=np.ascontiguousarray(rb2),
                  sb2_r=np.ascontiguousarray(sb2))
    in_maps = []
    for c in range(NC):
        m = dict(shared)
        m["x_loc"] = np.ascontiguousarray(x[c * T:(c + 1) * T])
        m["meta"] = np.array([[c, 0, 0, 0, 0, 0, 0, 0]], np.int32)
        in_maps.append(m)
    return in_maps


def kernel(**inputs):
    cfg = REAL_CFG
    cf = derive(cfg)
    nc = build_nc(cfg)
    in_maps = prep_inputs(cfg, inputs)
    res = run_bass_kernel_spmd(nc, in_maps, core_ids=list(range(cfg["NC"])))
    out = np.concatenate([res.results[i]["out_loc"] for i in range(cfg["NC"])], axis=0)
    return out.reshape(1, cfg["S"], cfg["D"]).astype(np.float32)


# version 17
# speedup vs baseline: 1.2088x; 1.0101x over previous
import os
import numpy as np
from itertools import zip_longest
from contextlib import ExitStack

import concourse.bass as bass
import concourse.mybir as mybir
from concourse.bass_utils import run_bass_kernel_spmd

F32 = mybir.dt.float32
BF16 = mybir.dt.bfloat16
I32 = mybir.dt.int32
AX = mybir.AxisListType.X
OP = mybir.AluOpType
ACTF = mybir.ActivationFunctionType

NEG_BIG = -1.0e30

REAL_CFG = dict(D=2048, S=8192, F=1024, E=16, SH=2, NC=8)


def derive(cfg):
    c = dict(cfg)
    c["H"] = c["D"] // 128
    assert c["H"] == 2 * c["NC"], "one head-pair per core"
    c["DC"] = c["D"] // 128
    c["FC"] = c["F"] // 128
    c["NT"] = 2 * c["NC"]
    c["TS"] = c["S"] // c["NT"]
    assert c["TS"] % 128 == 0 and c["TS"] <= 512
    c["T"] = 2 * c["TS"]
    c["TCH"] = c["T"] // 128
    c["TBW"] = c["TS"]
    c["QS"] = c["TBW"] // 128
    c["EXP"] = c["E"] + c["SH"]
    c["GE"] = 2
    assert c["EXP"] % c["GE"] == 0
    c["NG"] = c["EXP"] // c["GE"]
    c["DQW"] = min(512, c["D"])
    c["NDQ"] = c["D"] // c["DQW"]
    return c


ENGS = ("tensor", "vector", "scalar", "gpsimd", "sync")
FIFO_ENGS = set()


NCHAN = 8


class Tok:
    __slots__ = ("eng", "idx", "dma", "val", "chan")

    def __init__(self, eng, idx, dma, chan):
        self.eng = eng
        self.idx = idx
        self.dma = dma
        self.chan = chan
        self.val = None


class Rec:
    def __init__(self):
        self.ops = {e: [] for e in ENGS}
        self.used = set()
        self.dma_cnt = {e: 0 for e in ENGS}
        self.last_dma = {}

    def add(self, eng, fn, deps=(), dma=False, tag="", group=None):
        deps = [d for d in deps if d is not None]
        chan = None
        if dma:
            chan = self.dma_cnt[eng] % NCHAN
            self.dma_cnt[eng] += 1
            prev = self.last_dma.get((eng, chan))
            if prev is not None:
                deps = deps + [prev]
            self.last_dma[(eng, chan)] = None
        t = Tok(eng, len(self.ops[eng]), dma, chan)
        if dma:
            self.last_dma[(eng, chan)] = t
        self.ops[eng].append((fn, deps, t, tag, group))
        return t

    def join(self, eng, deps, tag="join"):
        return self.add(eng, lambda e: e.nop(nofuse=True), deps=deps, tag=tag)

    def finalize_and_emit(self, nc, sems):
        for eng in ENGS:
            cc = 0
            cd = {}
            ops = self.ops[eng]
            n = len(ops)
            i = 0
            while i < n:
                fn, deps, t, tag, group = ops[i]
                if t.dma:
                    cd[t.chan] = cd.get(t.chan, 0) + 16
                    t.val = cd[t.chan]
                    i += 1
                    continue
                j = i
                if group is not None:
                    while j + 1 < n and (not ops[j + 1][2].dma) and ops[j + 1][4] == group:
                        j += 1
                cc += 1
                for k in range(i, j + 1):
                    ops[k][2].val = cc
                i = j + 1
        rec = self

        def skey(d):
            return (d.eng, d.chan) if d.dma else (d.eng, None)

        def emit_for(eng_name):
            def closure(eng):
                waited = {}
                chain = 0
                ops = rec.ops[eng_name]
                n = len(ops)
                i = 0
                while i < n:
                    fn, deps, t, tag, group = ops[i]
                    j = i
                    if (not t.dma) and group is not None:
                        while j + 1 < n and (not ops[j + 1][2].dma) and ops[j + 1][4] == group:
                            j += 1
                    for k in range(i, j + 1):
                        fnk, depsk, tk, tagk, _g = ops[k]
                        need = {}
                        for d in depsk:
                            if d.val is None:
                                raise RuntimeError(f"unvalued dep used by {eng_name}:{tagk}")
                            kk = skey(d)
                            if need.get(kk, 0) < d.val:
                                need[kk] = d.val
                        attach = None
                        if not tk.dma:
                            need.pop((eng_name, None), None)
                            cpos = j if eng_name == "tensor" else i
                            if k == cpos and chain > 0:
                                attach = ((eng_name, None), chain)
                        else:
                            ck = (eng_name, tk.chan)
                            if ck in need:
                                attach = (ck, need.pop(ck))
                        stand = []
                        for kk, v in sorted(need.items(), key=lambda kv: str(kv[0])):
                            if waited.get(kk, 0) < v:
                                stand.append((kk, v))
                        if attach is None and stand:
                            attach = stand.pop()
                        for kk, v in stand:
                            eng.wait_ge(sems[kk], v)
                            waited[kk] = v
                        inst = fnk(eng)
                        if inst is None:
                            if attach is not None:
                                kk, v = attach
                                eng.wait_ge(sems[kk], v)
                                waited[kk] = max(waited.get(kk, 0), v)
                            continue
                        if attach is not None:
                            kk, v = attach
                            inst._wait_ge(sems[kk], v)
                            if waited.get(kk, 0) < v:
                                waited[kk] = v
                        if tk.dma:
                            inst.then_inc(sems[(eng_name, tk.chan)], 16)
                        elif k == j:
                            chain += 1
                            inst.then_inc(sems[(eng_name, None)], 1)
                    i = j + 1
            return closure

        with nc.Block() as blk:
            blk.tensor(emit_for("tensor"))
            blk.vector(emit_for("vector"))
            blk.scalar(emit_for("scalar"))
            blk.gpsimd(emit_for("gpsimd"))
            blk.sync(emit_for("sync"))


def build_nc(cfg, debug=False, phases=4):
    cf = derive(cfg)
    D, S, F, E, SH = cf["D"], cf["S"], cf["F"], cf["E"], cf["SH"]
    NC, H, DC, FC = cf["NC"], cf["H"], cf["DC"], cf["FC"]
    TS, T, TCH = cf["TS"], cf["T"], cf["TCH"]
    TBW, QS, EXP, GE, NG = cf["TBW"], cf["QS"], cf["EXP"], cf["GE"], cf["NG"]
    DQW, NDQ = cf["DQW"], cf["NDQ"]
    NTB = 2

    nc = bass.Bass("TRN2", target_bir_lowering=False, debug=debug)

    x_loc = nc.dram_tensor("x_loc", [T, D], F32, kind="ExternalInput")
    meta = nc.dram_tensor("meta", [1, 8], I32, kind="ExternalInput")
    wqkv_r = nc.dram_tensor("wqkv_r", [3 * D, DC * 128], F32, kind="ExternalInput")
    wout_r = nc.dram_tensor("wout_r", [D, D], F32, kind="ExternalInput")
    wa_r = nc.dram_tensor("wa_r", [128, DC * E], F32, kind="ExternalInput")
    gq_r = nc.dram_tensor("gq_r", [128, DC], F32, kind="ExternalInput")
    gk_r = nc.dram_tensor("gk_r", [128, DC], F32, kind="ExternalInput")
    g_all = nc.dram_tensor("g_all", [EXP * FC * 128, DC * 128], F32, kind="ExternalInput")
    w1_all = nc.dram_tensor("w1_all", [EXP * FC * 128, DC * 128], F32, kind="ExternalInput")
    w2_all = nc.dram_tensor("w2_all", [EXP * NDQ * 128, FC * DQW], F32, kind="ExternalInput")
    gb_all = nc.dram_tensor("gb_all", [EXP, F], F32, kind="ExternalInput")
    b1_all = nc.dram_tensor("b1_all", [EXP, F], F32, kind="ExternalInput")
    rb2_r = nc.dram_tensor("rb2_r", [E, D], F32, kind="ExternalInput")
    sb2_r = nc.dram_tensor("sb2_r", [SH, D], F32, kind="ExternalInput")
    out_loc = nc.dram_tensor("out_loc", [T, D], F32, kind="ExternalOutput")

    qkv_loc = nc.dram_tensor("qkv_loc", [3 * D, T], BF16)
    q_g = nc.dram_tensor("q_g", [NC * D, T], BF16,
                         addr_space="Shared" if NC > 4 else "Local")
    k_g = nc.dram_tensor("k_g", [NC * D, T], BF16,
                         addr_space="Shared" if NC > 4 else "Local")
    v_g = nc.dram_tensor("v_g", [NC * D, T], BF16,
                         addr_space="Shared" if NC > 4 else "Local")
    ctx_loc = nc.dram_tensor("ctx_loc", [256, S], BF16)
    ctx_g = nc.dram_tensor("ctx_g", [NC * 256, S], BF16,
                           addr_space="Shared" if NC > 4 else "Local")

    es = ExitStack()
    A = es.enter_context
    r = Rec()
    sems = {}
    for e in ENGS:
        sems[(e, None)] = A(nc.semaphore(f"sc_{e}"))
        for ch in range(NCHAN):
            sems[(e, ch)] = A(nc.semaphore(f"sd_{e}{ch}"))

    def dap(t, off, dims):
        return bass.AP(t.ap().tensor, off, [[int(a), int(b)] for a, b in dims])

    xn_fm = A(nc.sbuf_tensor("xn_fm", [128, DC * T], BF16))
    comb_b = A(nc.sbuf_tensor("comb_b", [E, T], BF16))
    gq2_sb = A(nc.sbuf_tensor("gq2_sb", [128, DC], F32))
    gk_sb = A(nc.sbuf_tensor("gk_sb", [128, DC], F32))
    gbT = A(nc.sbuf_tensor("gbT", [128, FC * EXP], F32))
    b1T = A(nc.sbuf_tensor("b1T", [128, FC * EXP], F32))
    rb2_sb = A(nc.sbuf_tensor("rb2_sb", [E, D], BF16))
    sb2_sb = A(nc.sbuf_tensor("sb2_sb", [SH, D], BF16))
    id_bf = A(nc.sbuf_tensor("id_bf", [128, 128], BF16))
    id_f32 = A(nc.sbuf_tensor("id_f32", [128, 128], F32))
    ones128f = A(nc.sbuf_tensor("ones128f", [128, 128], F32))
    ones128b = A(nc.sbuf_tensor("ones128b", [128, 128], BF16))
    ones_c = A(nc.sbuf_tensor("ones_c", [128, 1], BF16))
    ones_rb = A(nc.sbuf_tensor("ones_rb", [1, 128], BF16))
    ones_rf = A(nc.sbuf_tensor("ones_rf", [1, 128], F32))
    onehot = A(nc.sbuf_tensor("onehot", [E, E * 128], BF16))
    wa_sb = A(nc.sbuf_tensor("wa_sb", [128, DC * E], F32))
    meta_sb = A(nc.sbuf_tensor("meta_sb", [1, 8], I32))

    mcell = {}

    if debug:
        dbg_xn = nc.dram_tensor("dbg_xn", [128, DC * T], BF16, kind="ExternalOutput")
        dbg_comb = nc.dram_tensor("dbg_comb", [E, T], BF16, kind="ExternalOutput")
        dbg_acc = nc.dram_tensor("dbg_acc", [128, TCH * D], F32, kind="ExternalOutput")

    t_meta = r.add("sync", lambda e: e.dma_start(out=meta_sb[:], in_=meta.ap()), dma=True, tag="meta")

    r.join("sync", [t_meta], tag="metaj")

    def loadv(e):
        c = e.value_load(meta_sb[0:1, 0:1])
        mcell["c"] = c
        mcell["qk"] = e.snap(c * (256 * T))
        mcell["v"] = e.snap(c * (TCH * 128 * 256))
        mcell["ctx"] = e.snap(c * T)
        return e.nop(nofuse=True)
    r.add("sync", loadv, tag="loadv")

    t_gq = r.add("sync", lambda e: e.dma_start(out=gq2_sb[:], in_=gq_r.ap()), dma=True)
    t_gk = r.add("sync", lambda e: e.dma_start(out=gk_sb[:], in_=gk_r.ap()), dma=True)
    t_wa = r.add("sync", lambda e: e.dma_start(out=wa_sb[:], in_=wa_r.ap()), dma=True)
    t_rb2 = r.add("gpsimd", lambda e: e.dma_start(out=rb2_sb[:], in_=rb2_r.ap()), dma=True)
    t_sb2 = r.add("gpsimd", lambda e: e.dma_start(out=sb2_sb[:], in_=sb2_r.ap()), dma=True)

    t_o128f = r.add("gpsimd", lambda e: e.memset(ones128f[:], 1.0))
    t_o128b = r.add("gpsimd", lambda e: e.memset(ones128b[:], 1.0))
    t_onc = r.add("gpsimd", lambda e: e.memset(ones_c[:], 1.0))
    t_onrb = r.add("gpsimd", lambda e: e.memset(ones_rb[:], 1.0))
    t_onrf = r.add("gpsimd", lambda e: e.memset(ones_rf[:], 1.0))
    t_oh0 = r.add("gpsimd", lambda e: e.memset(onehot[:], 1.0))
    t_idb = r.add("gpsimd", lambda e: e.affine_select(
        out=id_bf[:], in_=ones128b[:], compare_op=OP.is_equal, fill=0.0,
        base=0, pattern=[[-1, 128]], channel_multiplier=1), deps=[t_o128b])
    t_idf = r.add("gpsimd", lambda e: e.affine_select(
        out=id_f32[:], in_=ones128f[:], compare_op=OP.is_equal, fill=0.0,
        base=0, pattern=[[-1, 128]], channel_multiplier=1), deps=[t_o128f])
    t_oh = []
    for ee in range(E):
        t_oh.append(r.add("gpsimd", lambda e, ee=ee: e.affine_select(
            out=onehot[:, ee * 128:(ee + 1) * 128], in_=onehot[:, ee * 128:(ee + 1) * 128],
            compare_op=OP.is_equal, fill=0.0,
            base=-ee, pattern=[[0, 128]], channel_multiplier=1), deps=[t_oh0]))
    t_oh_j = r.join("gpsimd", t_oh, tag="ohj")

    t_gq2 = r.add("vector", lambda e: e.tensor_scalar_mul(
        gq2_sb[:], gq2_sb[:], float(1.0 / np.sqrt(128.0))), deps=[t_gq])

    p1 = ExitStack()
    A1 = p1.enter_context
    bias_raw = A1(nc.sbuf_tensor("bias_raw", [EXP, 2 * F], F32))
    x_sb = [A1(nc.sbuf_tensor(f"x_sb{i}", [128, D], F32)) for i in range(2)]
    xnf = A1(nc.sbuf_tensor("xnf", [128, D], F32))
    sqf = A1(nc.sbuf_tensor("sqf", [128, D], F32))
    xnT_f = [A1(nc.sbuf_tensor(f"xnT_f{i}", [128, 128], F32)) for i in range(2)]
    stat_c = A1(nc.sbuf_tensor("stat_c", [128, 8], F32))
    sg_sb = A1(nc.sbuf_tensor("sg_sb", [128, E], F32))
    tp_sc = A1(nc.sbuf_tensor("tp_sc", [128, 2 * E], F32))
    qraw = A1(nc.sbuf_tensor("qraw", [128, DC * T], BF16))
    kraw = A1(nc.sbuf_tensor("kraw", [128, DC * T], BF16))
    sq_bf = A1(nc.sbuf_tensor("sq_bf", [128, T], BF16))
    wqk_sb = [A1(nc.sbuf_tensor(f"wqk_sb{i}", [128, DC * 128], BF16)) for i in range(2)]
    vtmp = [A1(nc.sbuf_tensor(f"vtmp{i}", [128, TBW], BF16)) for i in range(2)]
    vt_st = [A1(nc.sbuf_tensor(f"vt_st{i}", [128, QS * 128], BF16)) for i in range(2)]
    qn_sc = [A1(nc.sbuf_tensor(f"qn_sc{i}", [128, TBW], F32)) for i in range(2)]
    qn_st = [A1(nc.sbuf_tensor(f"qn_st{i}", [128, TBW], BF16)) for i in range(2)]
    rows = A1(nc.sbuf_tensor("rows", [1, 3 * T], F32))

    t_biasg = r.add("sync", lambda e: e.dma_start(out=bias_raw[:, 0:F], in_=gb_all.ap()), dma=True)
    t_biasb = r.add("sync", lambda e: e.dma_start(out=bias_raw[:, F:2 * F], in_=b1_all.ap()), dma=True)

    pA = ExitStack()
    ps_tr = [pA.enter_context(nc.psum_tensor(f"ps_tr{i}", [128, 512], F32)) for i in range(2)]
    ps_aff = [pA.enter_context(nc.psum_tensor(f"ps_aff{i}", [128, 512], F32)) for i in range(2)]

    ev_bias = {}
    for j, (soff, dst) in enumerate([(0, gbT), (F, b1T)]):
        for fc in range(FC):
            i = j * FC + fc
            tt_ = r.add("tensor", lambda e, soff=soff, fc=fc, i=i: e.transpose(
                ps_tr[i % 2][:, 0:EXP],
                bias_raw[0:EXP, soff + fc * 128: soff + (fc + 1) * 128],
                id_f32[0:EXP, 0:EXP]),
                deps=[t_biasg, t_biasb, t_idf, ev_bias.get(i - 2)], tag=f"biasT{i}")
            ev_bias[i] = r.add("vector", lambda e, dst=dst, fc=fc, i=i: e.tensor_copy(
                dst[:, fc * EXP:(fc + 1) * EXP], ps_tr[i % 2][:, 0:EXP]),
                deps=[tt_], tag=f"biasE{i}")
    t_bias_j = r.join("vector", list(ev_bias.values()), tag="biasj")

    if phases < 0.18:
        t_fin = []
        for tch in range(TCH):
            t_fin.append(r.add("sync", lambda e, tch=tch: e.dma_start(
                out=out_loc.ap()[tch * 128:(tch + 1) * 128, :],
                in_=x_loc.ap()[tch * 128:(tch + 1) * 128, :]),
                dma=True, deps=[t_bias_j, t_oh_j, t_gq2, t_rb2, t_sb2, t_idb], tag=f"dummyout{tch}"))
        r.join("sync", t_fin, tag="final")
        r.finalize_and_emit(nc, sems)
        p1.close()
        es.close()
        return nc


    ev_xn = {}
    ev_comb = {}
    ev_combT = {}
    ev_tr_last = {}
    t_sg_tok = {}
    last_xn_apply = {}
    TCHrun = TCH if phases >= 0.22 else 1
    for tch in range(TCHrun):
        t_x = r.add("sync", lambda e, tch=tch: e.dma_start(
            out=x_sb[tch % 2][:], in_=x_loc.ap()[tch * 128:(tch + 1) * 128, :]),
            dma=True, deps=[last_xn_apply.get(tch - 2)], tag=f"x{tch}")
        t_sq = r.add("scalar", lambda e, tch=tch: e.activation(
            sqf[:], x_sb[tch % 2][:], ACTF.Square),
            deps=[t_x, last_xn_apply.get(tch - 1)], tag=f"sq{tch}")
        c1 = r.add("vector", lambda e, tch=tch: e.reduce_sum(
            stat_c[:, 0:1], x_sb[tch % 2][:], axis=AX), deps=[t_x])
        c2 = r.add("vector", lambda e: e.reduce_sum(stat_c[:, 1:2], sqf[:], axis=AX),
                   deps=[t_sq])
        c3 = r.add("vector", lambda e: e.tensor_scalar_mul(stat_c[:, 0:1], stat_c[:, 0:1], 1.0 / D), deps=[c1])
        c4 = r.add("vector", lambda e: e.tensor_scalar_mul(stat_c[:, 1:2], stat_c[:, 1:2], 1.0 / D), deps=[c2])
        c5 = r.add("vector", lambda e: e.tensor_tensor(stat_c[:, 2:3], stat_c[:, 0:1], stat_c[:, 0:1], OP.mult), deps=[c3])
        c6 = r.add("vector", lambda e: e.tensor_tensor(stat_c[:, 3:4], stat_c[:, 1:2], stat_c[:, 2:3], OP.subtract), deps=[c4, c5])
        t_var = r.add("vector", lambda e: e.tensor_scalar_add(stat_c[:, 3:4], stat_c[:, 3:4], 1e-5), deps=[c6])
        t_sqrt = r.add("scalar", lambda e: e.activation(stat_c[:, 4:5], stat_c[:, 3:4], ACTF.Sqrt),
                       deps=[t_var], tag=f"sqrt{tch}")
        t_rec = r.add("vector", lambda e: e.reciprocal(stat_c[:, 5:6], stat_c[:, 4:5]),
                      deps=[t_sqrt])
        last_xn_apply[tch] = r.add("vector", lambda e, tch=tch: e.tensor_scalar(
            xnf[:], x_sb[tch % 2][:], stat_c[:, 0:1], stat_c[:, 5:6],
            OP.subtract, OP.mult),
            deps=[t_rec, c3, ev_tr_last.get(tch - 1)], tag=f"xn{tch}")
        SKIP = os.environ.get("K_SKIP", "")
        if "ln_only" in SKIP:
            ev_combT[tch] = last_xn_apply[tch]
            continue
        ev_evf = {}
        last_router_mm = {}
        for dc in range(DC):
            war = [ev_xn.get((dc - 2, tch)), ev_evf.get(dc - 2)]
            if dc < 2:
                war.append(ev_combT.get(tch - 1))
                if tch == 0:
                    war.append(t_bias_j)
            t_tr = r.add("tensor", lambda e, dc=dc: e.transpose(
                ps_tr[dc % 2][:, 0:128], xnf[:, dc * 128:(dc + 1) * 128], id_f32[:, :]),
                deps=[last_xn_apply[tch], t_idf] + war, tag=f"tr{tch}_{dc}")
            ev_evf[dc] = r.add("vector", lambda e, dc=dc: e.tensor_copy(
                xnT_f[dc % 2][:], ps_tr[dc % 2][:, 0:128]),
                deps=[t_tr, last_router_mm.get(dc - 2)], tag=f"evf{tch}_{dc}")
            ev_xn[(dc, tch)] = r.add("scalar", lambda e, dc=dc, tch=tch: e.activation(
                xn_fm[:, dc * T + tch * 128: dc * T + (tch + 1) * 128],
                ps_tr[dc % 2][:, 0:128], ACTF.Copy),
                deps=[t_tr, ev_evf[dc]], tag=f"xnb{tch}_{dc}")
            if "no_aff" in SKIP:
                last_router_mm[dc] = ev_evf[dc]
            else:
                last_router_mm[dc] = r.add("tensor", lambda e, dc=dc, tch=tch: e.matmul(
                    ps_aff[tch % 2][:, 0:E], xnT_f[dc % 2][:], wa_sb[:, dc * E:(dc + 1) * E],
                    start=(dc == 0), stop=(dc == DC - 1)),
                    deps=[ev_evf[dc], t_wa, t_sg_tok.get(tch - 2)], tag=f"aff{tch}_{dc}")
        ev_tr_last[tch] = last_router_mm[DC - 1]
        if "no_router_tail" in SKIP:
            ev_combT[tch] = r.join("scalar", [ev_xn[(dc, tch)] for dc in range(DC)], tag=f"sk{tch}")
            t_sg_tok[tch] = ev_combT[tch]
            ev_comb[tch] = ev_combT[tch]
            continue
        t_sg = r.add("scalar", lambda e, tch=tch: e.activation(
            sg_sb[:], ps_aff[tch % 2][:, 0:E], ACTF.Sigmoid),
            deps=[last_router_mm[DC - 1], ev_comb.get(tch - 1)], tag=f"sig{tch}")
        t_sg_tok[tch] = t_sg
        u1 = r.add("vector", lambda e: e.reduce_max(stat_c[:, 6:7], sg_sb[:], axis=AX), deps=[t_sg])
        u2 = r.add("vector", lambda e: e.tensor_single_scalar(
            tp_sc[:, 0:E], sg_sb[:], stat_c[:, 6:7], OP.is_ge), deps=[u1])
        u3 = r.add("vector", lambda e: e.scalar_tensor_tensor(
            tp_sc[:, E:2 * E], tp_sc[:, 0:E], NEG_BIG, sg_sb[:], OP.mult, OP.add), deps=[u2])
        u4 = r.add("vector", lambda e: e.reduce_max(stat_c[:, 7:8], tp_sc[:, E:2 * E], axis=AX), deps=[u3])
        u5 = r.add("vector", lambda e: e.tensor_single_scalar(
            tp_sc[:, E:2 * E], tp_sc[:, E:2 * E], stat_c[:, 7:8], OP.is_ge), deps=[u4])
        u6 = r.add("vector", lambda e: e.tensor_tensor(
            tp_sc[:, 0:E], tp_sc[:, 0:E], tp_sc[:, E:2 * E], OP.add), deps=[u5])
        u7 = r.add("vector", lambda e: e.tensor_tensor(
            stat_c[:, 6:7], stat_c[:, 6:7], stat_c[:, 7:8], OP.add), deps=[u4, u6])
        t8 = r.add("vector", lambda e: e.reciprocal(stat_c[:, 7:8], stat_c[:, 6:7]), deps=[u7])
        ev_comb[tch] = r.add("vector", lambda e: e.scalar_tensor_tensor(
            tp_sc[:, E:2 * E], sg_sb[:], stat_c[:, 7:8], tp_sc[:, 0:E],
            OP.mult, OP.mult), deps=[t8, u6], tag=f"comb{tch}")
        if "no_combT" in SKIP:
            ev_combT[tch] = ev_comb[tch]
            continue
        t_ct = r.add("tensor", lambda e, tch=tch: e.transpose(
            ps_tr[tch % 2][0:E, 0:128], tp_sc[:, E:2 * E], id_f32[:, :]),
            deps=[ev_comb[tch],
                  ev_xn.get((DC - 2, tch)), ev_xn.get((DC - 1, tch)),
                  ev_evf.get(DC - 2), ev_evf.get(DC - 1)],
            tag=f"combT{tch}")
        ev_combT[tch] = r.add("scalar", lambda e, tch=tch: e.activation(
            comb_b[:, tch * 128:(tch + 1) * 128], ps_tr[tch % 2][0:E, 0:128], ACTF.Copy),
            deps=[t_ct], tag=f"combE{tch}")

    t_comb_j = r.join("scalar", [ev_combT[t] for t in range(TCHrun)], tag="combj")
    pA.close()

    if phases < 0.3:
        t_fin = []
        for tch in range(TCH):
            t_fin.append(r.add("sync", lambda e, tch=tch: e.dma_start(
                out=out_loc.ap()[tch * 128:(tch + 1) * 128, :],
                in_=x_loc.ap()[tch * 128:(tch + 1) * 128, :]),
                dma=True, deps=[t_comb_j], tag=f"dummyout{tch}"))
        r.join("sync", t_fin, tag="final")
        r.finalize_and_emit(nc, sems)
        p1.close()
        es.close()
        return nc


    pB = ExitStack()
    ps_qkv = [pB.enter_context(nc.psum_tensor(f"ps_qkv{i}", [128, TBW], F32)) for i in range(2)]
    ps_vtr = [pB.enter_context(nc.psum_tensor(f"ps_vtr{i}", [128, 128], BF16)) for i in range(2)]

    t_wqk = {}
    ev_epi = {}
    mm_last = {}
    t_vb = {}
    ev_vtr = {}
    vtr_cnt = 0
    for oc in range(3 * DC):
        t_wqk[oc] = r.add("gpsimd", lambda e, oc=oc: e.dma_start(
            out=wqk_sb[oc % 2][:], in_=wqkv_r.ap()[oc * 128:(oc + 1) * 128, :]),
            dma=True, deps=[mm_last.get((oc - 2, NTB - 1))], tag=f"wqkv{oc}")
        for tb in range(NTB):
            gi = oc * NTB + tb
            pp = gi % 2
            mt = None
            for dc in range(DC):
                deps = [t_wqk[oc], ev_epi.get(gi - 2)]
                if oc == 0:
                    deps += [ev_xn[(dc, tc2)] for tc2 in range(tb * QS, (tb + 1) * QS)]
                if gi < 2 and dc == 0:
                    deps += [t_comb_j, t_sg_tok[TCH - 1], ev_tr_last[TCH - 1],
                             ev_combT[TCH - 1]]
                mt = r.add("tensor", lambda e, oc=oc, tb=tb, dc=dc, pp=pp: e.matmul(
                    ps_qkv[pp][:, :], wqk_sb[oc % 2][:, dc * 128:(dc + 1) * 128],
                    xn_fm[:, dc * T + tb * TBW: dc * T + (tb + 1) * TBW],
                    start=(dc == 0), stop=(dc == DC - 1)),
                    deps=deps, tag=f"qkvmm{oc}_{tb}_{dc}", group=("qkv", oc, tb))
            mm_last[(oc, tb)] = mt
            if oc < 2 * DC:
                dst = qraw if oc < DC else kraw
                c0 = (oc % DC) * T + tb * TBW
                ev_epi[gi] = r.add("scalar", lambda e, dst=dst, c0=c0, pp=pp: e.activation(
                    dst[:, c0:c0 + TBW], ps_qkv[pp][:, :], ACTF.Copy),
                    deps=[mt], tag=f"qke{oc}_{tb}")
            else:
                vc = oc - 2 * DC
                t_ev = r.add("scalar", lambda e, pp=pp, tb=tb: e.activation(
                    vtmp[tb % 2][:, :], ps_qkv[pp][:, :], ACTF.Copy),
                    deps=[mt, t_vb.get((vc - 1, tb))], tag=f"vev{vc}_{tb}")
                ev_epi[gi] = t_ev
                for s in range(QS):
                    vtr_cnt += 1
                    vv = vtr_cnt % 2
                    t_vt = r.add("tensor", lambda e, s=s, tb=tb, vv=vv: e.transpose(
                        ps_vtr[vv][:, :], vtmp[tb % 2][:, s * 128:(s + 1) * 128], id_bf[:, :]),
                        deps=[t_ev, t_idb, ev_vtr.get(vtr_cnt - 2)] +
                             ([t_comb_j, t_sg_tok[TCH - 1]] if vtr_cnt <= 2 else []),
                        tag=f"vtr{vc}_{tb}_{s}")
                    ev_vtr[vtr_cnt] = r.add("vector", lambda e, s=s, tb=tb, vv=vv: e.tensor_copy(
                        vt_st[tb % 2][:, s * 128:(s + 1) * 128], ps_vtr[vv][:, :]),
                        deps=[t_vt, t_vb.get((vc - 1, tb))], tag=f"vte{vc}_{tb}_{s}")
                base = 2 * D * T + (vc // 2) * (TCH * 128 * 256) + (vc % 2) * 128 \
                    + tb * QS * 128 * 256
                t_vb[(vc, tb)] = r.add("sync", lambda e, base=base, tb=tb: e.dma_start(
                    out=dap(qkv_loc, base, [[256, 128], [128 * 256, QS], [1, 128]]),
                    in_=bass.AP(vt_st[tb % 2][:].tensor, vt_st[tb % 2][:].offset,
                                [[QS * 128, 128], [128, QS], [1, 128]])),
                    dma=True, deps=[ev_vtr[vtr_cnt - s2] for s2 in range(QS)], tag=f"vb{vc}_{tb}")

    t_qk_j = r.join("scalar", [ev_epi[oc * NTB + tb] for oc in range(2 * DC) for tb in range(NTB)],
                    tag="qkj")
    t_vtr_j = r.join("vector", [ev_vtr[i + 1] for i in range(vtr_cnt)], tag="vtrj")
    pB.close()

    if phases < 0.6:
        t_fin = []
        for tch in range(TCH):
            t_fin.append(r.add("sync", lambda e, tch=tch: e.dma_start(
                out=out_loc.ap()[tch * 128:(tch + 1) * 128, :],
                in_=x_loc.ap()[tch * 128:(tch + 1) * 128, :]),
                dma=True, deps=[t_qk_j, t_vtr_j] + list(t_vb.values()), tag=f"dummyout{tch}"))
        r.join("sync", t_fin, tag="final")
        r.finalize_and_emit(nc, sems)
        p1.close()
        es.close()
        return nc


    pC = ExitStack()
    ps_s1 = pC.enter_context(nc.psum_tensor("ps_s1", [1, T], F32))
    ps_s2 = pC.enter_context(nc.psum_tensor("ps_s2", [1, T], F32))

    def qk_ln(raw, gamma, row_base, tag, gates, apply_gates=(), row_gates=(),
              sq_gate=(), prev_outs=None):
        mm1 = mm2 = None
        prev_mm2 = None
        for qc in range(DC):
            t_sqx = r.add("scalar", lambda e, qc=qc: e.activation(
                sq_bf[:, :], raw[:, qc * T:(qc + 1) * T], ACTF.Square),
                deps=[t_qk_j, prev_mm2] + (list(sq_gate) if qc == 0 else []),
                tag=f"lnsq{tag}{qc}")
            for tb in range(NTB):
                mm1 = r.add("tensor", lambda e, qc=qc, tb=tb: e.matmul(
                    ps_s1[0:1, tb * TBW:(tb + 1) * TBW], ones_c[:, 0:1],
                    raw[:, qc * T + tb * TBW: qc * T + (tb + 1) * TBW],
                    start=(qc == 0), stop=(qc == DC - 1)),
                    deps=[t_qk_j, t_onc] + (gates if qc == 0 else []), tag=f"lns1{tag}{qc}_{tb}")
                mm2 = r.add("tensor", lambda e, qc=qc, tb=tb: e.matmul(
                    ps_s2[0:1, tb * TBW:(tb + 1) * TBW], ones_c[:, 0:1],
                    sq_bf[:, tb * TBW:(tb + 1) * TBW],
                    start=(qc == 0), stop=(qc == DC - 1)),
                    deps=[t_sqx, t_onc] + (gates if qc == 0 else []), tag=f"lns2{tag}{qc}_{tb}")
            prev_mm2 = mm2
        t_m = r.add("vector", lambda e: e.tensor_scalar_mul(rows[0:1, 0:T], ps_s1[0:1, :], 1.0 / D),
                    deps=[mm1] + list(row_gates), tag=f"lnm{tag}")
        t_s = r.add("vector", lambda e: e.tensor_scalar_mul(rows[0:1, 2 * T:3 * T], ps_s2[0:1, :], 1.0 / D),
                    deps=[mm2] + list(row_gates), tag=f"lns{tag}")
        w1_ = r.add("vector", lambda e: e.tensor_tensor(rows[0:1, T:2 * T], rows[0:1, 0:T], rows[0:1, 0:T], OP.mult), deps=[t_m])
        w2_ = r.add("vector", lambda e: e.tensor_tensor(rows[0:1, 2 * T:3 * T], rows[0:1, 2 * T:3 * T], rows[0:1, T:2 * T], OP.subtract), deps=[t_s, w1_])
        t_v = r.add("vector", lambda e: e.tensor_scalar_add(rows[0:1, 2 * T:3 * T], rows[0:1, 2 * T:3 * T], 1e-6), deps=[w2_])
        t_sq2 = r.add("scalar", lambda e: e.activation(rows[0:1, T:2 * T], rows[0:1, 2 * T:3 * T], ACTF.Sqrt),
                      deps=[t_v], tag=f"lnsqrt{tag}")
        t_r = r.add("vector", lambda e: e.reciprocal(rows[0:1, T:2 * T], rows[0:1, T:2 * T]),
                    deps=[t_sq2])
        pD = ExitStack()
        ps_bm = pD.enter_context(nc.psum_tensor(f"ps_bm{tag}", [128, TBW], F32))
        ps_br = pD.enter_context(nc.psum_tensor(f"ps_br{tag}", [128, TBW], F32))
        outs = []
        appl = []
        for tb in range(NTB):
            bgate = [t_m, t_s, t_r] + appl[-2:] + list(apply_gates)
            bm = r.add("tensor", lambda e, tb=tb: e.matmul(
                ps_bm[:, :], ones_rf[0:1, :], rows[0:1, tb * TBW:(tb + 1) * TBW],
                start=True, stop=True),
                deps=[t_m, t_onrf] + bgate, tag=f"lnbm{tag}{tb}")
            br = r.add("tensor", lambda e, tb=tb: e.matmul(
                ps_br[:, :], ones_rf[0:1, :], rows[0:1, T + tb * TBW: T + (tb + 1) * TBW],
                start=True, stop=True),
                deps=[t_r, t_onrf] + bgate, tag=f"lnbr{tag}{tb}")
            for qc in range(DC):
                i = tb * DC + qc
                pp = i % 2
                war_st = outs[i - 2] if i >= 2 else (prev_outs[i - 2] if prev_outs else None)
                tA = r.add("vector", lambda e, qc=qc, tb=tb, pp=pp: e.tensor_tensor(
                    qn_sc[pp][:, :], raw[:, qc * T + tb * TBW: qc * T + (tb + 1) * TBW],
                    ps_bm[:, :], OP.subtract),
                    deps=[bm, war_st], tag=f"lnap{tag}{i}")
                tB = r.add("vector", lambda e, qc=qc, pp=pp: e.scalar_tensor_tensor(
                    qn_st[pp][:, :], qn_sc[pp][:, :], gamma[:, qc:qc + 1], ps_br[:, :],
                    OP.mult, OP.mult),
                    deps=[tA, br, t_gq2, t_gk, war_st], tag=f"lnap2{tag}{i}")
                appl.append(tB)
                td = r.add("sync", lambda e, qc=qc, tb=tb, pp=pp, row_base=row_base: e.dma_start(
                    out=qkv_loc.ap()[row_base + qc * 128: row_base + (qc + 1) * 128,
                                     tb * TBW:(tb + 1) * TBW],
                    in_=qn_st[pp][:, :]),
                    dma=True, deps=[tB], tag=f"lnb{tag}{i}")
                outs.append(td)
        pD.close()
        gate_next = r.join("vector", appl[-4:] + [t_m, t_s, t_r], tag=f"lng{tag}")
        return outs, gate_next, dict(m=t_m, s=t_s, mm2=mm2, bm=bm, br=br, sq2=t_sq2)

    qb_outs, gate_q, qt = qk_ln(qraw, gq2_sb, 0, "q", gates=[t_qk_j, t_vtr_j])
    kb_outs, gate_k, _ = qk_ln(kraw, gk_sb, D, "k",
                               gates=[t_qk_j, qt["m"], qt["s"]],
                               sq_gate=[qt["mm2"]],
                               row_gates=[qt["bm"], qt["br"], qt["sq2"]],
                               apply_gates=[gate_q], prev_outs=qb_outs)
    pC.close()

    if phases < 0.9:
        t_fin = []
        for tch in range(TCH):
            t_fin.append(r.add("sync", lambda e, tch=tch: e.dma_start(
                out=out_loc.ap()[tch * 128:(tch + 1) * 128, :],
                in_=x_loc.ap()[tch * 128:(tch + 1) * 128, :]),
                dma=True, deps=qb_outs + kb_outs + list(t_vb.values()), tag=f"dummyout{tch}"))
        r.join("sync", t_fin, tag="final")
        r.finalize_and_emit(nc, sems)
        p1.close()
        es.close()
        return nc


    t_agv = r.add("gpsimd", lambda e: e.collective_compute(
        "AllGather", OP.bypass, replica_groups=[list(range(NC))],
        ins=[qkv_loc.ap()[2 * D:3 * D, :].opt()], outs=[v_g.ap().opt()]),
        deps=list(t_vb.values()), tag="agv")
    t_agq = r.add("gpsimd", lambda e: e.collective_compute(
        "AllGather", OP.bypass, replica_groups=[list(range(NC))],
        ins=[qkv_loc.ap()[0:D, :].opt()], outs=[q_g.ap().opt()]),
        deps=qb_outs, tag="agq")
    t_agk = r.add("gpsimd", lambda e: e.collective_compute(
        "AllGather", OP.bypass, replica_groups=[list(range(NC))],
        ins=[qkv_loc.ap()[D:2 * D, :].opt()], outs=[k_g.ap().opt()]),
        deps=kb_outs, tag="agk")
    t_ag1 = t_agk

    if debug:
        r.add("sync", lambda e: e.dma_start(out=dbg_xn.ap(), in_=xn_fm[:, :]),
              dma=True, deps=[t_qk_j], tag="dbgxn")
        r.add("sync", lambda e: e.dma_start(out=dbg_comb.ap(), in_=comb_b[:, :]),
              dma=True, deps=[t_comb_j], tag="dbgcomb")

    p1.close()

    if phases < 2:
        t_fin = []
        for tch in range(TCH):
            t_fin.append(r.add("sync", lambda e, tch=tch: e.dma_start(
                out=out_loc.ap()[tch * 128:(tch + 1) * 128, :],
                in_=x_loc.ap()[tch * 128:(tch + 1) * 128, :]),
                dma=True, deps=[t_ag1], tag=f"dummyout{tch}"))
        r.join("sync", t_fin, tag="final")
        r.finalize_and_emit(nc, sems)
        es.close()
        return nc

    p3 = ExitStack()
    A3 = p3.enter_context
    q2_sb = A3(nc.sbuf_tensor("q2_sb", [128, 2 * S], BF16))
    k2_sb = A3(nc.sbuf_tensor("k2_sb", [128, 2 * S], BF16))
    v2_sb = A3(nc.sbuf_tensor("v2_sb", [128, 2 * S], BF16))
    maskA = A3(nc.sbuf_tensor("maskA", [128, QS * TBW], BF16))
    maskB = A3(nc.sbuf_tensor("maskB", [128, QS * TBW], BF16))
    sA_sc = [A3(nc.sbuf_tensor(f"sA_sc{i}", [128, TBW], F32)) for i in range(2)]
    sB_sc = [A3(nc.sbuf_tensor(f"sB_sc{i}", [128, TBW], F32)) for i in range(3)]
    pt_sb = [A3(nc.sbuf_tensor(f"pt_sb{i}", [128, TBW], BF16)) for i in range(4)]
    nm_sb = A3(nc.sbuf_tensor("nm_sb", [128, 2 * QS + 2], F32))
    nmrow = [A3(nc.sbuf_tensor(f"nmrow{i}", [1, TBW], BF16)) for i in range(2)]
    l_row = [A3(nc.sbuf_tensor(f"l_row{i}", [1, TBW], F32)) for i in range(2)]
    linv_sb = [A3(nc.sbuf_tensor(f"linv_sb{i}", [128, TBW], F32)) for i in range(2)]
    ctx_st = [A3(nc.sbuf_tensor(f"ctx_st{i}", [128, TBW], BF16)) for i in range(2)]

    pP3 = ExitStack()
    ps_sa = [pP3.enter_context(nc.psum_tensor(f"ps_sa{i}", [128, TBW], F32)) for i in range(4)]
    ps_sb_ = [pP3.enter_context(nc.psum_tensor(f"ps_sb{i}", [128, TBW], F32)) for i in range(2)]
    ps_ctx = [pP3.enter_context(nc.psum_tensor("ps_ctx0", [128, TBW], F32))]
    ps_lb = pP3.enter_context(nc.psum_tensor("ps_lb", [128, TBW], F32))
    ps_l = ps_lb

    t_mA0 = r.add("gpsimd", lambda e: e.memset(maskA[:], 0.0), deps=[t_agv])
    t_mB0 = r.add("gpsimd", lambda e: e.memset(maskB[:], 0.0), deps=[t_agv])
    t_mA = []
    t_mB = []
    for qs in range(QS):
        t_mA.append(r.add("gpsimd", lambda e, qs=qs: e.affine_select(
            out=maskA[:, qs * TBW:(qs + 1) * TBW], in_=maskA[:, qs * TBW:(qs + 1) * TBW],
            compare_op=OP.is_ge, fill=NEG_BIG,
            base=qs * 128, pattern=[[-1, TBW]], channel_multiplier=1), deps=[t_mA0]))
    for kc in range(QS):
        t_mB.append(r.add("gpsimd", lambda e, kc=kc: e.affine_select(
            out=maskB[:, kc * TBW:(kc + 1) * TBW], in_=maskB[:, kc * TBW:(kc + 1) * TBW],
            compare_op=OP.is_ge, fill=NEG_BIG,
            base=-kc * 128, pattern=[[1, TBW]], channel_multiplier=-1), deps=[t_mB0]))
    t_mA_j = r.join("gpsimd", t_mA, tag="mAj")
    t_mB_j = r.join("gpsimd", t_mB, tag="mBj")

    t_qld_all, t_kld_all = [], []
    for hh in range(2):
        t_qld_all.append(r.add("sync", lambda e, hh=hh: e.dma_start(
            out=bass.AP(q2_sb[:].tensor, q2_sb[:].offset + hh * S,
                        [[2 * S, 128], [T, NC], [1, T]]),
            in_=dap(q_g, mcell["qk"] + hh * 128 * T,
                    [[T, 128], [D * T, NC], [1, T]])),
            dma=True, deps=[t_agq], tag=f"qld{hh}"))
        t_kld_all.append(r.add("sync", lambda e, hh=hh: e.dma_start(
            out=bass.AP(k2_sb[:].tensor, k2_sb[:].offset + hh * S,
                        [[2 * S, 128], [T, NC], [1, T]]),
            in_=dap(k_g, mcell["qk"] + hh * 128 * T,
                    [[T, 128], [D * T, NC], [1, T]])),
            dma=True, deps=[t_agk], tag=f"kld{hh}"))
    t_vld = {}
    for j in range(NC):
        t_vld[j] = r.add("sync", lambda e, j=j: e.dma_start(
            out=v2_sb[:, j * (TCH * 256): (j + 1) * (TCH * 256)],
            in_=dap(v_g, mcell["v"] + j * D * T,
                    [[256, 128], [128 * 256, TCH], [1, 256]])),
            dma=True, deps=[t_agv], tag=f"vld{j}")
    t_qld = {k_: t_qld_all[k_[1]] for k_ in [(j, hh) for j in range(NC) for hh in range(2)]}
    t_kld = {k_: t_kld_all[k_[1]] for k_ in [(j, hh) for j in range(NC) for hh in range(2)]}

    NQT = S // TBW
    seq = [(hh, qt) for hh in range(2) for qt in range(NQT)]
    ev_red = {}
    ev_exp = {}
    ev_ptuse = {}
    ev_norm = {}
    ev_nmT = {}
    ev_nmrow = {}
    ev_lastB = {}
    ev_linv = {}
    t_ctxst = []

    Acons = {}
    t_nmdma = {}
    t_nmj = {}
    arot_box = [0]

    def pass_A(i):
        hh, qt = seq[i]
        scr = 2 * QS
        for kt in range(qt + 1):
            for half in range(2):
                arot = arot_box[0]
                grp = ("pa", i, kt, half)
                cons = []
                war = Acons.get(arot - 2, [])
                for t2 in range(2):
                    qs = 2 * half + t2
                    c0 = 2 * qs
                    bk = (2 * arot + t2) % 4
                    diag = (kt == qt)
                    mm = r.add("tensor", lambda e, hh=hh, qt=qt, qs=qs, kt=kt, bk=bk: e.matmul(
                        ps_sa[bk][:, :],
                        q2_sb[:, hh * S + qt * TBW + qs * 128: hh * S + qt * TBW + (qs + 1) * 128],
                        k2_sb[:, hh * S + kt * TBW: hh * S + (kt + 1) * TBW],
                        start=True, stop=True),
                        deps=[t_qld[((qt * TBW) // T, hh)], t_kld[((kt * TBW) // T, hh)]] + war,
                        tag=f"pA{i}_{qs}_{kt}", group=grp)
                    war = []
                    src_ = ps_sa[bk]
                    dep_m = [mm]
                    if diag:
                        tmk = r.add("vector", lambda e, qs=qs, bk=bk: e.tensor_tensor(
                            sA_sc[qs % 2][:, :], ps_sa[bk][:, :],
                            maskA[:, qs * TBW:(qs + 1) * TBW], OP.add),
                            deps=[mm, t_mA_j], tag=f"pAm{i}_{qs}")
                        src_ = sA_sc[qs % 2]
                        dep_m = [tmk]
                        cons.append(tmk)
                    if kt == 0:
                        ev_red[(i, qs, kt)] = r.add("vector", lambda e, src_=src_, c0=c0: e.reduce_max(
                            nm_sb[:, c0:c0 + 1], src_[:, :], axis=AX, negate=True),
                            deps=dep_m + [t_nmdma.get((i - 1, qs)), t_nmdma.get((i - 2, qs))],
                            tag=f"pAr{i}_{qs}_{kt}")
                    else:
                        t_rm = r.add("vector", lambda e, src_=src_: e.reduce_max(
                            nm_sb[:, scr:scr + 1], src_[:, :], axis=AX, negate=True),
                            deps=dep_m + [ev_red[(i, qs, kt - 1)]], tag=f"pAr{i}_{qs}_{kt}")
                        ev_red[(i, qs, kt)] = r.add("vector", lambda e, kt=kt, c0=c0: e.tensor_tensor(
                            nm_sb[:, c0 + kt % 2:c0 + kt % 2 + 1],
                            nm_sb[:, c0 + (kt - 1) % 2:c0 + (kt - 1) % 2 + 1],
                            nm_sb[:, scr:scr + 1], OP.min),
                            deps=[t_rm, ev_red[(i, qs, kt - 1)]])
                    if not diag:
                        cons.append(ev_red[(i, qs, kt)])
                Acons[arot] = cons
                arot_box[0] = arot + 1
                if kt == qt:
                    for qs in (2 * half, 2 * half + 1):
                        fin = 2 * qs + qt % 2
                        t_nmdma[(i, qs)] = r.add("gpsimd", lambda e, i=i, qs=qs, fin=fin: e.dma_start(
                            out=nmrow[i % 2][0:1, qs * 128:(qs + 1) * 128],
                            in_=nm_sb[:, fin:fin + 1]),
                            dma=True, deps=[ev_red[(i, qs, qt)], ev_lastB.get(i - 2)],
                            tag=f"nmD{i}_{qs}")
                yield
        t_nmj[i] = r.join("gpsimd", [t_nmdma[(i, qs)] for qs in range(QS)], tag=f"nmJ{i}")
        yield

    Bexp = []
    Pgrp = {}
    Qgrp = {}
    Bgrp_base = {}
    sB_last = {}
    CH = 2

    def emit_exp(i, kc):
        hh, qt = seq[i]
        g = Bgrp_base[i] + kc
        bb = g % CH
        pt4 = g % 4
        diag = (kc >= qt * QS)
        if diag:
            kcd = kc - qt * QS
            tmk = r.add("vector", lambda e, kcd=kcd, bb=bb: e.tensor_tensor(
                sB_sc[bb][:, :], ps_sb_[bb][:, :],
                maskB[:, kcd * TBW:(kcd + 1) * TBW], OP.add),
                deps=[Pgrp[g // CH], Qgrp.get(g // CH - 2), t_mB_j,
                      sB_last.get(bb)], tag=f"pBm{i}_{kc}")
            te = r.add("scalar", lambda e, bb=bb, pt4=pt4: e.activation(
                pt_sb[pt4][:, :], sB_sc[bb][:, :], ACTF.Exp),
                deps=[tmk], tag=f"pBe{i}_{kc}")
            sB_last[bb] = te
        else:
            te = r.add("scalar", lambda e, bb=bb, pt4=pt4: e.activation(
                pt_sb[pt4][:, :], ps_sb_[bb][:, :], ACTF.Exp),
                deps=[Pgrp[g // CH], Qgrp.get(g // CH - 2)], tag=f"pBe{i}_{kc}")
        assert len(Bexp) == g
        Bexp.append(te)

    def emit_P(i, c):
        hh, qt = seq[i]
        gc = (Bgrp_base[i] + c * CH) // CH
        grp = ("pbP", i, c)
        war = Bexp[CH * gc - 1] if CH * gc >= 1 else None
        tok = None
        for j in range(CH):
            kc = c * CH + j
            tok = r.add("tensor", lambda e, hh=hh, qt=qt, kc=kc, j=j: e.matmul(
                ps_sb_[j][:, :],
                k2_sb[:, hh * S + kc * 128: hh * S + (kc + 1) * 128],
                q2_sb[:, hh * S + qt * TBW: hh * S + (qt + 1) * TBW],
                start=True, stop=False),
                deps=[war], tag=f"pB{i}_{kc}", group=grp)
            war = None
            tok = r.add("tensor", lambda e, i=i, j=j: e.matmul(
                ps_sb_[j][:, :], ones_rb[0:1, :], nmrow[i % 2][0:1, :],
                start=False, stop=True),
                deps=[t_nmj[i], t_onrb], tag=f"pBr{i}_{kc}", group=grp)
        Pgrp[gc] = tok

    def emit_Q(i, c):
        hh, qt = seq[i]
        nkc = (qt + 1) * QS
        gc = (Bgrp_base[i] + c * CH) // CH
        grp = ("pbQ", i, c)
        war = Bexp[CH * gc + CH - 1]
        tok = None
        for j in range(CH):
            kt_ = c * CH + j
            pt4 = (CH * gc + j) % 4
            tok = r.add("tensor", lambda e, kt_=kt_, pt4=pt4, nkc=nkc: e.matmul(
                ps_l[0:1, :], ones_c[:, 0:1], pt_sb[pt4][:, :],
                start=(kt_ == 0), stop=(kt_ == nkc - 1)),
                deps=[war, t_onc,
                      (ev_linv.get(i - 1) if kt_ == 0 else None)],
                tag=f"pBl{i}_{kt_}", group=grp)
            war = None
            tok = r.add("tensor", lambda e, hh=hh, kt_=kt_, pt4=pt4, nkc=nkc, i=i: e.matmul(
                ps_ctx[0][:, :],
                v2_sb[:, kt_ * 256 + hh * 128: kt_ * 256 + (hh + 1) * 128],
                pt_sb[pt4][:, :],
                start=(kt_ == 0), stop=(kt_ == nkc - 1)),
                deps=[war, t_vld[(kt_ * 128) // T],
                      (ev_norm.get(i - 1) if kt_ == 0 else None)],
                tag=f"pBc{i}_{kt_}", group=grp)
            ev_ptuse[(i, kt_)] = tok
        Qgrp[gc] = tok

    def pass_B(i):
        hh, qt = seq[i]
        nkc = (qt + 1) * QS
        nch = nkc // CH
        Bgrp_base[i] = len(Bexp)
        for c in range(nch):
            if c >= 1:
                for j in range(CH):
                    emit_exp(i, CH * (c - 1) + j)
            emit_P(i, c)
            if c >= 1:
                emit_Q(i, c - 1)
            yield
        for j in range(CH):
            emit_exp(i, CH * (nch - 1) + j)
        yield
        emit_Q(i, nch - 1)
        ev_lastB[i] = ev_ptuse[(i, nkc - 1)]
        t_lev = r.add("vector", lambda e, i=i: e.tensor_copy(l_row[i % 2][0:1, :], ps_l[0:1, 0:TBW]),
                      deps=[ev_ptuse[(i, nkc - 1)]], tag=f"lev{i}")
        t_lrec = r.add("vector", lambda e, i=i: e.reciprocal(l_row[i % 2][0:1, :], l_row[i % 2][0:1, :]),
                       deps=[t_lev])
        for _ in range(5):
            yield
        t_lb = r.add("tensor", lambda e, i=i: e.matmul(
            ps_lb[:, :], ones_rf[0:1, :], l_row[i % 2][0:1, :], start=True, stop=True),
            deps=[t_lrec, t_onrf, ev_linv.get(i - 1)], tag=f"lb{i}")
        ev_linv[i] = r.add("vector", lambda e, i=i: e.tensor_copy(linv_sb[i % 2][:, :], ps_lb[:, :]),
                           deps=[t_lb], tag=f"linv{i}")
        ev_norm[i] = r.add("vector", lambda e, i=i: e.tensor_tensor(
            ctx_st[i % 2][:, :], ps_ctx[0][:, :], linv_sb[i % 2][:, :], OP.mult),
            deps=[ev_linv[i], ev_ptuse[(i, nkc - 1)],
                  (t_ctxst[i - 2] if i >= 2 else None)], tag=f"norm{i}")
        t_ctxst.append(r.add("sync", lambda e, hh=hh, qt=qt, i=i: e.dma_start(
            out=ctx_loc.ap()[hh * 128:(hh + 1) * 128, qt * TBW:(qt + 1) * TBW],
            in_=ctx_st[i % 2][:, :]),
            dma=True, deps=[ev_norm[i]], tag=f"ctxst{i}"))
        yield

    SENT = object()
    for i in range(len(seq)):
        ga = pass_A(i)
        gb = pass_B(i - 1) if i > 0 else None
        a_done = False
        b_done = gb is None
        for _ in range(5):
            if not a_done:
                a_done = next(ga, SENT) is SENT
        while not (a_done and b_done):
            if not b_done:
                b_done = next(gb, SENT) is SENT
            if not a_done:
                a_done = next(ga, SENT) is SENT
    for _ in pass_B(len(seq) - 1):
        pass

    t_ag2 = r.add("gpsimd", lambda e: e.collective_compute(
        "AllGather", OP.bypass, replica_groups=[list(range(NC))],
        ins=[ctx_loc.ap().opt()], outs=[ctx_g.ap().opt()]),
        deps=t_ctxst, tag="ag2")

    pP3.close()
    p3.close()

    if phases < 3:
        t_fin = []
        for tch in range(TCH):
            t_fin.append(r.add("sync", lambda e, tch=tch: e.dma_start(
                out=out_loc.ap()[tch * 128:(tch + 1) * 128, :],
                in_=x_loc.ap()[tch * 128:(tch + 1) * 128, :]),
                dma=True, deps=[t_ag2], tag=f"dummyout{tch}"))
        r.join("sync", t_fin, tag="final")
        r.finalize_and_emit(nc, sems)
        es.close()
        return nc

    p2a = ExitStack()
    acc = p2a.enter_context(nc.sbuf_tensor("acc", [128, TCH * D], F32))
    p2 = ExitStack()
    A2 = p2.enter_context
    y_sb = A2(nc.sbuf_tensor("y_sb", [128, GE * FC * NTB * TBW], BF16))
    wg_sb = [A2(nc.sbuf_tensor(f"wg_sb{i}", [128, DC * 128], BF16)) for i in range(2)]
    w1s_sb = [A2(nc.sbuf_tensor(f"w1s_sb{i}", [128, DC * 128], BF16)) for i in range(2)]
    w2b_sb = [A2(nc.sbuf_tensor(f"w2b_sb{i}", [128, GE * FC * DQW], BF16)) for i in range(2)]
    gel_sb = [A2(nc.sbuf_tensor(f"gel_sb{i}", [128, TBW], F32)) for i in range(2)]
    cb_sb = [A2(nc.sbuf_tensor(f"cb_sb{i}", [128, TBW], F32)) for i in range(2)]
    yt_sb = [A2(nc.sbuf_tensor(f"yt_sb{i}", [128, TBW], F32)) for i in range(2)]

    pP2 = ExitStack()
    ps_a = [pP2.enter_context(nc.psum_tensor(f"ps_a{i}", [128, TBW], F32)) for i in range(2)]
    ps_b = [pP2.enter_context(nc.psum_tensor(f"ps_b{i}", [128, TBW], F32)) for i in range(2)]
    ps_cb = [pP2.enter_context(nc.psum_tensor(f"ps_cb{i}", [128, TBW], F32)) for i in range(2)]
    ps_o = [pP2.enter_context(nc.psum_tensor(f"ps_o{i}", [128, DQW], F32)) for i in range(2)]

    gate_p2 = r.join("vector", [ev_norm[len(seq) - 1], ev_lastB[len(seq) - 1],
                                t_ctxst[-1], t_ctxst[-2]], tag="p3done")

    ev_gel = {}
    ev_yt = {}
    ev_y = {}
    ev_yfc = {}
    t_cb = {}
    ev_spill = {}
    t_spill_j = {}
    ev_bmmw = {}
    cnt_a = 0
    sb_cnt = 0
    for g in range(NG):
        exps = [g * GE + i for i in range(GE)]
        for esl, e_ in enumerate(exps):
            routed = e_ < E
            for fc in range(FC):
                wi = (g * GE + esl) * FC + fc
                t_wg = r.add("gpsimd", lambda e, e_=e_, fc=fc, wi=wi: e.dma_start(
                    out=wg_sb[wi % 2][:],
                    in_=g_all.ap()[(e_ * FC + fc) * 128:(e_ * FC + fc + 1) * 128, :]),
                    dma=True, deps=[gate_p2, ev_yfc.get((wi - 2, NTB - 1))], tag=f"wg{wi}")
                t_w1s = r.add("gpsimd", lambda e, e_=e_, fc=fc, wi=wi: e.dma_start(
                    out=w1s_sb[wi % 2][:],
                    in_=w1_all.ap()[(e_ * FC + fc) * 128:(e_ * FC + fc + 1) * 128, :]),
                    dma=True, deps=[gate_p2, ev_yfc.get((wi - 2, NTB - 1))], tag=f"w1{wi}")
                for tb in range(NTB):
                    if routed and fc == 0:
                        t_cbm = r.add("tensor", lambda e, e_=e_, tb=tb: e.matmul(
                            ps_cb[tb][:, :], onehot[:, e_ * 128:(e_ + 1) * 128],
                            comb_b[:, tb * TBW:(tb + 1) * TBW], start=True, stop=True),
                            deps=[t_comb_j, t_oh_j,
                                  t_cb.get((e_ - 1, tb))],
                            tag=f"cb{e_}_{tb}")
                        t_cb[(e_, tb)] = r.add("vector", lambda e, tb=tb: e.tensor_copy(
                            cb_sb[tb][:, :], ps_cb[tb][:, :]),
                            deps=[t_cbm, ev_yt.get((e_ - 1, FC - 1, tb))],
                            tag=f"cbe{e_}_{tb}")
                    cnt_a += 1
                    ppa = cnt_a % 2
                    mt = None
                    for dc in range(DC):
                        mt = r.add("tensor", lambda e, wi=wi, dc=dc, tb=tb, ppa=ppa: e.matmul(
                            ps_a[ppa][:, :], wg_sb[wi % 2][:, dc * 128:(dc + 1) * 128],
                            xn_fm[:, dc * T + tb * TBW: dc * T + (tb + 1) * TBW],
                            start=(dc == 0), stop=(dc == DC - 1)),
                            deps=[t_wg, ev_gel.get(cnt_a - 2)], tag=f"ffA{wi}_{tb}_{dc}",
                            group=("ffa", wi, tb))
                    mtb = None
                    for dc in range(DC):
                        mtb = r.add("tensor", lambda e, wi=wi, dc=dc, tb=tb, ppa=ppa: e.matmul(
                            ps_b[ppa][:, :], w1s_sb[wi % 2][:, dc * 128:(dc + 1) * 128],
                            xn_fm[:, dc * T + tb * TBW: dc * T + (tb + 1) * TBW],
                            start=(dc == 0), stop=(dc == DC - 1)),
                            deps=[t_w1s, ev_yt.get(("b", cnt_a - 2))], tag=f"ffB{wi}_{tb}_{dc}",
                            group=("ffb", wi, tb))
                    bidx = fc * EXP + e_
                    ev_gel[cnt_a] = r.add("scalar", lambda e, bidx=bidx, ppa=ppa: e.activation(
                        gel_sb[ppa][:, :], ps_a[ppa][:, :], ACTF.Gelu,
                        bias=gbT[:, bidx:bidx + 1]),
                        deps=[mt, t_bias_j, ev_y.get(cnt_a - 2)], tag=f"gel{wi}_{tb}")
                    slot = ((esl * FC + fc) * NTB + tb) * TBW
                    warY = ev_bmmw.get(("g", g - 1))
                    if routed:
                        tyt = r.add("vector", lambda e, bidx=bidx, ppa=ppa, tb=tb: e.scalar_tensor_tensor(
                            yt_sb[ppa][:, :], ps_b[ppa][:, :], b1T[:, bidx:bidx + 1],
                            cb_sb[tb][:, :], OP.add, OP.mult),
                            deps=[mtb, t_cb[(e_, tb)], t_bias_j, ev_y.get(cnt_a - 2)],
                            tag=f"yt{wi}_{tb}")
                        ev_yt[("b", cnt_a)] = tyt
                        ev_yt[(e_, fc, tb)] = tyt
                        ev_y[cnt_a] = r.add("vector", lambda e, slot=slot, ppa=ppa: e.tensor_tensor(
                            y_sb[:, slot:slot + TBW], yt_sb[ppa][:, :], gel_sb[ppa][:, :], OP.mult),
                            deps=[tyt, ev_gel[cnt_a], warY], tag=f"y{wi}_{tb}")
                    else:
                        tyt = r.add("vector", lambda e, bidx=bidx, ppa=ppa, slot=slot: e.scalar_tensor_tensor(
                            y_sb[:, slot:slot + TBW], ps_b[ppa][:, :], b1T[:, bidx:bidx + 1],
                            gel_sb[ppa][:, :], OP.add, OP.mult),
                            deps=[mtb, ev_gel[cnt_a], t_bias_j, warY], tag=f"ysh{wi}_{tb}")
                        ev_yt[("b", cnt_a)] = tyt
                        ev_yt[(e_, fc, tb)] = tyt
                        ev_y[cnt_a] = tyt
                    ev_yfc[(wi, tb)] = ev_y[cnt_a]
        y_done = {tb: [ev_yfc[((g * GE + esl) * FC + fc, tb)]
                       for esl in range(GE) for fc in range(FC)] for tb in range(NTB)}
        bmms_all = []
        for dq in range(NDQ):
            w = g * NDQ + dq
            dmas = []
            for esl, e_ in enumerate(exps):
                dmas.append(r.add("gpsimd", lambda e, e_=e_, dq=dq, w=w, esl=esl: e.dma_start(
                    out=w2b_sb[w % 2][:, esl * FC * DQW:(esl + 1) * FC * DQW],
                    in_=w2_all.ap()[(e_ * NDQ + dq) * 128:(e_ * NDQ + dq + 1) * 128, :]),
                    dma=True, deps=[gate_p2, ev_bmmw.get(w - 2)], tag=f"w2{w}_{esl}"))
            t_w2b = r.join("gpsimd", dmas, tag=f"w2j{w}")
            bmms = []
            for tb in range(NTB):
                for tc in range(QS):
                    oi = (tb * QS + tc) * NDQ + dq
                    sb_cnt += 1
                    po = sb_cnt % 2
                    warO = ev_spill.get(sb_cnt - 2)
                    if warO is None and g > 0:
                        warO = t_spill_j[g - 1]
                    mt = None
                    nmm = GE * FC
                    im = 0
                    for esl in range(GE):
                        for fc in range(FC):
                            slot = ((esl * FC + fc) * NTB + tb) * TBW + tc * 128
                            mt = r.add("tensor", lambda e, slot=slot, esl=esl, fc=fc, po=po, im=im, nmm=nmm, w=w: e.matmul(
                                ps_o[po][:, :], y_sb[:, slot:slot + 128],
                                w2b_sb[w % 2][:, (esl * FC + fc) * DQW:(esl * FC + fc + 1) * DQW],
                                start=(im == 0), stop=(im == nmm - 1)),
                                deps=[t_w2b] + ([warO] + y_done[tb] if im == 0 else []),
                                tag=f"ffO{g}_{oi}_{im}", group=("ffo", g, sb_cnt))
                            im += 1
                    bmms.append(mt)
                    bmms_all.append(mt)
                    aslice = (tb * QS + tc) * D + dq * DQW
                    if g == 0:
                        ev_spill[sb_cnt] = r.add("vector", lambda e, aslice=aslice, po=po: e.tensor_copy(
                            acc[:, aslice:aslice + DQW], ps_o[po][:, :]),
                            deps=[mt], tag=f"sp{g}_{oi}")
                    else:
                        ev_spill[sb_cnt] = r.add("vector", lambda e, aslice=aslice, po=po: e.tensor_tensor(
                            acc[:, aslice:aslice + DQW], ps_o[po][:, :],
                            acc[:, aslice:aslice + DQW], OP.add),
                            deps=[mt, ev_spill.get(sb_cnt - TCH * NDQ)], tag=f"sp{g}_{oi}")
            ev_bmmw[w] = r.join("tensor", bmms, tag=f"bmmw{w}")
        ev_bmmw[("g", g)] = ev_bmmw[g * NDQ + NDQ - 1]
        t_spill_j[g] = r.join("vector", [ev_spill[sb_cnt - i] for i in range(TCH * NDQ)],
                              tag=f"spj{g}")

    t_ffn_j = t_spill_j[NG - 1]
    if debug:
        t_dbg_acc = r.add("sync", lambda e: e.dma_start(out=dbg_acc.ap(), in_=acc[:, :]),
                          dma=True, deps=[t_ffn_j], tag="dbgacc")
        t_ffn_j = r.join("vector", [t_ffn_j, t_dbg_acc], tag="ffnjd")
    pP2.close()
    p2.close()

    if phases < 4:
        t_fin = []
        for tch in range(TCH):
            t_fin.append(r.add("sync", lambda e, tch=tch: e.dma_start(
                out=out_loc.ap()[tch * 128:(tch + 1) * 128, :],
                in_=x_loc.ap()[tch * 128:(tch + 1) * 128, :]),
                dma=True, deps=[t_ffn_j], tag=f"dummyout{tch}"))
        r.join("sync", t_fin, tag="final")
        p2a.close()
        r.finalize_and_emit(nc, sems)
        es.close()
        return nc

    p4 = ExitStack()
    A4 = p4.enter_context
    CG = 4 if DC % 4 == 0 else 2
    ctx_gr = [A4(nc.sbuf_tensor(f"ctx_gr{i}", [128, CG * T], BF16)) for i in range(2)]
    wo_sb = [A4(nc.sbuf_tensor(f"wo_sb{i}", [128, CG * D], BF16)) for i in range(2)]
    x2_sb = [A4(nc.sbuf_tensor(f"x2_sb{i}", [128, D], F32)) for i in range(2)]
    xnT_sb = [A4(nc.sbuf_tensor(f"xnT_sb{i}", [128, D], BF16)) for i in range(2)]
    stg_sb = [A4(nc.sbuf_tensor(f"stg_sb{i}", [128, D], F32)) for i in range(2)]

    pP4 = ExitStack()
    ps_p = [pP4.enter_context(nc.psum_tensor(f"ps_p{i}", [128, DQW], F32)) for i in range(2)]
    ps_tx = [pP4.enter_context(nc.psum_tensor(f"ps_tx{i}", [128, 128], BF16)) for i in range(2)]

    gate_p4 = t_ffn_j

    ev_psp = {}
    pj_cnt = 0
    last_psp = {}
    for cg in range(DC // CG):
        ccs = [cg * CG + i for i in range(CG)]
        lds = [r.add("sync", lambda e, cg=cg: e.dma_start(
            out=bass.AP(ctx_gr[cg % 2][:].tensor, ctx_gr[cg % 2][:].offset,
                        [[CG * T, 128], [T, CG], [1, T]]),
            in_=dap(ctx_g, mcell["ctx"] + cg * CG * 128 * S,
                    [[S, 128], [128 * S, CG], [1, T]])),
            dma=True, deps=[t_ag2, gate_p4, last_psp.get(cg - 2)],
            tag=f"ctxg{cg}")]
        for cc in ccs:
            lds.append(r.add("gpsimd", lambda e, cc=cc, cg=cg: e.dma_start(
                out=wo_sb[cg % 2][:, (cc % CG) * D:(cc % CG + 1) * D],
                in_=wout_r.ap()[cc * 128:(cc + 1) * 128, :]),
                dma=True, deps=[gate_p4, last_psp.get(cg - 2)],
                tag=f"wog{cc}"))
        t_ldj = r.join("sync", lds, tag=f"ldj{cg}")
        for tch in range(TCH):
            for dq in range(NDQ):
                pj_cnt += 1
                pi = pj_cnt % 2
                mt = None
                for i, cc in enumerate(ccs):
                    mt = r.add("tensor", lambda e, cc=cc, i=i, tch=tch, dq=dq, pi=pi, cg=cg: e.matmul(
                        ps_p[pi][:, :],
                        ctx_gr[cg % 2][:, (cc % CG) * T + tch * 128:(cc % CG) * T + (tch + 1) * 128],
                        wo_sb[cg % 2][:, (cc % CG) * D + dq * DQW:(cc % CG) * D + (dq + 1) * DQW],
                        start=(i == 0), stop=(i == CG - 1)),
                        deps=[t_ldj] + ([ev_psp.get(pj_cnt - 2)] if i == 0 else []),
                        tag=f"prj{cg}_{tch}_{dq}_{i}", group=("prj", pj_cnt))
                aslice = tch * D + dq * DQW
                ev_psp[pj_cnt] = r.add("vector", lambda e, aslice=aslice, pi=pi: e.tensor_tensor(
                    acc[:, aslice:aslice + DQW], ps_p[pi][:, :],
                    acc[:, aslice:aslice + DQW], OP.add),
                    deps=[mt, t_ffn_j, ev_psp.get(pj_cnt - TCH * NDQ)], tag=f"prjsp{cg}_{tch}_{dq}")
        last_psp[cg] = r.join("vector", [ev_psp[pj_cnt - i] for i in range(TCH * NDQ)],
                              tag=f"pspj{cg}")

    t_proj_j = last_psp[DC // CG - 1]
    t_out = []
    ev_fin = {}
    ev_xnT = {}
    fin_cnt = 0
    tx_cnt = 0
    for tch in range(TCH):
        t_x2 = r.add("sync", lambda e, tch=tch: e.dma_start(
            out=x2_sb[tch % 2][:], in_=x_loc.ap()[tch * 128:(tch + 1) * 128, :]),
            dma=True, deps=[t_ag2, (t_out[tch - 2] if tch >= 2 else None)], tag=f"x2{tch}")
        xts = []
        for dc in range(DC):
            tx_cnt += 1
            pv = tx_cnt % 2
            t_tx = r.add("tensor", lambda e, dc=dc, tch=tch, pv=pv: e.transpose(
                ps_tx[pv][:, :], xn_fm[:, dc * T + tch * 128: dc * T + (tch + 1) * 128],
                id_bf[:, :]),
                deps=[t_idb, ev_xnT.get(tx_cnt - 2),
                      (t_out[tch - 2] if (tch >= 2 and dc < 2) else None),
                      (t_proj_j if (tch == 0 and dc < 2) else None)],
                tag=f"xtx{tch}_{dc}")
            ev_xnT[tx_cnt] = r.add("vector", lambda e, dc=dc, tch=tch, pv=pv: e.tensor_copy(
                xnT_sb[tch % 2][:, dc * 128:(dc + 1) * 128], ps_tx[pv][:, :]),
                deps=[t_tx, (t_out[tch - 2] if tch >= 2 else None)], tag=f"xte{tch}_{dc}")
            xts.append(ev_xnT[tx_cnt])
        for dq in range(NDQ):
            fin_cnt += 1
            pi = fin_cnt % 2
            m1 = r.add("tensor", lambda e, tch=tch, dq=dq, pi=pi: e.matmul(
                ps_p[pi][:, :], comb_b[:, tch * 128:(tch + 1) * 128],
                rb2_sb[:, dq * DQW:(dq + 1) * DQW], start=True, stop=False),
                deps=[t_comb_j, t_rb2, t_proj_j, ev_fin.get(fin_cnt - 2)],
                tag=f"fb2{tch}_{dq}", group=("fin", fin_cnt))
            m2 = r.add("tensor", lambda e, dq=dq, pi=pi: e.matmul(
                ps_p[pi][:, :], ones128b[0:SH, 0:128], sb2_sb[:, dq * DQW:(dq + 1) * DQW],
                start=False, stop=True),
                deps=[t_sb2, t_o128b], tag=f"fsb2{tch}_{dq}", group=("fin", fin_cnt))
            aslice = tch * D + dq * DQW
            f1 = r.add("vector", lambda e, aslice=aslice, pi=pi, tch=tch, dq=dq: e.tensor_tensor(
                stg_sb[tch % 2][:, dq * DQW:(dq + 1) * DQW], ps_p[pi][:, :],
                acc[:, aslice:aslice + DQW], OP.add),
                deps=[m2, t_proj_j,
                      (t_out[tch - 2] if tch >= 2 else None)], tag=f"f1_{tch}_{dq}")
            f2 = r.add("vector", lambda e, tch=tch, dq=dq: e.tensor_tensor(
                stg_sb[tch % 2][:, dq * DQW:(dq + 1) * DQW],
                stg_sb[tch % 2][:, dq * DQW:(dq + 1) * DQW],
                x2_sb[tch % 2][:, dq * DQW:(dq + 1) * DQW], OP.add),
                deps=[t_x2, f1])
            ev_fin[fin_cnt] = r.add("vector", lambda e, tch=tch, dq=dq: e.tensor_tensor(
                stg_sb[tch % 2][:, dq * DQW:(dq + 1) * DQW],
                stg_sb[tch % 2][:, dq * DQW:(dq + 1) * DQW],
                xnT_sb[tch % 2][:, dq * DQW:(dq + 1) * DQW], OP.add),
                deps=[f2] + xts, tag=f"fin{tch}_{dq}")
        t_out.append(r.add("sync", lambda e, tch=tch: e.dma_start(
            out=out_loc.ap()[tch * 128:(tch + 1) * 128, :], in_=stg_sb[tch % 2][:, :]),
            dma=True, deps=[ev_fin[fin_cnt - i] for i in range(NDQ)], tag=f"out{tch}"))

    r.join("sync", t_out, tag="final")

    pP4.close()
    p4.close()
    p2a.close()

    r.finalize_and_emit(nc, sems)
    es.close()
    return nc


def prep_inputs(cfg, inp):
    cf = derive(cfg)
    D, S, F, E, SH = cf["D"], cf["S"], cf["F"], cf["E"], cf["SH"]
    NC, DC, FC = cf["NC"], cf["DC"], cf["FC"]
    T, DQW, NDQ, EXP = cf["T"], cf["DQW"], cf["NDQ"], cf["EXP"]

    f32 = lambda a: np.ascontiguousarray(np.asarray(a, dtype=np.float32))
    x = f32(inp["L0_attn_x"]).reshape(S, D)
    Wqkv = f32(inp["L0_attn_Wqkv"])
    outW = f32(inp["L0_attn_out_W"])
    wa = f32(inp["L0_ffn_wa"])
    gq = f32(inp["L0_attn_q_ln_scale"])
    gk = f32(inp["L0_attn_k_ln_scale"])
    rg, rgb = f32(inp["L0_ffn_rg"]), f32(inp["L0_ffn_rgb"])
    rw1, rb1 = f32(inp["L0_ffn_rw1"]), f32(inp["L0_ffn_rb1"])
    rw2, rb2 = f32(inp["L0_ffn_rw2"]), f32(inp["L0_ffn_rb2"])
    sg, sgb = f32(inp["L0_ffn_sg"]), f32(inp["L0_ffn_sgb"])
    sw1, sb1 = f32(inp["L0_ffn_sw1"]), f32(inp["L0_ffn_sb1"])
    sw2, sb2 = f32(inp["L0_ffn_sw2"]), f32(inp["L0_ffn_sb2"])

    def qk_layout(w):
        return np.ascontiguousarray(
            w.reshape(DC, 128, DC, 128).transpose(2, 1, 0, 3).reshape(D, DC * 128))

    wqkv_r = np.ascontiguousarray(np.concatenate(
        [qk_layout(Wqkv[:, 0:D]), qk_layout(Wqkv[:, D:2 * D]), qk_layout(Wqkv[:, 2 * D:3 * D])],
        axis=0))
    wa_r = np.ascontiguousarray(wa.reshape(DC, 128, E).transpose(1, 0, 2).reshape(128, DC * E))
    gq_r = np.ascontiguousarray(gq.reshape(DC, 128).T)
    gk_r = np.ascontiguousarray(gk.reshape(DC, 128).T)

    gmat = np.concatenate([rg, sg], axis=0)
    w1mat = np.concatenate([rw1, sw1], axis=0)
    w2mat = np.concatenate([rw2, sw2], axis=0)
    g_allr = np.ascontiguousarray(
        gmat.reshape(EXP, DC, 128, FC, 128).transpose(0, 3, 2, 1, 4).reshape(EXP * FC * 128, DC * 128))
    w1_allr = np.ascontiguousarray(
        w1mat.reshape(EXP, DC, 128, FC, 128).transpose(0, 3, 2, 1, 4).reshape(EXP * FC * 128, DC * 128))
    w2_allr = np.ascontiguousarray(
        w2mat.reshape(EXP, FC, 128, NDQ, DQW).transpose(0, 3, 2, 1, 4).reshape(EXP * NDQ * 128, FC * DQW))
    gb_allr = np.ascontiguousarray(np.concatenate([rgb, sgb], axis=0))
    b1_allr = np.ascontiguousarray(np.concatenate([rb1, sb1], axis=0))

    shared = dict(wqkv_r=wqkv_r, wout_r=outW, wa_r=wa_r, gq_r=gq_r, gk_r=gk_r,
                  g_all=g_allr, w1_all=w1_allr, w2_all=w2_allr,
                  gb_all=gb_allr, b1_all=b1_allr, rb2_r=np.ascontiguousarray(rb2),
                  sb2_r=np.ascontiguousarray(sb2))
    in_maps = []
    for c in range(NC):
        m = dict(shared)
        m["x_loc"] = np.ascontiguousarray(x[c * T:(c + 1) * T])
        m["meta"] = np.array([[c, 0, 0, 0, 0, 0, 0, 0]], np.int32)
        in_maps.append(m)
    return in_maps


def kernel(**inputs):
    cfg = REAL_CFG
    cf = derive(cfg)
    nc = build_nc(cfg)
    in_maps = prep_inputs(cfg, inputs)
    res = run_bass_kernel_spmd(nc, in_maps, core_ids=list(range(cfg["NC"])))
    out = np.concatenate([res.results[i]["out_loc"] for i in range(cfg["NC"])], axis=0)
    return out.reshape(1, cfg["S"], cfg["D"]).astype(np.float32)
